# version 02
# baseline (speedup 1.0000x reference)
import contextlib
import numpy as np
import concourse.bass as bass
import concourse.mybir as mybir
from concourse.bass_utils import run_bass_kernel_spmd

F32 = mybir.dt.float32
BF16 = mybir.dt.bfloat16
AF = mybir.ActivationFunctionType
ALU = mybir.AluOpType
AX = mybir.AxisListType

ENGINES = ("tensor", "vector", "scalar", "gpsimd", "sync")

T = 4096
D = 1024
NT = T // 128
DC = D // 128


class Prog:

    def __init__(self, nc):
        self.nc = nc
        self.q = {e: [] for e in ENGINES}
        self.sem_names = []
        self.counts = {}
        self.last_wait = {e: {} for e in ENGINES}

    def sem(self, name, owner=None):
        name = f"{name}_{len(self.sem_names)}"
        self.sem_names.append(name)
        self.counts[name] = 0
        self.owner = getattr(self, "owner", {})
        self.owner[name] = owner
        return name

    def barrier(self):
        snap = [(sname, c) for sname, c in self.counts.items() if c > 0]
        for eng in ENGINES:
            for sname, c in snap:
                if self.owner.get(sname) == eng:
                    continue
                if c <= self.last_wait[eng].get(sname, 0):
                    continue
                self.last_wait[eng][sname] = c
                self.q[eng].append(([(sname, c)],
                                    (lambda h: h.nop()), None, 1, False))

    def now(self, s):
        return self.counts[s]

    def emit(self, eng, fn, waits=(), inc=None, amt=1, drain=False):
        wl = []
        for s, c in waits:
            assert c <= self.counts[s], f"wait on future count: {s} {c} > {self.counts[s]}"
            if c <= 0 or c <= self.last_wait[eng].get(s, 0):
                continue
            if self.owner.get(s) == eng:
                continue
            self.last_wait[eng][s] = c
            wl.append((s, c))
        if inc is not None:
            self.counts[inc] += amt
        self.q[eng].append((wl, fn, inc, amt, drain))
        return self.counts[inc] if inc is not None else None

    def finalize(self):
        nc = self.nc
        with contextlib.ExitStack() as stack:
            sems = {n: stack.enter_context(nc.semaphore(n)) for n in self.sem_names}
            blk = stack.enter_context(nc.Block())
            for eng in ENGINES:
                ops = self.q[eng]
                if not ops:
                    continue

                def body(h, ops=ops):
                    for wl, fn, inc, amt, drain in ops:
                        for s, c in wl:
                            h.wait_ge(sems[s], c)
                        r = fn(h)
                        if drain:
                            r = h.drain()
                        if inc is not None:
                            r.then_inc(sems[inc], amt)

                getattr(blk, eng)(body)
        return nc


def chunkmajor_ap(dram, rows, row_stride, ncols, col0=0, nrow_chunks=None):
    if nrow_chunks is None:
        nrow_chunks = rows // 128
    ap = dram.ap()
    flat = ap.copy()
    while len(flat.ap) > 1:
        flat.ap.pop()
    flat.ap[0] = [row_stride, 128]
    flat.ap.append([row_stride * 128, nrow_chunks])
    flat.ap.append([1, ncols])
    flat.offset = col0
    return flat




class Arena:

    def __init__(self, nc, nbytes):
        self.t = nc.alloc_sbuf_tensor("arena", [128, nbytes], mybir.dt.uint8)
        self.nbytes = nbytes
        self.off = 0
        self.high = 0

    def reset(self):
        self.off = 0

    def alloc(self, name, shape, dtype):
        assert len(shape) == 2 and shape[0] <= 128, (name, shape)
        per = shape[1] * mybir.dt.size(dtype)
        off = (self.off + 31) // 32 * 32
        self.off = off + per
        self.high = max(self.high, self.off)
        assert self.off <= self.nbytes, f"arena overflow at {name}: {self.off}"
        v = self.t.ap()[0:shape[0], off:off + per].bitcast(dtype)
        return v


class PsumPool:
    def __init__(self, nc):
        self.f32 = [nc.alloc_psum_tensor(f"pp{i}", [128, 512], F32) for i in range(6)]
        self.bf = [nc.alloc_psum_tensor(f"pb{i}", [128, 1024], BF16) for i in range(2)]
        self.reset()

    def reset(self):
        self.i32 = 0
        self.ibf = 0

    def alloc(self, name, shape, dtype):
        if dtype == BF16:
            t = self.bf[self.ibf]
            self.ibf += 1
        else:
            t = self.f32[self.i32]
            self.i32 += 1
        return t[0:shape[0], 0:shape[1]]


class FusedCtx:
    def __init__(self, nc, p, arena, pool):
        self.nc = nc
        self.p = p
        self.arena = arena
        self.pool = pool
        self.phase = ""
        self.redirect = {}
        self.input_names = []

    def begin_phase(self, phase):
        self.phase = phase
        self.arena.reset()
        self.pool.reset()
        self.p.barrier()

    def dram(self, name, shape, dtype, kind):
        key = (self.phase, name)
        if key in self.redirect:
            return self.redirect[key]
        fname = f"{self.phase}_{name}"
        h = self.nc.dram_tensor(fname, shape, dtype, kind=kind)
        if kind == "ExternalInput":
            self.input_names.append((fname, self.phase, name))
        return h

    def sb(self, name, shape, dtype):
        return self.arena.alloc(name, shape, dtype)

    def psum(self, name, shape, dtype):
        return self.pool.alloc(name, shape, dtype)


def _ctx_or_new(ctx, phase):
    if ctx is None:
        nc = bass.Bass("TRN2", target_bir_lowering=False)
        p = Prog(nc)

        def DT(name, shape, dtype, kind):
            return nc.dram_tensor(name, shape, dtype, kind=kind)

        def psum(name, shape, dtype):
            return nc.alloc_psum_tensor(name, shape, dtype)

        return nc, p, DT, nc.alloc_sbuf_tensor, psum, True
    ctx.begin_phase(phase)
    return ctx.nc, ctx.p, (lambda n, sh, dt, kind: ctx.dram(n, sh, dt, kind)), \
        ctx.sb, ctx.psum, False



def build_launch_d(debug=False, ctx=None):
    nc, p, DT, sb, psum, fin_own = _ctx_or_new(ctx, "d")


    res3 = DT("res3", [T, D], F32, "ExternalInput")
    w1t = DT("w1t", [8, D, 4096], F32, "ExternalInput")
    w2t = DT("w2t", [8, 4096, D], F32, "ExternalInput")
    b1c = DT("b1c", [8, 128, 32], F32, "ExternalInput")
    b2r = DT("b2r", [8, 1, D], F32, "ExternalInput")
    wrt = DT("wrt", [D, 8], F32, "ExternalInput")
    brv = DT("brv", [1, 8], F32, "ExternalInput")
    lnw = DT("lnw", [1, D], F32, "ExternalInput")
    lnb = DT("lnb", [1, D], F32, "ExternalInput")
    idx8 = DT("idx8", [128, 8], F32, "ExternalInput")
    idx8m8 = DT("idx8m8", [128, 8], F32, "ExternalInput")
    ident = DT("ident", [128, 128], F32, "ExternalInput")
    out = DT("out", [T, D], F32, "ExternalOutput")
    ht_scr = DT("ht_scr", [D, T], BF16, "Internal")
    acc_scr = DT("acc_scr", [T, D], F32, "Internal")
    if debug:
        dbg_lg = DT("dbg_lg", [128, NT * 8], F32, "ExternalOutput")
        dbg_cw = DT("dbg_cw", [128, NT * 8], F32, "ExternalOutput")
        dbg_acc = DT("dbg_acc", [T, D], F32, "ExternalOutput")
        dbg_ht = DT("dbg_ht", [D, T], BF16, "ExternalOutput")
        dbg_hts = DT("dbg_hts", [128, D], BF16, "ExternalOutput")
        dbg_hbf = DT("dbg_hbf", [128, D], BF16, "ExternalOutput")
        dbg_h32 = DT("dbg_h32", [128, D], F32, "ExternalOutput")

    rbuf = [sb(f"rbuf{i}", [128, D], F32) for i in range(2)]
    scratch = sb("scratch", [128, D], F32)
    h32 = sb("h32", [128, D], F32)
    hbf = sb("hbf", [128, D], BF16)
    hlo = sb("hlo", [128, D], BF16)
    hts = sb("hts", [128, D], BF16)
    htl = sb("htl", [128, D], BF16)
    stats = sb("stats", [128, 24], F32)
    epsc = sb("epsc", [128, 2], F32)
    lgall = sb("lgall", [128, NT * 8], F32) if debug else None
    cw = sb("cw", [128, NT * 8], F32)
    idx8s = sb("idx8s", [128, 8], F32)
    idx8m8s = sb("idx8m8s", [128, 8], F32)
    l8 = sb("l8", [128, 8], F32)
    tmp8a = sb("tmp8a", [128, 8], F32)
    tmp8b = sb("tmp8b", [128, 8], F32)
    maskA = sb("maskA", [128, 8], F32)
    maskB = sb("maskB", [128, 8], F32)
    wrf = sb("wrf", [128, 64], F32)
    wrh = sb("wrh", [128, 64], BF16)
    wrl = sb("wrl", [128, 64], BF16)
    brb = sb("brb", [128, 8], F32)
    lnwb = sb("lnwb", [128, D], F32)
    lnbb = sb("lnbb", [128, D], F32)
    onesf = sb("onesf", [1, 128], F32)
    onesb = sb("onesb", [1, 128], BF16)
    identb = sb("identb", [128, 128], BF16)
    w1s = sb("w1s", [128, 8 * 4096], BF16)
    w2s = sb("w2s", [128, 32 * D], BF16)
    b1s = sb("b1s", [128, 32], F32)
    b2s = sb("b2s", [1, D], BF16)
    ht2 = [sb(f"ht2_{i}", [128, 8 * 256], BF16) for i in range(2)]
    gt = sb("gt", [128, 32 * 256], BF16)
    stage = [sb(f"stage{i}", [128, D], F32) for i in range(2)]
    atile = [sb("atile0", [128, D], F32)]
    obuf = [stage[1]]

    pm1 = [psum(f"pm1_{i}", [128, 512], F32) for i in range(2)]
    pm2 = [psum(f"pm2_{i}", [128, 512], F32) for i in range(2)]
    ptr = psum("ptr", [128, 1024], BF16)
    ptr2 = psum("ptr2", [128, 1024], BF16)
    pbc = psum("pbc", [128, 512], F32)

    s_ld = p.sem("s_ld")
    s_wt = p.sem("s_wt")
    s_pe = p.sem("s_pe", owner="tensor")
    s_v = p.sem("s_v", owner="vector")
    s_sc = p.sem("s_sc", owner="scalar")
    s_st = p.sem("s_st")
    s_g = p.sem("s_g", owner="gpsimd")

    lnwr = scratch[0:1, :]
    lnbr = h32[0:1, :]
    brs = cw[0:1, 0:8]

    E = p.emit

    E("gpsimd", lambda g: g.memset(onesf[:], 1.0), inc=s_g)
    E("gpsimd", lambda g: g.memset(epsc[:, 0:1], 1e-5), inc=s_g)
    E("gpsimd", lambda g: g.memset(epsc[:, 1:2], 1e-12), inc=s_g)
    E("gpsimd", lambda g: g.memset(onesb[:], 1.0), inc=s_g)
    E("gpsimd", lambda g: g.dma_start(out=identb[:], in_=ident.ap()), inc=s_wt, amt=16)
    E("sync", lambda y: y.dma_start(out=wrf[:], in_=chunkmajor_ap(wrt, D, 8, 8)),
      inc=s_ld, amt=16)
    E("sync", lambda y: y.dma_start(out=brs, in_=brv.ap()), inc=s_ld, amt=16)
    E("sync", lambda y: y.dma_start(out=lnwr, in_=lnw.ap()), inc=s_ld, amt=16)
    E("sync", lambda y: y.dma_start(out=lnbr, in_=lnb.ap()), inc=s_ld, amt=16)
    E("sync", lambda y: y.dma_start(out=idx8s[:], in_=idx8.ap()), inc=s_ld, amt=16)
    E("sync", lambda y: y.dma_start(out=idx8m8s[:], in_=idx8m8.ap()), inc=s_ld, amt=16)
    ld_setup = p.now(s_ld)
    E("vector", lambda v: v.tensor_copy(wrh[:], wrf[:]), waits=[(s_ld, ld_setup)],
      inc=s_v, drain=True)
    E("vector", lambda v: v.tensor_tensor(wrl[:], wrf[:], wrh[:], ALU.subtract),
      inc=s_v, drain=True)
    v_wr = p.now(s_v)
    for (dst, srcrow, wid) in ((lnwb, lnwr, D), (lnbb, lnbr, D), (brb, brs, 8)):
        for c0 in range(0, wid, 512):
            w = min(512, wid - c0)
            E("tensor",
              lambda t, srcrow=srcrow, c0=c0, w=w: t.matmul(
                  pbc[:, 0:w], onesf[:], srcrow[0:1, c0:c0 + w], start=True, stop=True),
              waits=[(s_g, 1), (s_ld, ld_setup), (s_v, p.now(s_v))], inc=s_pe)
            E("vector",
              lambda v, dst=dst, c0=c0, w=w: v.tensor_copy(dst[:, c0:c0 + w], pbc[:, 0:w]),
              waits=[(s_pe, p.now(s_pe))], inc=s_v, drain=True)

    LNI = 1.0 / D
    EPS1 = 1e-5
    rel_rbuf = [0, 0]
    rel_ptr = 0
    rel_ptr2 = 0
    rel_pbc = p.now(s_v)
    need_hts = 0
    hts_mm = 0
    cw_done_v = 0

    def emit_ln_stats(src, eps_ap, mcol, rstdcol, wv, wsc):
        E("vector", lambda v: v.tensor_reduce(stats[:, 0:1], src[:], AX.X, ALU.add),
          waits=wv, inc=s_v, drain=True)
        E("vector", lambda v: v.tensor_tensor(scratch[:], src[:], src[:], ALU.mult), inc=s_v, drain=True)
        E("vector", lambda v: v.tensor_reduce(stats[:, 2:3], scratch[:], AX.X, ALU.add), inc=s_v, drain=True)
        E("vector", lambda v: v.tensor_scalar(stats[:, mcol:mcol + 1], stats[:, 0:1],
                                              LNI, 0.0, ALU.mult, ALU.add), inc=s_v, drain=True)
        E("vector", lambda v: v.tensor_scalar(stats[:, 3:4], stats[:, 2:3],
                                              LNI, 0.0, ALU.mult, ALU.add), inc=s_v, drain=True)
        E("vector", lambda v: v.tensor_tensor(stats[:, 4:5], stats[:, mcol:mcol + 1],
                                              stats[:, mcol:mcol + 1], ALU.mult), inc=s_v, drain=True)
        E("vector", lambda v: v.tensor_tensor(stats[:, 5:6], stats[:, 3:4],
                                              stats[:, 4:5], ALU.subtract), inc=s_v, drain=True)
        vc = p.now(s_v)
        E("scalar", lambda s: s.activation(stats[:, 6:7], stats[:, 5:6], AF.Ln,
                                           bias=eps_ap),
          waits=[(s_v, vc), (s_g, 4)] + wsc, inc=s_sc, drain=True)
        E("scalar", lambda s: s.activation(stats[:, rstdcol:rstdcol + 1], stats[:, 6:7],
                                           AF.Exp, scale=-0.5), inc=s_sc, drain=True)
        return p.now(s_sc)

    for i in range(NT):
        b = i % 2
        E("sync", lambda y, i=i, b=b: y.dma_start(
            out=rbuf[b][:], in_=res3.ap()[i * 128:(i + 1) * 128, :]),
          waits=[(s_v, rel_rbuf[b])], inc=s_ld, amt=16)
        ldc = p.now(s_ld)
        sc = emit_ln_stats(rbuf[b], epsc[:, 0:1], 1, 7, [(s_ld, ldc)], [])
        E("vector", lambda v, b=b: v.tensor_scalar(
            h32[:], rbuf[b][:], stats[:, 1:2], stats[:, 7:8], ALU.subtract, ALU.mult),
          waits=[(s_sc, sc)], inc=s_v, drain=True)
        rel_rbuf[b] = p.now(s_v)
        E("vector", lambda v: v.tensor_copy(hbf[:], h32[:]), inc=s_v, drain=True)
        E("vector", lambda v: v.tensor_tensor(hlo[:], h32[:], hbf[:], ALU.subtract),
          inc=s_v, drain=True)
        v_hbf = p.now(s_v)
        for c in range(DC):
            E("tensor", lambda t, c=c: t.transpose(
                ptr[:, c * 128:(c + 1) * 128], hbf[:, c * 128:(c + 1) * 128], identb[:]),
              waits=[(s_v, v_hbf), (s_wt, 16), (s_v, rel_ptr)],
              inc=s_pe if c == DC - 1 else None)
        pe_tr1 = p.now(s_pe)
        E("vector", lambda v: v.tensor_copy(hts[:], ptr[:]),
          waits=[(s_pe, pe_tr1), (s_st, need_hts), (s_pe, hts_mm)], inc=s_v, drain=True)
        rel_ptr = p.now(s_v)
        for c in range(DC):
            E("tensor", lambda t, c=c: t.transpose(
                ptr2[:, c * 128:(c + 1) * 128], hlo[:, c * 128:(c + 1) * 128], identb[:]),
              waits=[(s_v, p.now(s_v)), (s_v, rel_ptr2)],
              inc=s_pe if c == DC - 1 else None)
        pe_tr2 = p.now(s_pe)
        E("vector", lambda v: v.tensor_copy(htl[:], ptr2[:]),
          waits=[(s_pe, pe_tr2)], inc=s_v, drain=True)
        rel_ptr2 = p.now(s_v)
        v_ht = p.now(s_v)
        E("sync", lambda y, i=i: y.dma_start(
            out=chunkmajor_ap(ht_scr, D, T, 128, col0=i * 128), in_=hts[:]),
          waits=[(s_v, v_ht)], inc=s_st, amt=16)
        need_hts = p.now(s_st)
        if debug and i == 0:
            E("sync", lambda y: y.dma_start(out=dbg_hts.ap(), in_=hts[:]),
              waits=[(s_v, v_ht)], inc=s_st, amt=16)
            E("sync", lambda y: y.dma_start(out=dbg_hbf.ap(), in_=hbf[:]),
              waits=[(s_v, v_ht)], inc=s_st, amt=16)
            E("sync", lambda y: y.dma_start(out=dbg_h32.ap(), in_=h32[:]),
              waits=[(s_v, v_ht)], inc=s_st, amt=16)
            need_hts = p.now(s_st)
        nmm = 0
        for c in range(DC):
            for (lt, rt) in ((hts, wrh), (hts, wrl), (htl, wrh)):
                nmm += 1
                E("tensor", lambda t, c=c, lt=lt, rt=rt, first=(nmm == 1), last=(nmm == 24):
                  t.matmul(pbc[:, 0:8], lt[:, c * 128:(c + 1) * 128],
                           rt[:, c * 8:(c + 1) * 8], start=first, stop=last),
                  waits=[(s_v, v_ht), (s_v, v_wr), (s_v, rel_pbc)],
                  inc=s_pe if nmm == 24 else None)
        hts_mm = p.now(s_pe)
        E("vector", lambda v: v.tensor_tensor(l8[:], pbc[:, 0:8], brb[:], ALU.add),
          waits=[(s_pe, hts_mm)], inc=s_v, drain=True)
        rel_pbc = p.now(s_v)
        E("vector", lambda v: v.tensor_reduce(stats[:, 8:9], l8[:], AX.X, ALU.max), inc=s_v, drain=True)
        E("vector", lambda v: v.tensor_scalar(maskA[:], l8[:], stats[:, 8:9], 0.0,
                                              ALU.is_equal, ALU.add), inc=s_v, drain=True)
        E("vector", lambda v: v.tensor_tensor(tmp8a[:], maskA[:], idx8m8s[:], ALU.mult), inc=s_v, drain=True)
        E("vector", lambda v: v.tensor_scalar(tmp8a[:], tmp8a[:], 8.0, 0.0,
                                              ALU.add, ALU.add), inc=s_v, drain=True)
        E("vector", lambda v: v.tensor_reduce(stats[:, 10:11], tmp8a[:], AX.X, ALU.min), inc=s_v, drain=True)
        E("vector", lambda v: v.tensor_scalar(maskA[:], idx8s[:], stats[:, 10:11], 0.0,
                                              ALU.is_equal, ALU.add), inc=s_v, drain=True)
        E("vector", lambda v: v.scalar_tensor_tensor(tmp8b[:], maskA[:], -1e30, l8[:],
                                                     ALU.mult, ALU.add), inc=s_v, drain=True)
        E("vector", lambda v: v.tensor_reduce(stats[:, 9:10], tmp8b[:], AX.X, ALU.max), inc=s_v, drain=True)
        E("vector", lambda v: v.tensor_scalar(maskB[:], tmp8b[:], stats[:, 9:10], 0.0,
                                              ALU.is_equal, ALU.add), inc=s_v, drain=True)
        E("vector", lambda v: v.tensor_tensor(tmp8a[:], maskB[:], idx8m8s[:], ALU.mult), inc=s_v, drain=True)
        E("vector", lambda v: v.tensor_scalar(tmp8a[:], tmp8a[:], 8.0, 0.0,
                                              ALU.add, ALU.add), inc=s_v, drain=True)
        E("vector", lambda v: v.tensor_reduce(stats[:, 11:12], tmp8a[:], AX.X, ALU.min), inc=s_v, drain=True)
        E("vector", lambda v: v.tensor_scalar(maskB[:], idx8s[:], stats[:, 11:12], 0.0,
                                              ALU.is_equal, ALU.add), inc=s_v, drain=True)
        E("vector", lambda v: v.tensor_tensor(stats[:, 12:13], stats[:, 8:9],
                                              stats[:, 9:10], ALU.subtract), inc=s_v, drain=True)
        vsw = p.now(s_v)
        E("scalar", lambda s: s.activation(stats[:, 13:14], stats[:, 12:13], AF.Exp,
                                           scale=-1.0), waits=[(s_v, vsw)], inc=s_sc, drain=True)
        E("vector", lambda v: v.tensor_scalar(stats[:, 14:15], stats[:, 13:14], 1.0, 0.0,
                                              ALU.add, ALU.add),
          waits=[(s_sc, p.now(s_sc))], inc=s_v, drain=True)
        E("vector", lambda v: v.reciprocal(stats[:, 15:16], stats[:, 14:15]), inc=s_v, drain=True)
        E("vector", lambda v: v.tensor_tensor(stats[:, 16:17], stats[:, 13:14],
                                              stats[:, 15:16], ALU.mult), inc=s_v, drain=True)
        E("vector", lambda v: v.tensor_scalar(tmp8a[:], maskB[:], stats[:, 16:17], 0.0,
                                              ALU.mult, ALU.add), inc=s_v, drain=True)
        E("vector", lambda v, i=i: v.scalar_tensor_tensor(
            cw[:, i * 8:(i + 1) * 8], maskA[:], stats[:, 15:16], tmp8a[:],
            ALU.mult, ALU.add), inc=s_v, drain=True)
        if debug:
            E("vector", lambda v, i=i: v.tensor_copy(lgall[:, i * 8:(i + 1) * 8], l8[:]),
              inc=s_v, drain=True)
        cw_done_v = p.now(s_v)

    phaseA_st = p.now(s_st)
    phaseA_pe = p.now(s_pe)

    rel_w1s = 0
    rel_w2s = 0
    rel_b = 0
    rel_ht2 = [0, 0]
    rel_gt = 0
    rel_pm1 = [0, 0]
    rel_pm2 = [0, 0]
    rel_stage = [0, 0]
    prev_acc_st = 0
    TT = 16

    for e in range(8):
        E("gpsimd", lambda g, e=e: g.dma_start(
            out=w1s[:], in_=chunkmajor_ap(w1t, D, 4096, 4096,
                                          col0=e * D * 4096)),
          waits=[(s_pe, rel_w1s)], inc=s_wt, amt=16)
        E("gpsimd", lambda g, e=e: g.dma_start(
            out=w2s[:], in_=chunkmajor_ap(w2t, 4096, D, D,
                                          col0=e * 4096 * D, nrow_chunks=32)),
          waits=[(s_pe, rel_w2s)], inc=s_wt, amt=16)
        E("gpsimd", lambda g, e=e: g.dma_start(
            out=b1s[:], in_=b1c.ap()[e]), waits=[(s_sc, rel_b)], inc=s_wt, amt=16)
        E("gpsimd", lambda g, e=e: g.dma_start(
            out=b2s[:], in_=b2r.ap()[e]), waits=[(s_pe, rel_w2s)], inc=s_wt, amt=16)
        wt_e = p.now(s_wt)
        for j in range(TT):
            bj = j % 2
            E("sync", lambda y, j=j, bj=bj: y.dma_start(
                out=ht2[bj][:], in_=chunkmajor_ap(ht_scr, D, T, 256, col0=j * 256)),
              waits=[(s_st, phaseA_st), (s_pe, rel_ht2[bj])], inc=s_ld, amt=16)
            ld_ht = p.now(s_ld)
            for s in range(32):
                bs = s % 2
                for c in range(DC):
                    E("tensor", lambda t, s=s, c=c, bs=bs, bj=bj:
                      t.matmul(pm1[bs][:, 0:256],
                               w1s[:, c * 4096 + s * 128: c * 4096 + (s + 1) * 128],
                               ht2[bj][:, c * 256:(c + 1) * 256],
                               start=(c == 0), stop=(c == DC - 1)),
                      waits=[(s_wt, wt_e), (s_ld, ld_ht), (s_sc, rel_pm1[bs])],
                      inc=s_pe if c == DC - 1 else None)
                pe_mm1 = p.now(s_pe)
                E("scalar", lambda sc_, s=s, bs=bs: sc_.activation(
                    gt[:, s * 256:(s + 1) * 256], pm1[bs][:, 0:256], AF.Gelu,
                    bias=b1s[:, s:s + 1]),
                  waits=[(s_pe, pe_mm1), (s_wt, wt_e), (s_pe, rel_gt)], inc=s_sc)
                rel_pm1[bs] = p.now(s_sc)
            sc_gelu = p.now(s_sc)
            for ts_ in range(2):
                for co in range(2):
                    for s in range(32):
                        E("tensor", lambda t, s=s, ts_=ts_, co=co:
                          t.matmul(pm2[co][:],
                                   gt[:, s * 256 + ts_ * 128: s * 256 + (ts_ + 1) * 128],
                                   w2s[:, s * D + co * 512: s * D + (co + 1) * 512],
                                   start=(s == 0), stop=False),
                          waits=[(s_sc, sc_gelu), (s_wt, wt_e), (s_v, rel_pm2[co])],
                          inc=None)
                    E("tensor", lambda t, ts_=ts_, co=co:
                      t.matmul(pm2[co][:], onesb[:, 0:128],
                               b2s[:, co * 512:(co + 1) * 512],
                               start=False, stop=True),
                      waits=[(s_g, 4)], inc=s_pe)
                    pe_mm2 = p.now(s_pe)
                    gidx = j * 2 + ts_
                    E("vector", lambda v, co=co, gidx=gidx, e=e, bst=(j * 2 + ts_) % 2:
                      v.tensor_scalar(stage[bst][:, co * 512:(co + 1) * 512],
                                      pm2[co][:], cw[:, gidx * 8 + e: gidx * 8 + e + 1],
                                      0.0, ALU.mult, ALU.add),
                      waits=[(s_pe, pe_mm2), (s_v, cw_done_v),
                             (s_st, rel_stage[(j * 2 + ts_) % 2])], inc=s_v)
                    rel_pm2[co] = p.now(s_v)
                v_stage = p.now(s_v)
                bst = (j * 2 + ts_) % 2
                gidx = j * 2 + ts_
                if e == 0:
                    E("gpsimd", lambda g, gidx=gidx, bst=bst: g.dma_start(
                        out=acc_scr.ap()[gidx * 128:(gidx + 1) * 128, :],
                        in_=stage[bst][:]),
                      waits=[(s_v, v_stage)], inc=s_st, amt=16)
                else:
                    E("gpsimd", lambda g, gidx=gidx, bst=bst: g.dma_start(
                        out=acc_scr.ap()[gidx * 128:(gidx + 1) * 128, :],
                        in_=stage[bst][:], accum_op=ALU.add),
                      waits=[(s_v, v_stage), (s_st, prev_acc_st)], inc=s_st, amt=16)
                rel_stage[bst] = p.now(s_st)
            rel_gt = p.now(s_pe)
            rel_ht2[bj] = p.now(s_pe)
        rel_w1s = p.now(s_pe)
        rel_w2s = p.now(s_pe)
        rel_b = p.now(s_sc)
        prev_acc_st = p.now(s_st)

    acc_done = p.now(s_st)

    rel_stage_f = max(rel_stage)
    rel_atile = [rel_pbc]
    rel_obuf = [0]
    for i in range(NT):
        b = i % 2
        E("sync", lambda y, i=i, b=b: y.dma_start(
            out=rbuf[b][:], in_=res3.ap()[i * 128:(i + 1) * 128, :]),
          waits=[(s_v, rel_rbuf[b])], inc=s_ld, amt=16)
        E("sync", lambda y, i=i: y.dma_start(
            out=atile[0][:], in_=acc_scr.ap()[i * 128:(i + 1) * 128, :]),
          waits=[(s_st, acc_done), (s_v, rel_atile[0]), (s_st, rel_stage_f)], inc=s_ld, amt=16)
        ldc = p.now(s_ld)
        sc = emit_ln_stats(rbuf[b], epsc[:, 0:1], 1, 7, [(s_ld, ldc)], [])
        E("vector", lambda v, b=b: v.tensor_scalar(
            h32[:], rbuf[b][:], stats[:, 1:2], stats[:, 7:8], ALU.subtract, ALU.mult),
          waits=[(s_sc, sc)], inc=s_v, drain=True)
        E("vector", lambda v: v.tensor_tensor(atile[0][:], h32[:], atile[0][:],
                                                   ALU.add), inc=s_v, drain=True)
        sc2 = emit_ln_stats(atile[0], epsc[:, 1:2], 1, 7, [(s_v, p.now(s_v))], [])
        E("vector", lambda v: v.tensor_scalar(
            scratch[:], atile[0][:], stats[:, 1:2], stats[:, 7:8], ALU.subtract, ALU.mult),
          waits=[(s_sc, sc2)], inc=s_v, drain=True)
        rel_atile[0] = p.now(s_v)
        E("vector", lambda v: v.tensor_tensor(scratch[:], scratch[:], lnwb[:], ALU.mult),
          inc=s_v, drain=True)
        E("vector", lambda v, b=b: v.tensor_tensor(scratch[:], scratch[:], lnbb[:], ALU.add),
          inc=s_v, drain=True)
        E("vector", lambda v, b=b: v.tensor_tensor(obuf[0][:], scratch[:], rbuf[b][:],
                                                   ALU.add),
          waits=[(s_st, rel_obuf[0])], inc=s_v, drain=True)
        rel_rbuf[b] = p.now(s_v)
        vo = p.now(s_v)
        E("gpsimd", lambda g, i=i, b=b: g.dma_start(
            out=out.ap()[i * 128:(i + 1) * 128, :], in_=obuf[0][:]),
          waits=[(s_v, vo)], inc=s_st, amt=16)
        rel_obuf[0] = p.now(s_st)

    if debug:
        vv = p.now(s_v)
        E("gpsimd", lambda g: g.dma_start(out=dbg_lg.ap(), in_=lgall[:]),
          waits=[(s_v, vv)], inc=s_st, amt=16)
        E("gpsimd", lambda g: g.dma_start(out=dbg_cw.ap(), in_=cw[:]),
          waits=[(s_v, vv)], inc=s_st, amt=16)
        E("gpsimd", lambda g: g.dma_start(out=dbg_acc.ap(), in_=acc_scr.ap()),
          waits=[(s_st, acc_done)], inc=s_st, amt=16)
        E("gpsimd", lambda g: g.dma_start(out=dbg_ht.ap(), in_=ht_scr.ap()),
          waits=[(s_st, phaseA_st)], inc=s_st, amt=16)

    fin = p.now(s_st)
    E("gpsimd", lambda g: g.nop(), waits=[(s_st, fin)])

    return p.finalize()



def prep_launch_d(inputs):
    W1 = np.ascontiguousarray(inputs["L1_ffn_W1"])
    W2 = np.ascontiguousarray(inputs["L1_ffn_W2"])
    d = dict(
        w1t=np.ascontiguousarray(W1.transpose(0, 2, 1)),
        w2t=np.ascontiguousarray(W2.transpose(0, 2, 1)),
        b1c=np.ascontiguousarray(
            np.asarray(inputs["L1_ffn_b1"]).reshape(8, 32, 128).transpose(0, 2, 1)),
        b2r=np.ascontiguousarray(np.asarray(inputs["L1_ffn_b2"]).reshape(8, 1, D)),
        wrt=np.ascontiguousarray(np.asarray(inputs["L1_ffn_Wr"]).T),
        brv=np.asarray(inputs["L1_ffn_br"]).reshape(1, 8),
        lnw=np.asarray(inputs["L1_ffn_ln_w"]).reshape(1, D),
        lnb=np.asarray(inputs["L1_ffn_ln_b"]).reshape(1, D),
        idx8=np.tile(np.arange(8, dtype=np.float32), (128, 1)),
        idx8m8=np.tile(np.arange(8, dtype=np.float32) - 8.0, (128, 1)),
        ident=np.eye(128, dtype=np.float32),
    )
    return {k: np.ascontiguousarray(np.asarray(v, dtype=np.float32)) for k, v in d.items()}


def run_launch_d(res3_full, inputs, debug=False):
    nc = build_launch_d(debug)
    com = prep_launch_d(inputs)
    in_maps = []
    for b in range(8):
        m = dict(com)
        m["res3"] = np.ascontiguousarray(res3_full[b])
        in_maps.append(m)
    res = run_bass_kernel_spmd(nc, in_maps, core_ids=list(range(8)))
    if debug:
        return res.results
    return np.stack([r["out"] for r in res.results])



def build_launch_c(ctx=None):
    nc, p, DT, sb, psum, fin_own = _ctx_or_new(ctx, "c")


    res2 = DT("res2", [T, D], F32, "ExternalInput")
    ln1g = DT("ln1g", [1, D], F32, "ExternalInput")
    ln1b = DT("ln1b", [1, D], F32, "ExternalInput")
    ln2g = DT("ln2g", [1, D], F32, "ExternalInput")
    ln2b = DT("ln2b", [1, D], F32, "ExternalInput")
    tmcD = DT("tmc", [128, 8], F32, "ExternalInput")
    cmcD = DT("cmc", [128, 8], F32, "ExternalInput")
    ftmcD = DT("ftmc", [128, 8], F32, "ExternalInput")
    fcmcD = DT("fcmc", [128, 8], F32, "ExternalInput")
    tdcD = DT("tdc", [128, 8], F32, "ExternalInput")
    tfcD = DT("tfc", [128, 8], F32, "ExternalInput")
    mloD = DT("mlo", [128, 8], F32, "ExternalInput")
    wktD = DT("wkt", [D, D], F32, "ExternalInput")
    wvtD = DT("wvt", [D, D], F32, "ExternalInput")
    wrtD = DT("wrt", [D, D], F32, "ExternalInput")
    wotD = DT("wot", [D, D], F32, "ExternalInput")
    fktD = DT("fkt", [D, 4096], F32, "ExternalInput")
    fvtD = DT("fvt", [4096, D], F32, "ExternalInput")
    frtD = DT("frt", [D, D], F32, "ExternalInput")
    ident = DT("ident", [128, 128], F32, "ExternalInput")
    out = DT("out", [T, D], F32, "ExternalOutput")

    x1_scr = DT("x1_scr", [T, D], F32, "Internal")
    x1t_scr = DT("x1t_scr", [D, T + 2], BF16, "Internal")
    x3_scr = DT("x3_scr", [T, D], F32, "Internal")
    x3t_scr = DT("x3t_scr", [D, T + 2], BF16, "Internal")
    xm2t_scr = DT("xm2t_scr", [D, T], BF16, "Internal")
    kv_scr = DT("kv_scr", [T, D], F32, "Internal")

    rbuf = [sb(f"rbuf{i}", [128, D], F32) for i in range(2)]
    x1til = [sb(f"x1til{i}", [128, D], F32) for i in range(2)]
    scratch = sb("scratch", [128, D], F32)
    h32 = sb("h32", [128, D], F32)
    x3w = sb("x3w", [128, D], F32)
    hbf = sb("hbf", [128, D], BF16)
    hts = sb("hts", [128, D], BF16)
    obuf = sb("obuf", [128, D], F32)
    stats = sb("stats", [128, 16], F32)
    epsc = sb("epsc", [128, 1], F32)
    lnbc = [sb(f"lnbc{i}", [128, D], F32) for i in range(4)]
    onesf = sb("onesf", [1, 128], F32)
    ones512 = sb("ones512", [128, 512], F32)
    identb = sb("identb", [128, 128], BF16)
    zcolb = sb("zcolb", [128, 8], BF16)
    cols = sb("cols", [128, 8 * 12], F32)
    def C(i):
        return cols[:, i * 8:(i + 1) * 8]
    wk_sb = sb("wk_sb", [128, 8 * D], BF16)
    wv_sb = sb("wv_sb", [128, 8 * D], BF16)
    wr_sb = sb("wr_sb", [128, 8 * D], BF16)
    wo_sb = sb("wo_sb", [128, 8 * D], BF16)
    fkb, fvb, frt_sb = wk_sb, wv_sb, wr_sb
    x1t_c = sb("x1t_c", [128, 8 * 514], BF16)
    xm_c = sb("xm_c", [128, 8 * 512], BF16)
    P_c = sb("P_c", [128, 8 * 512], BF16)
    gk = sb("gk", [128, 512], F32)
    gv = sb("gv", [128, 512], F32)
    gkv = sb("gkv", [128, 512], F32)
    gy = sb("gy", [128, 512], F32)
    gsr = sb("gsr", [128, 512], F32)
    gt1 = sb("gt1", [128, 512], F32)
    gt2 = sb("gt2", [128, 512], F32)
    edb = sb("edb", [128, 512], F32)
    Abuf = sb("Abuf", [128, 513], F32)
    Bbuf = sb("Bbuf", [128, 513], F32)
    carryA = sb("carryA", [128, 8], F32)
    carryB = sb("carryB", [128, 8], F32)
    xs = [sb(f"xs{i}", [128, 8 * 128], BF16) for i in range(2)]
    k2sq = sb("k2sq", [128, 8 * 128], BF16)
    sq_in = sb("sq_in", [128, 128], BF16)
    kstage = [sb(f"kstage{i}", [128, D], F32) for i in range(2)]
    srt = sb("srt", [128, D], F32)

    pmA = psum("pmA", [128, 512], F32)
    pmB = psum("pmB", [128, 512], F32)
    pmC = psum("pmC", [128, 512], F32)
    pmD = psum("pmD", [128, 512], F32)
    ptr = psum("ptr", [128, 1024], BF16)
    pbc = psum("pbc", [128, 512], F32)

    s_ld = p.sem("s_ld")
    s_wt = p.sem("s_wt")
    s_pe = p.sem("s_pe", owner="tensor")
    s_v = p.sem("s_v", owner="vector")
    s_sc = p.sem("s_sc", owner="scalar")
    s_st = p.sem("s_st")
    s_g = p.sem("s_g", owner="gpsimd")

    E = p.emit
    LNI = 1.0 / D

    E("gpsimd", lambda g: g.memset(onesf[:], 1.0), inc=s_g)
    E("gpsimd", lambda g: g.memset(ones512[:], 1.0), inc=s_g)
    E("gpsimd", lambda g: g.memset(epsc[:], 1e-5), inc=s_g)
    E("gpsimd", lambda g: g.memset(carryA[:], 0.0), inc=s_g)
    E("gpsimd", lambda g: g.memset(carryB[:], 0.0), inc=s_g)
    E("gpsimd", lambda g: g.memset(zcolb[:], 0.0), inc=s_g)
    NGS = 6
    E("gpsimd", lambda g: g.dma_start(out=identb[:], in_=ident.ap()), inc=s_wt, amt=16)
    for (dst, src) in ((C(0), tmcD), (C(1), cmcD), (C(2), ftmcD), (C(3), fcmcD),
                       (C(4), tdcD), (C(5), tfcD), (C(6), mloD)):
        E("sync", lambda y, dst=dst, src=src: y.dma_start(out=dst, in_=src.ap()),
          inc=s_ld, amt=16)
    rowbuf = scratch[0:1, :]
    for i, src in enumerate((ln1g, ln1b, ln2g, ln2b)):
        E("sync", lambda y, src=src: y.dma_start(out=rowbuf, in_=src.ap()),
          waits=[(s_v, p.now(s_v))], inc=s_ld, amt=16)
        ldc = p.now(s_ld)
        for c0 in range(0, D, 512):
            E("tensor", lambda t, c0=c0: t.matmul(
                pbc[:], onesf[:], rowbuf[0:1, c0:c0 + 512], start=True, stop=True),
              waits=[(s_g, 1), (s_ld, ldc), (s_v, p.now(s_v))], inc=s_pe)
            E("vector", lambda v, i=i, c0=c0: v.tensor_copy(
                lnbc[i][:, c0:c0 + 512], pbc[:]),
              waits=[(s_pe, p.now(s_pe))], inc=s_v, drain=True)
    for scr in (x1t_scr, x3t_scr):
        for col in (0, T + 1):
            E("gpsimd", lambda g, scr=scr, col=col: g.dma_start(
                out=chunkmajor_ap(scr, D, T + 2, 1, col0=col), in_=zcolb[:],
                allow_slow_non_contiguous=True),
              waits=[(s_g, NGS)], inc=s_st, amt=16)
    ld7 = p.now(s_ld)
    E("scalar", lambda s: s.activation(C(11), C(4), AF.Exp),
      waits=[(s_ld, ld7)], inc=s_sc, drain=True)
    E("scalar", lambda s: s.activation(C(7), C(11), AF.Exp, scale=-1.0),
      inc=s_sc, drain=True)
    E("scalar", lambda s: s.activation(C(8), C(5), AF.Exp), inc=s_sc, drain=True)
    sc0 = p.now(s_sc)
    E("vector", lambda v: v.tensor_scalar(C(9), C(0), -1.0, 1.0, ALU.mult, ALU.add),
      waits=[(s_ld, ld7), (s_sc, sc0)], inc=s_v, drain=True)
    E("vector", lambda v: v.tensor_tensor(C(11), C(1), C(6), ALU.mult), inc=s_v, drain=True)
    E("vector", lambda v: v.tensor_tensor(C(9), C(9), C(11), ALU.add), inc=s_v, drain=True)
    E("vector", lambda v: v.tensor_scalar(C(11), C(6), -1.0, 1.0, ALU.mult, ALU.add),
      inc=s_v, drain=True)
    E("vector", lambda v: v.tensor_tensor(C(10), C(1), C(11), ALU.mult), inc=s_v, drain=True)
    for (dst, src) in ((wk_sb, wktD), (wv_sb, wvtD), (wr_sb, wrtD), (wo_sb, wotD)):
        E("gpsimd", lambda g, dst=dst, src=src: g.dma_start(
            out=dst[:], in_=chunkmajor_ap(src, D, D, D)), inc=s_wt, amt=16)
    wt_tm = p.now(s_wt)

    def ln_stats(src, eps_ap, wv):
        E("vector", lambda v: v.tensor_reduce(stats[:, 0:1], src[:], AX.X, ALU.add),
          waits=wv, inc=s_v, drain=True)
        E("vector", lambda v: v.tensor_tensor(scratch[:], src[:], src[:], ALU.mult),
          inc=s_v, drain=True)
        E("vector", lambda v: v.tensor_reduce(stats[:, 2:3], scratch[:], AX.X, ALU.add),
          inc=s_v, drain=True)
        E("vector", lambda v: v.tensor_scalar(stats[:, 1:2], stats[:, 0:1],
                                              LNI, 0.0, ALU.mult, ALU.add),
          inc=s_v, drain=True)
        E("vector", lambda v: v.tensor_scalar(stats[:, 3:4], stats[:, 2:3],
                                              LNI, 0.0, ALU.mult, ALU.add),
          inc=s_v, drain=True)
        E("vector", lambda v: v.tensor_tensor(stats[:, 4:5], stats[:, 1:2],
                                              stats[:, 1:2], ALU.mult), inc=s_v, drain=True)
        E("vector", lambda v: v.tensor_tensor(stats[:, 5:6], stats[:, 3:4],
                                              stats[:, 4:5], ALU.subtract),
          inc=s_v, drain=True)
        vc = p.now(s_v)
        E("scalar", lambda s: s.activation(stats[:, 6:7], stats[:, 5:6], AF.Ln,
                                           bias=epsc[:, 0:1]),
          waits=[(s_v, vc), (s_g, NGS)], inc=s_sc, drain=True)
        E("scalar", lambda s: s.activation(stats[:, 7:8], stats[:, 6:7],
                                           AF.Exp, scale=-0.5), inc=s_sc, drain=True)
        return p.now(s_sc)

    rel_ptr = 0
    rel_hts_pe = 0
    need_hts = 0

    def transpose_store(srcbf, scr, col0, extra_pe_wait=0):
        nonlocal rel_ptr, need_hts
        vsrc = p.now(s_v)
        for c in range(DC):
            E("tensor", lambda t, c=c, srcbf=srcbf: t.transpose(
                ptr[:, c * 128:(c + 1) * 128], srcbf[:, c * 128:(c + 1) * 128], identb[:]),
              waits=[(s_v, vsrc), (s_wt, 16), (s_v, rel_ptr)],
              inc=s_pe if c == DC - 1 else None)
        pe_tr = p.now(s_pe)
        E("vector", lambda v: v.tensor_copy(hts[:], ptr[:]),
          waits=[(s_pe, pe_tr), (s_st, need_hts)], inc=s_v, drain=True)
        rel_ptr = p.now(s_v)
        E("gpsimd", lambda g, scr=scr, col0=col0: g.dma_start(
            out=chunkmajor_ap(scr, D, T + 2, 128, col0=col0), in_=hts[:]),
          waits=[(s_v, rel_ptr)], inc=s_st, amt=16)
        need_hts = p.now(s_st)

    rel_rbuf = [0, 0]
    rel_x1til = [p.now(s_st), p.now(s_st)]
    for i in range(NT):
        b = i % 2
        E("sync", lambda y, i=i, b=b: y.dma_start(
            out=rbuf[b][:], in_=res2.ap()[i * 128:(i + 1) * 128, :]),
          waits=[(s_v, rel_rbuf[b])], inc=s_ld, amt=16)
        ldc = p.now(s_ld)
        sc = ln_stats(rbuf[b], epsc[:, 0:1], [(s_ld, ldc)])
        E("vector", lambda v, b=b: v.tensor_scalar(
            h32[:], rbuf[b][:], stats[:, 1:2], stats[:, 7:8], ALU.subtract, ALU.mult),
          waits=[(s_sc, sc)], inc=s_v, drain=True)
        rel_rbuf[b] = p.now(s_v)
        sc = ln_stats(h32, epsc[:, 0:1], [])
        E("vector", lambda v: v.tensor_scalar(
            x3w[:], h32[:], stats[:, 1:2], stats[:, 7:8], ALU.subtract, ALU.mult),
          waits=[(s_sc, sc)], inc=s_v, drain=True)
        E("vector", lambda v: v.tensor_tensor(x3w[:], x3w[:], lnbc[0][:], ALU.mult),
          inc=s_v, drain=True)
        E("vector", lambda v, b=b: v.tensor_tensor(x1til[b][:], x3w[:], lnbc[1][:],
                                                   ALU.add),
          waits=[(s_st, rel_x1til[b])], inc=s_v, drain=True)
        vx1 = p.now(s_v)
        E("gpsimd", lambda g, i=i, b=b: g.dma_start(
            out=x1_scr.ap()[i * 128:(i + 1) * 128, :], in_=x1til[b][:]),
          waits=[(s_v, vx1)], inc=s_st, amt=16)
        rel_x1til[b] = p.now(s_st)
        E("vector", lambda v, b=b: v.tensor_copy(hbf[:], x1til[b][:]), inc=s_v, drain=True)
        transpose_store(hbf, x1t_scr, 1 + i * 128)
    p1_st = p.now(s_st)

    rel_x1tc = 0
    rel_xm = 0
    rel_Pc = 0
    rel_pmA_sc = 0
    rel_pmA_v = 0
    rel_pmB_v = 0
    rel_pmC_v = 0

    def ln_aff2_store(x2src, tile_idx, b):
        sc = ln_stats(x2src, epsc[:, 0:1], [])
        E("vector", lambda v: v.tensor_scalar(
            x3w[:], x2src[:], stats[:, 1:2], stats[:, 7:8], ALU.subtract, ALU.mult),
          waits=[(s_sc, sc)], inc=s_v, drain=True)
        E("vector", lambda v: v.tensor_tensor(x3w[:], x3w[:], lnbc[2][:], ALU.mult),
          inc=s_v, drain=True)
        E("vector", lambda v, b=b: v.tensor_tensor(x1til[b][:], x3w[:], lnbc[3][:],
                                                   ALU.add),
          waits=[(s_st, rel_x1til[b])], inc=s_v, drain=True)
        vx3 = p.now(s_v)
        E("gpsimd", lambda g, tile_idx=tile_idx, b=b: g.dma_start(
            out=x3_scr.ap()[tile_idx * 128:(tile_idx + 1) * 128, :], in_=x1til[b][:]),
          waits=[(s_v, vx3)], inc=s_st, amt=16)
        E("vector", lambda v, b=b: v.tensor_copy(hbf[:], x1til[b][:]), inc=s_v, drain=True)
        transpose_store(hbf, x3t_scr, 1 + tile_idx * 128)

    for c in range(8):
        E("sync", lambda y, c=c: y.dma_start(
            out=x1t_c[:], in_=chunkmajor_ap(x1t_scr, D, T + 2, 514, col0=c * 512)),
          waits=[(s_st, p1_st if c == 0 else need_hts), (s_v, rel_x1tc),
                 (s_pe, rel_xm)],
          inc=s_ld, amt=16)
        ld_x1t = p.now(s_ld)
        for g_ in range(8):
            base = g_ * 514
            E("vector", lambda v, g_=g_, base=base: v.tensor_scalar(
                gt1[:], x1t_c[:, base + 1: base + 513], C(0)[:, g_:g_ + 1], 0.0,
                ALU.mult, ALU.add),
              waits=[(s_ld, ld_x1t), (s_pe, rel_xm)], inc=s_v, drain=True)
            E("vector", lambda v, g_=g_, base=base: v.scalar_tensor_tensor(
                gt1[:], x1t_c[:, base: base + 512], C(9)[:, g_:g_ + 1], gt1[:],
                ALU.mult, ALU.add), inc=s_v, drain=True)
            E("vector", lambda v, g_=g_, base=base: v.scalar_tensor_tensor(
                xm_c[:, g_ * 512:(g_ + 1) * 512], x1t_c[:, base + 2: base + 514],
                C(10)[:, g_:g_ + 1], gt1[:], ALU.mult, ALU.add), inc=s_v, drain=True)
        rel_x1tc = p.now(s_v)
        v_xm = p.now(s_v)
        for g_ in range(8):
            for cc in range(DC):
                E("tensor", lambda t, g_=g_, cc=cc: t.matmul(
                    pmA[:], wk_sb[:, cc * D + g_ * 128: cc * D + g_ * 128 + 128],
                    xm_c[:, cc * 512:(cc + 1) * 512],
                    start=(cc == 0), stop=(cc == DC - 1)),
                  waits=[(s_wt, wt_tm), (s_v, v_xm), (s_v, rel_pmA_v),
                         (s_sc, rel_pmA_sc)],
                  inc=s_pe if cc == DC - 1 else None)
            pe_k = p.now(s_pe)
            E("vector", lambda v: v.tensor_scalar(gt1[:], pmA[:], 60.0, 0.0,
                                                  ALU.min, ALU.add),
              waits=[(s_pe, pe_k)], inc=s_v, drain=True)
            rel_pmA_v = p.now(s_v)
            E("scalar", lambda s: s.activation(gk[:], gt1[:], AF.Exp),
              waits=[(s_v, rel_pmA_v)], inc=s_sc, drain=True)
            sc_k = p.now(s_sc)
            for cc in range(DC):
                E("tensor", lambda t, g_=g_, cc=cc: t.matmul(
                    pmB[:], wv_sb[:, cc * D + g_ * 128: cc * D + g_ * 128 + 128],
                    xm_c[:, cc * 512:(cc + 1) * 512],
                    start=(cc == 0), stop=(cc == DC - 1)),
                  waits=[(s_v, rel_pmB_v)], inc=s_pe if cc == DC - 1 else None)
            pe_v = p.now(s_pe)
            E("vector", lambda v: v.tensor_copy(gv[:], pmB[:]),
              waits=[(s_pe, pe_v)], inc=s_v, drain=True)
            rel_pmB_v = p.now(s_v)
            for cc in range(DC):
                E("tensor", lambda t, g_=g_, cc=cc: t.matmul(
                    pmC[:], wr_sb[:, cc * D + g_ * 128: cc * D + g_ * 128 + 128],
                    xm_c[:, cc * 512:(cc + 1) * 512],
                    start=(cc == 0), stop=(cc == DC - 1)),
                  waits=[(s_v, rel_pmC_v)], inc=s_pe if cc == DC - 1 else None)
            pe_r = p.now(s_pe)
            E("scalar", lambda s: s.activation(gt2[:], pmC[:], AF.Exp, scale=-1.0),
              waits=[(s_pe, pe_r), (s_sc, sc_k)], inc=s_sc, drain=True)
            sc_r = p.now(s_sc)
            E("vector", lambda v: v.tensor_scalar(gt2[:], gt2[:], 1.0, 0.0,
                                                  ALU.add, ALU.add),
              waits=[(s_sc, sc_r)], inc=s_v, drain=True)
            rel_pmC_v = p.now(s_v)
            E("vector", lambda v: v.reciprocal(gsr[:], gt2[:]), inc=s_v, drain=True)
            E("vector", lambda v: v.tensor_tensor(gkv[:], gk[:], gv[:], ALU.mult),
              waits=[(s_sc, sc_k)], inc=s_v, drain=True)
            E("vector", lambda v, g_=g_: v.tensor_copy(Abuf[:, 0:1], carryA[:, g_:g_ + 1]),
              inc=s_v, drain=True)
            E("vector", lambda v, g_=g_: v.tensor_copy(Bbuf[:, 0:1], carryB[:, g_:g_ + 1]),
              inc=s_v, drain=True)
            E("vector", lambda v, g_=g_: v.tensor_scalar(
                edb[:], ones512[:], C(7)[:, g_:g_ + 1], 0.0, ALU.mult, ALU.add),
              waits=[(s_g, 2)], inc=s_v, drain=True)
            E("vector", lambda v, g_=g_: v.tensor_tensor_scan(
                Abuf[:, 1:513], edb[:], gkv[:], carryA[:, g_:g_ + 1],
                ALU.mult, ALU.add), inc=s_v, drain=True)
            E("vector", lambda v, g_=g_: v.tensor_tensor_scan(
                Bbuf[:, 1:513], edb[:], gk[:], carryB[:, g_:g_ + 1],
                ALU.mult, ALU.add), inc=s_v, drain=True)
            E("vector", lambda v, g_=g_: v.tensor_copy(carryA[:, g_:g_ + 1],
                                                       Abuf[:, 512:513]),
              inc=s_v, drain=True)
            E("vector", lambda v, g_=g_: v.tensor_copy(carryB[:, g_:g_ + 1],
                                                       Bbuf[:, 512:513]),
              inc=s_v, drain=True)
            E("vector", lambda v, g_=g_: v.scalar_tensor_tensor(
                gt1[:], gkv[:], C(8)[:, g_:g_ + 1], Abuf[:, 0:512], ALU.mult, ALU.add),
              inc=s_v, drain=True)
            E("vector", lambda v, g_=g_: v.scalar_tensor_tensor(
                gt2[:], gk[:], C(8)[:, g_:g_ + 1], Bbuf[:, 0:512], ALU.mult, ALU.add),
              inc=s_v, drain=True)
            E("vector", lambda v: v.reciprocal(gt2[:], gt2[:]), inc=s_v, drain=True)
            E("vector", lambda v: v.tensor_tensor(gy[:], gt1[:], gt2[:], ALU.mult),
              inc=s_v, drain=True)
            E("vector", lambda v, g_=g_: v.tensor_tensor(
                P_c[:, g_ * 512:(g_ + 1) * 512], gy[:], gsr[:], ALU.mult),
              waits=[(s_pe, rel_Pc)], inc=s_v, drain=True)
        v_P = p.now(s_v)
        for ts4 in range(4):
            ti = c * 4 + ts4
            b = ts4 % 2
            E("sync", lambda y, ti=ti, b=b: y.dma_start(
                out=rbuf[b][:], in_=x1_scr.ap()[ti * 128:(ti + 1) * 128, :]),
              waits=[(s_st, p1_st), (s_v, rel_rbuf[b])], inc=s_ld, amt=16)
            ld_x1 = p.now(s_ld)
            for co in range(2):
                for cc in range(DC):
                    E("tensor", lambda t, ts4=ts4, co=co, cc=cc: t.matmul(
                        pmD[:], P_c[:, cc * 512 + ts4 * 128: cc * 512 + ts4 * 128 + 128],
                        wo_sb[:, cc * D + co * 512: cc * D + co * 512 + 512],
                        start=(cc == 0), stop=(cc == DC - 1)),
                      waits=[(s_v, v_P), (s_v, rel_pmA_v)],
                      inc=s_pe if cc == DC - 1 else None)
                pe_wo = p.now(s_pe)
                E("vector", lambda v, co=co, b=b: v.tensor_tensor(
                    h32[:, co * 512:(co + 1) * 512], pmD[:],
                    rbuf[b][:, co * 512:(co + 1) * 512], ALU.add),
                  waits=[(s_pe, pe_wo), (s_ld, ld_x1)], inc=s_v, drain=True)
                rel_pmA_v = p.now(s_v)
            rel_rbuf[b] = p.now(s_v)
            ln_aff2_store(h32, ti, b)
        rel_Pc = p.now(s_pe)
        rel_xm = p.now(s_pe)
    p2_st = p.now(s_st)
    p2_pe = p.now(s_pe)

    vnow = p.now(s_v)
    E("vector", lambda v: v.tensor_scalar(C(9), C(2), -1.0, 1.0, ALU.mult, ALU.add),
      waits=[(s_v, vnow)], inc=s_v, drain=True)
    E("vector", lambda v: v.tensor_tensor(C(11), C(3), C(6), ALU.mult), inc=s_v, drain=True)
    E("vector", lambda v: v.tensor_tensor(C(9), C(9), C(11), ALU.add), inc=s_v, drain=True)
    E("vector", lambda v: v.tensor_scalar(C(11), C(6), -1.0, 1.0, ALU.mult, ALU.add),
      inc=s_v, drain=True)
    E("vector", lambda v: v.tensor_tensor(C(10), C(3), C(11), ALU.mult), inc=s_v, drain=True)
    rel_xmst = 0
    for c in range(8):
        E("sync", lambda y, c=c: y.dma_start(
            out=x1t_c[:], in_=chunkmajor_ap(x3t_scr, D, T + 2, 514, col0=c * 512)),
          waits=[(s_st, p2_st), (s_v, rel_x1tc)], inc=s_ld, amt=16)
        ld_x3t = p.now(s_ld)
        for g_ in range(8):
            base = g_ * 514
            E("vector", lambda v, g_=g_, base=base: v.tensor_scalar(
                gt1[:], x1t_c[:, base + 1: base + 513], C(2)[:, g_:g_ + 1], 0.0,
                ALU.mult, ALU.add),
              waits=[(s_ld, ld_x3t), (s_st, rel_xmst)], inc=s_v, drain=True)
            E("vector", lambda v, g_=g_, base=base: v.scalar_tensor_tensor(
                gt1[:], x1t_c[:, base: base + 512], C(9)[:, g_:g_ + 1], gt1[:],
                ALU.mult, ALU.add), inc=s_v, drain=True)
            E("vector", lambda v, g_=g_, base=base: v.scalar_tensor_tensor(
                xm_c[:, g_ * 512:(g_ + 1) * 512], x1t_c[:, base + 2: base + 514],
                C(10)[:, g_:g_ + 1], gt1[:], ALU.mult, ALU.add), inc=s_v, drain=True)
        rel_x1tc = p.now(s_v)
        v_xm2 = p.now(s_v)
        E("gpsimd", lambda g, c=c: g.dma_start(
            out=chunkmajor_ap(xm2t_scr, D, T, 512, col0=c * 512), in_=xm_c[:]),
          waits=[(s_v, v_xm2)], inc=s_st, amt=16)
        rel_xmst = p.now(s_st)
    p3a_st = p.now(s_st)

    rel_xs = [0, 0]
    rel_k2 = 0
    rel_kst = [p3a_st, p3a_st]
    rel_pm1sc = 0
    prev_kv_st = 0
    for blk in range(4):
        E("gpsimd", lambda g, blk=blk: g.dma_start(
            out=fkb[:], in_=chunkmajor_ap(fktD, D, 4096, 1024, col0=blk * 1024)),
          waits=[(s_pe, p2_pe if blk == 0 else p.now(s_pe))], inc=s_wt, amt=16)
        E("gpsimd", lambda g, blk=blk: g.dma_start(
            out=fvb[:], in_=chunkmajor_ap(fvtD, 4096, D, D, col0=blk * 1024 * D,
                                          nrow_chunks=8)),
          waits=[], inc=s_wt, amt=16)
        wt_cm = p.now(s_wt)
        for ti in range(NT):
            b2 = ti % 2
            E("sync", lambda y, ti=ti, b2=b2: y.dma_start(
                out=xs[b2][:], in_=chunkmajor_ap(xm2t_scr, D, T, 128, col0=ti * 128)),
              waits=[(s_st, p3a_st), (s_pe, rel_xs[b2])], inc=s_ld, amt=16)
            ld_xs = p.now(s_ld)
            for s_ in range(8):
                for cc in range(DC):
                    E("tensor", lambda t, s_=s_, cc=cc, b2=b2: t.matmul(
                        pmA[:, 0:128],
                        fkb[:, cc * 1024 + s_ * 128: cc * 1024 + s_ * 128 + 128],
                        xs[b2][:, cc * 128:(cc + 1) * 128],
                        start=(cc == 0), stop=(cc == DC - 1)),
                      waits=[(s_wt, wt_cm), (s_ld, ld_xs), (s_v, rel_pmA_v),
                             (s_sc, rel_pm1sc)],
                      inc=s_pe if cc == DC - 1 else None)
                pe1 = p.now(s_pe)
                E("vector", lambda v: v.tensor_scalar(sq_in[:], pmA[:, 0:128], 0.0, 0.0,
                                                      ALU.max, ALU.add),
                  waits=[(s_pe, pe1), (s_sc, rel_pm1sc)], inc=s_v, drain=True)
                rel_pmA_v = p.now(s_v)
                E("scalar", lambda s, s_=s_: s.activation(
                    k2sq[:, s_ * 128:(s_ + 1) * 128], sq_in[:], AF.Square),
                  waits=[(s_v, rel_pmA_v), (s_pe, rel_k2)], inc=s_sc, drain=True)
                rel_pm1sc = p.now(s_sc)
            sc_k2 = p.now(s_sc)
            for co in range(2):
                for s_ in range(8):
                    E("tensor", lambda t, s_=s_, co=co: t.matmul(
                        pmB[:],
                        k2sq[:, s_ * 128:(s_ + 1) * 128],
                        fvb[:, s_ * D + co * 512: s_ * D + co * 512 + 512],
                        start=(s_ == 0), stop=(s_ == 7)),
                      waits=[(s_sc, sc_k2), (s_v, rel_pmB_v)],
                      inc=s_pe if s_ == 7 else None)
                pe2 = p.now(s_pe)
                E("vector", lambda v, co=co, b2=b2: v.tensor_copy(
                    kstage[b2][:, co * 512:(co + 1) * 512], pmB[:]),
                  waits=[(s_pe, pe2), (s_st, rel_kst[b2])], inc=s_v, drain=True)
                rel_pmB_v = p.now(s_v)
            v_kst = p.now(s_v)
            if blk == 0:
                E("gpsimd", lambda g, ti=ti, b2=b2: g.dma_start(
                    out=kv_scr.ap()[ti * 128:(ti + 1) * 128, :], in_=kstage[b2][:]),
                  waits=[(s_v, v_kst)], inc=s_st, amt=16)
            else:
                E("gpsimd", lambda g, ti=ti, b2=b2: g.dma_start(
                    out=kv_scr.ap()[ti * 128:(ti + 1) * 128, :], in_=kstage[b2][:],
                    accum_op=ALU.add),
                  waits=[(s_v, v_kst), (s_st, prev_kv_st)], inc=s_st, amt=16)
            rel_kst[b2] = p.now(s_st)
            rel_k2 = p.now(s_pe)
            rel_xs[b2] = p.now(s_pe)
        prev_kv_st = p.now(s_st)
    kv_done = p.now(s_st)

    E("gpsimd", lambda g: g.dma_start(
        out=frt_sb[:], in_=chunkmajor_ap(frtD, D, D, D)),
      waits=[(s_pe, p.now(s_pe))], inc=s_wt, amt=16)
    wt_fr = p.now(s_wt)
    rel_obuf = 0
    rel_srv = 0
    for ti in range(NT):
        b2 = ti % 2
        b = ti % 2
        E("sync", lambda y, ti=ti, b2=b2: y.dma_start(
            out=xs[b2][:], in_=chunkmajor_ap(xm2t_scr, D, T, 128, col0=ti * 128)),
          waits=[(s_st, p3a_st), (s_pe, rel_xs[b2])], inc=s_ld, amt=16)
        E("sync", lambda y, ti=ti, b=b: y.dma_start(
            out=rbuf[b][:], in_=x3_scr.ap()[ti * 128:(ti + 1) * 128, :]),
          waits=[(s_v, rel_rbuf[b])], inc=s_ld, amt=16)
        E("sync", lambda y, ti=ti, b=b: y.dma_start(
            out=x1til[b][:], in_=res2.ap()[ti * 128:(ti + 1) * 128, :]),
          waits=[(s_v, rel_x1til[b] if isinstance(rel_x1til[b], int) else 0)],
          inc=s_ld, amt=16)
        E("sync", lambda y, ti=ti, b=b: y.dma_start(
            out=kstage[b][:], in_=kv_scr.ap()[ti * 128:(ti + 1) * 128, :]),
          waits=[(s_st, kv_done), (s_v, rel_srv)], inc=s_ld, amt=16)
        ld3 = p.now(s_ld)
        for co in range(2):
            for cc in range(DC):
                E("tensor", lambda t, co=co, cc=cc, b2=b2: t.matmul(
                    pmC[:],
                    xs[b2][:, cc * 128:(cc + 1) * 128],
                    frt_sb[:, cc * D + co * 512: cc * D + co * 512 + 512],
                    start=(cc == 0), stop=(cc == DC - 1)),
                  waits=[(s_wt, wt_fr), (s_ld, ld3), (s_v, rel_pmC_v)],
                  inc=s_pe if cc == DC - 1 else None)
            pe_fr = p.now(s_pe)
            E("scalar", lambda s, co=co: s.activation(
                srt[:, co * 512:(co + 1) * 512], pmC[:], AF.Exp, scale=-1.0),
              waits=[(s_pe, pe_fr)], inc=s_sc, drain=True)
            scx = p.now(s_sc)
            E("vector", lambda v, co=co: v.tensor_scalar(
                srt[:, co * 512:(co + 1) * 512], srt[:, co * 512:(co + 1) * 512],
                1.0, 0.0, ALU.add, ALU.add),
              waits=[(s_sc, scx)], inc=s_v, drain=True)
            rel_pmC_v = p.now(s_v)
            E("vector", lambda v, co=co: v.reciprocal(
                srt[:, co * 512:(co + 1) * 512], srt[:, co * 512:(co + 1) * 512]),
              inc=s_v, drain=True)
        rel_xs[b2] = p.now(s_pe)
        E("vector", lambda v, b=b: v.tensor_tensor(srt[:], srt[:], kstage[b][:],
                                                   ALU.mult), inc=s_v, drain=True)
        rel_srv = p.now(s_v)
        E("vector", lambda v, b=b: v.tensor_tensor(srt[:], srt[:], rbuf[b][:], ALU.add),
          inc=s_v, drain=True)
        rel_rbuf[b] = p.now(s_v)
        E("vector", lambda v, b=b: v.tensor_tensor(obuf[:], srt[:], x1til[b][:], ALU.add),
          waits=[(s_st, rel_obuf)], inc=s_v, drain=True)
        rel_x1til[b] = p.now(s_v)
        vo = p.now(s_v)
        E("gpsimd", lambda g, ti=ti: g.dma_start(
            out=out.ap()[ti * 128:(ti + 1) * 128, :], in_=obuf[:]),
          waits=[(s_v, vo)], inc=s_st, amt=16)
        rel_obuf = p.now(s_st)

    fin = p.now(s_st)
    E("gpsimd", lambda g: g.nop(), waits=[(s_st, fin)])
    return p.finalize() if fin_own else None


def prep_launch_c(inputs):
    def colsfmt(vec):
        return np.ascontiguousarray(
            np.asarray(vec, np.float32).reshape(8, 128).T)
    d = dict(
        ln1g=np.asarray(inputs["L1_attn_ln1_g"], np.float32).reshape(1, D),
        ln1b=np.asarray(inputs["L1_attn_ln1_b"], np.float32).reshape(1, D),
        ln2g=np.asarray(inputs["L1_attn_ln2_g"], np.float32).reshape(1, D),
        ln2b=np.asarray(inputs["L1_attn_ln2_b"], np.float32).reshape(1, D),
        tmc=colsfmt(inputs["L1_attn_att_time_mix"]),
        cmc=colsfmt(inputs["L1_attn_att_combined_mix"]),
        ftmc=colsfmt(inputs["L1_attn_ffn_time_mix"]),
        fcmc=colsfmt(inputs["L1_attn_ffn_combined_mix"]),
        tdc=colsfmt(inputs["L1_attn_time_decay"]),
        tfc=colsfmt(inputs["L1_attn_time_first"]),
        mlo=colsfmt((np.arange(1024) < 512).astype(np.float32)),
        wkt=np.ascontiguousarray(np.asarray(inputs["L1_attn_Wk"], np.float32).T),
        wvt=np.ascontiguousarray(np.asarray(inputs["L1_attn_Wv"], np.float32).T),
        wrt=np.ascontiguousarray(np.asarray(inputs["L1_attn_Wr"], np.float32).T),
        wot=np.ascontiguousarray(np.asarray(inputs["L1_attn_Wo"], np.float32).T),
        fkt=np.ascontiguousarray(np.asarray(inputs["L1_attn_Fk"], np.float32).T),
        fvt=np.ascontiguousarray(np.asarray(inputs["L1_attn_Fv"], np.float32).T),
        frt=np.ascontiguousarray(np.asarray(inputs["L1_attn_Fr"], np.float32).T),
        ident=np.eye(128, dtype=np.float32),
    )
    return d


def run_launch_c(res2_full, inputs):
    nc = build_launch_c()
    com = prep_launch_c(inputs)
    in_maps = []
    for b in range(8):
        m = dict(com)
        m["res2"] = np.ascontiguousarray(res2_full[b])
        in_maps.append(m)
    res = run_bass_kernel_spmd(nc, in_maps, core_ids=list(range(8)))
    return np.stack([r["out"] for r in res.results])



SCALING = 64.0 ** -0.5

def build_launch_a(stop_after=None, dbg_p=False, ctx=None):
    nc, p, DT, sb, psum, fin_own = _ctx_or_new(ctx, "a")


    xin = DT("xin", [T, D], F32, "ExternalInput")
    wqD = DT("wq", [D, D], F32, "ExternalInput")
    wkD = DT("wk", [D, D], F32, "ExternalInput")
    wvD = DT("wv", [D, D], F32, "ExternalInput")
    woD = DT("wo", [D, D], F32, "ExternalInput")
    bqD = DT("bq", [128, 8], F32, "ExternalInput")
    bkD = DT("bk", [128, 8], F32, "ExternalInput")
    bvD = DT("bv", [1, D], F32, "ExternalInput")
    boD = DT("bo", [1, D], F32, "ExternalInput")
    ohD = DT("oh", [32, 1280], F32, "ExternalInput")
    relD = DT("rel", [32, 16], F32, "ExternalInput")
    gwtD = DT("gwt", [8, 64], F32, "ExternalInput")
    gbtD = DT("gbt", [8, 1], F32, "ExternalInput")
    ob2D = DT("ob2", [8, 2], F32, "ExternalInput")
    gc16D = DT("gc16", [16, 1], F32, "ExternalInput")
    flipD = DT("flip", [128, 128], F32, "ExternalInput")
    ident = DT("ident", [128, 128], F32, "ExternalInput")
    out = DT("out", [T, D], F32, "ExternalOutput")

    if dbg_p:
        dbg_pfr = DT("dbg_pfr", [128, 512], BF16, "ExternalOutput")
        dbg_pfl = DT("dbg_pfl", [128, 512], BF16, "ExternalOutput")
        dbg_row = DT("dbg_row", [1, 512], F32, "ExternalOutput")
    qt_scr = DT("qt_scr", [D, T], BF16, "Internal")
    kt_scr = DT("kt_scr", [D, T], BF16, "Internal")
    v_scr = DT("v_scr", [T, D], BF16, "Internal")
    at_scr = DT("at_scr", [D, T], BF16, "Internal")
    gate_scr = DT("gate_scr", [16, T], F32, "Internal")
    tblr_hbm = DT("tblr_hbm", [16, 1280], BF16, "Internal")
    csc_hbm = DT("csc_hbm", [16, 2], F32, "Internal")

    rbuf = [sb(f"rbuf{i}", [128, D], F32) for i in range(2)]
    scratch = sb("scratch", [128, D], F32)
    hbf = sb("hbf", [128, D], BF16)
    hts = sb("hts", [128, D], BF16)
    obuf = sb("obuf", [128, D], F32)
    stats = sb("stats", [128, 16], F32)
    epsc = sb("epsc", [128, 1], F32)
    onesf = sb("onesf", [1, 128], F32)
    onesrb = sb("onesrb", [1, 512], BF16)
    identb = sb("identb", [128, 128], BF16)
    flipb = sb("flipb", [128, 128], BF16)
    xlnT = sb("xlnT", [128, 8 * T], BF16)
    wq_sb = sb("wq_sb", [128, 8 * D], BF16)
    wk_sb = sb("wk_sb", [128, 8 * D], BF16)
    wv_sb = sb("wv_sb", [128, 8 * D], BF16)
    wo_sb = wq_sb
    bqc = sb("bqc", [128, 8], F32)
    bkc = sb("bkc", [128, 8], F32)
    bvb = sb("bvb", [128, D], F32)
    bob = sb("bob", [1, D], BF16)
    stage = [sb(f"stage{i}", [128, 512], BF16) for i in range(2)]
    relb = sb("relb", [32, 16], BF16)
    ohb = sb("ohb", [32, 1280], BF16)
    tblf = sb("tblf", [16, 1280], F32)
    tblb = sb("tblb", [16, 1280], BF16)
    gwtb = sb("gwtb", [8, 64], BF16)
    gbtb = sb("gbtb", [8, 1], BF16)
    ob2b = sb("ob2b", [8, 2], BF16)
    gw2 = sb("gw2", [64, 2], F32)
    gbs = sb("gbs", [1, 2], F32)
    wa_sb = sb("wa_sb", [128, 8 * 16], BF16)
    wb_sb = sb("wb_sb", [128, 8 * 16], BF16)
    bsrA = sb("bsrA", [1, 16], F32)
    bsrB = sb("bsrB", [1, 16], F32)
    bsrAb = sb("bsrAb", [1, 16], BF16)
    bsrBb = sb("bsrBb", [1, 16], BF16)
    gc16 = sb("gc16s", [16, 1], F32)
    sga = sb("sga", [16, 512], F32)
    sgb = sb("sgb", [16, 512], F32)
    gt16 = sb("gt16", [16, 512], F32)
    kth = sb("kth", [64, T], BF16)
    vh = sb("vh", [128, 32 * 65], BF16)
    qsl = sb("qsl", [64, 512], BF16)
    grow = sb("grow", [1, T], F32)
    crow = sb("crow", [1, 2], F32)
    rowF = sb("rowF", [1, 512], F32)
    gb_sb = sb("gb_sb", [128, 512], BF16)
    Bt = sb("Bt", [128, 512], BF16)
    Bg = sb("Bg", [128, 512], BF16)
    Pb = [sb(f"Pb{i}", [128, 512], BF16) for i in range(3)]
    denr = sb("denr", [1, 512], F32)
    dr_sb = sb("dr_sb", [64, 512], F32)
    atst = sb("atst", [64, 512], BF16)
    xs2 = [sb(f"xs2{i}", [128, 8 * 128], BF16) for i in range(2)]

    pS = [psum(f"pS{i}", [128, 512], F32) for i in range(2)]
    pO = psum("pO", [65, 512], F32)
    pbc = psum("pbc", [128, 512], F32)
    ptr = psum("ptr", [128, 1024], BF16)
    pG = psum("pG", [16, 512], F32)

    s_ld = p.sem("s_ld")
    s_wt = p.sem("s_wt")
    s_pe = p.sem("s_pe", owner="tensor")
    s_v = p.sem("s_v", owner="vector")
    s_sc = p.sem("s_sc", owner="scalar")
    s_st = p.sem("s_st")
    s_g = p.sem("s_g", owner="gpsimd")

    E = p.emit
    LNI = 1.0 / D

    E("gpsimd", lambda g: g.memset(onesf[:], 1.0), inc=s_g)
    E("gpsimd", lambda g: g.memset(onesrb[:], 1.0), inc=s_g)
    E("gpsimd", lambda g: g.memset(epsc[:], 1e-5), inc=s_g)
    E("gpsimd", lambda g: g.memset(vh[:], 1.0), inc=s_g)
    NGS = 4
    for (dst, src) in ((identb, ident), (flipb, flipD), (relb, relD), (ohb, ohD),
                       (gwtb, gwtD), (gbtb, gbtD), (ob2b, ob2D), (bob, boD)):
        E("gpsimd", lambda g, dst=dst, src=src: g.dma_start(out=dst[:], in_=src.ap()),
          inc=s_wt, amt=16)
    for (dst, src) in ((bqc, bqD), (bkc, bkD), (gc16, gc16D)):
        E("sync", lambda y, dst=dst, src=src: y.dma_start(out=dst[:], in_=src.ap()),
          inc=s_ld, amt=16)
    if stop_after == 's0':
        E("gpsimd", lambda g: g.dma_start(out=out.ap()[0:128, :], in_=rbuf[0][:]),
          waits=[(s_wt, p.now(s_wt)), (s_ld, p.now(s_ld)), (s_g, p.now(s_g))],
          inc=s_st, amt=16)
        fin0 = p.now(s_st)
        E("gpsimd", lambda g: g.nop(), waits=[(s_st, fin0)])
        return p.finalize()
    rowbuf = scratch[0:1, :]
    E("sync", lambda y: y.dma_start(out=rowbuf, in_=bvD.ap()), inc=s_ld, amt=16)
    ld0 = p.now(s_ld)
    if stop_after == 's0a':
        E("gpsimd", lambda g: g.dma_start(out=out.ap()[0:128, :], in_=rbuf[0][:]),
          waits=[(s_ld, ld0), (s_wt, p.now(s_wt))], inc=s_st, amt=16)
        fin0 = p.now(s_st)
        E("gpsimd", lambda g: g.nop(), waits=[(s_st, fin0)])
        return p.finalize()
    for c0 in range(0, D, 512):
        E("tensor", lambda t, c0=c0: t.matmul(
            pbc[:], onesf[:], rowbuf[0:1, c0:c0 + 512], start=True, stop=True),
          waits=[(s_g, 1), (s_ld, ld0), (s_v, p.now(s_v))], inc=s_pe)
        E("vector", lambda v, c0=c0: v.tensor_copy(bvb[:, c0:c0 + 512], pbc[:]),
          waits=[(s_pe, p.now(s_pe))], inc=s_v, drain=True)
    if stop_after == 's1':
        E("gpsimd", lambda g: g.dma_start(out=out.ap()[0:128, :], in_=bvb[:]),
          waits=[(s_v, p.now(s_v))], inc=s_st, amt=16)
        fin0 = p.now(s_st)
        E("gpsimd", lambda g: g.nop(), waits=[(s_st, fin0)])
        return p.finalize()
    wt0 = p.now(s_wt)
    for c0 in range(0, 1280, 512):
        w = min(512, 1280 - c0)
        E("tensor", lambda t, c0=c0, w=w: t.matmul(
            pG[:, 0:w], relb[:], ohb[:, c0:c0 + w], start=True, stop=True),
          waits=[(s_wt, wt0), (s_v, p.now(s_v))], inc=s_pe)
        E("vector", lambda v, c0=c0, w=w: v.tensor_copy(tblf[:, c0:c0 + w], pG[:, 0:w]),
          waits=[(s_pe, p.now(s_pe))], inc=s_v, drain=True)
    E("vector", lambda v: v.tensor_copy(tblb[:], tblf[:]), inc=s_v, drain=True)
    vt = p.now(s_v)
    E("gpsimd", lambda g: g.dma_start(out=tblr_hbm.ap(), in_=tblb[:]),
      waits=[(s_v, vt)], inc=s_st, amt=16)
    E("vector", lambda v: v.tensor_copy(stats[0:16, 8:9], tblf[:, 0:1]), inc=s_v, drain=True)
    E("vector", lambda v: v.tensor_copy(stats[0:16, 9:10], tblf[:, 1279:1280]),
      inc=s_v, drain=True)
    vt = p.now(s_v)
    E("gpsimd", lambda g: g.dma_start(out=csc_hbm.ap(), in_=stats[0:16, 8:10]),
      waits=[(s_v, vt)], inc=s_st, amt=16)
    if stop_after == 's2':
        E("gpsimd", lambda g: g.dma_start(out=out.ap()[0:128, :], in_=bvb[:]),
          waits=[(s_st, p.now(s_st))], inc=s_st, amt=16)
        fin0 = p.now(s_st)
        E("gpsimd", lambda g: g.nop(), waits=[(s_st, fin0)])
        return p.finalize()
    E("tensor", lambda t: t.matmul(pbc[0:64, 0:2], gwtb[:], ob2b[:], start=True,
                                   stop=True),
      waits=[(s_wt, wt0), (s_v, p.now(s_v))], inc=s_pe)
    E("vector", lambda v: v.tensor_copy(gw2[:], pbc[0:64, 0:2]),
      waits=[(s_pe, p.now(s_pe))], inc=s_v, drain=True)
    E("tensor", lambda t: t.matmul(pbc[0:1, 0:2], gbtb[:], ob2b[:], start=True,
                                   stop=True),
      waits=[(s_v, p.now(s_v))], inc=s_pe)
    E("vector", lambda v: v.tensor_copy(gbs[:], pbc[0:1, 0:2]),
      waits=[(s_pe, p.now(s_pe))], inc=s_v, drain=True)
    E("vector", lambda v: v.tensor_scalar(bsrA[:], onesf[0:1, 0:16], gbs[0:1, 0:1],
                                          0.0, ALU.mult, ALU.add), inc=s_v, drain=True)
    E("vector", lambda v: v.tensor_scalar(bsrB[:], onesf[0:1, 0:16], gbs[0:1, 1:2],
                                          0.0, ALU.mult, ALU.add), inc=s_v, drain=True)
    E("vector", lambda v: v.tensor_copy(bsrAb[:], bsrA[:]), inc=s_v, drain=True)
    E("vector", lambda v: v.tensor_copy(bsrBb[:], bsrB[:]), inc=s_v, drain=True)
    v_gp = p.now(s_v)
    E("gpsimd", lambda g: g.memset(wa_sb[:], 0.0), inc=s_g)
    E("gpsimd", lambda g: g.memset(wb_sb[:], 0.0), inc=s_g)
    for h in range(16):
        p0 = (h % 2) * 64
        cc = h // 2
        E("gpsimd", lambda g, h=h, p0=p0, cc=cc: g.dma_start(
            out=wa_sb[p0:p0 + 64, cc * 16 + h: cc * 16 + h + 1], in_=gw2[:, 0:1]),
          waits=[(s_v, v_gp), (s_g, NGS + 2)], inc=s_wt, amt=16)
        E("gpsimd", lambda g, h=h, p0=p0, cc=cc: g.dma_start(
            out=wb_sb[p0:p0 + 64, cc * 16 + h: cc * 16 + h + 1], in_=gw2[:, 1:2]),
          inc=s_wt, amt=16)
    wt_wab = p.now(s_wt)
    if stop_after == 's3':
        E("gpsimd", lambda g: g.dma_start(out=out.ap()[0:128, :], in_=bvb[:]),
          waits=[(s_wt, p.now(s_wt))], inc=s_st, amt=16)
        fin0 = p.now(s_st)
        E("gpsimd", lambda g: g.nop(), waits=[(s_st, fin0)])
        return p.finalize()
    for (dst, src) in ((wq_sb, wqD), (wk_sb, wkD), (wv_sb, wvD)):
        E("gpsimd", lambda g, dst=dst, src=src: g.dma_start(
            out=dst[:], in_=chunkmajor_ap(src, D, D, D)), inc=s_wt, amt=16)
    wt_qkv = p.now(s_wt)

    def ln_stats(src, wv):
        E("vector", lambda v: v.tensor_reduce(stats[:, 0:1], src[:], AX.X, ALU.add),
          waits=wv, inc=s_v, drain=True)
        E("vector", lambda v: v.tensor_tensor(scratch[:], src[:], src[:], ALU.mult),
          inc=s_v, drain=True)
        E("vector", lambda v: v.tensor_reduce(stats[:, 2:3], scratch[:], AX.X, ALU.add),
          inc=s_v, drain=True)
        E("vector", lambda v: v.tensor_scalar(stats[:, 1:2], stats[:, 0:1],
                                              LNI, 0.0, ALU.mult, ALU.add),
          inc=s_v, drain=True)
        E("vector", lambda v: v.tensor_scalar(stats[:, 3:4], stats[:, 2:3],
                                              LNI, 0.0, ALU.mult, ALU.add),
          inc=s_v, drain=True)
        E("vector", lambda v: v.tensor_tensor(stats[:, 4:5], stats[:, 1:2],
                                              stats[:, 1:2], ALU.mult), inc=s_v, drain=True)
        E("vector", lambda v: v.tensor_tensor(stats[:, 5:6], stats[:, 3:4],
                                              stats[:, 4:5], ALU.subtract),
          inc=s_v, drain=True)
        vc = p.now(s_v)
        E("scalar", lambda s: s.activation(stats[:, 6:7], stats[:, 5:6], AF.Ln,
                                           bias=epsc[:, 0:1]),
          waits=[(s_v, vc), (s_g, NGS)], inc=s_sc, drain=True)
        E("scalar", lambda s: s.activation(stats[:, 7:8], stats[:, 6:7],
                                           AF.Exp, scale=-0.5), inc=s_sc, drain=True)
        return p.now(s_sc)

    rel_rbuf = [0, 0]
    rel_ptr = 0
    for i in range(NT):
        b = i % 2
        E("sync", lambda y, i=i, b=b: y.dma_start(
            out=rbuf[b][:], in_=xin.ap()[i * 128:(i + 1) * 128, :]),
          waits=[(s_v, rel_rbuf[b])], inc=s_ld, amt=16)
        ldc = p.now(s_ld)
        sc = ln_stats(rbuf[b], [(s_ld, ldc)])
        E("vector", lambda v, b=b: v.tensor_scalar(
            hbf[:], rbuf[b][:], stats[:, 1:2], stats[:, 7:8], ALU.subtract, ALU.mult),
          waits=[(s_sc, sc)], inc=s_v, drain=True)
        rel_rbuf[b] = p.now(s_v)
        v_hbf = p.now(s_v)
        for c in range(DC):
            E("tensor", lambda t, c=c: t.transpose(
                ptr[:, c * 128:(c + 1) * 128], hbf[:, c * 128:(c + 1) * 128], identb[:]),
              waits=[(s_v, v_hbf), (s_wt, 16), (s_v, rel_ptr)],
              inc=s_pe if c == DC - 1 else None)
        pe_tr = p.now(s_pe)
        for c in range(DC):
            E("vector", lambda v, c=c, i=i: v.tensor_copy(
                xlnT[:, c * T + i * 128: c * T + (i + 1) * 128],
                ptr[:, c * 128:(c + 1) * 128]),
              waits=[(s_pe, pe_tr)], inc=s_v, drain=(c == DC - 1))
        rel_ptr = p.now(s_v)
    v_xlnT = p.now(s_v)
    if stop_after == 'a1':
        E("gpsimd", lambda g: g.dma_start(out=out.ap()[0:128, :], in_=rbuf[0][:]),
          waits=[(s_v, p.now(s_v))], inc=s_st, amt=16)
        fin0 = p.now(s_st)
        E("gpsimd", lambda g: g.nop(), waits=[(s_st, fin0)])
        return p.finalize()

    rel_stage = [0, 0]
    rel_pS_v = [0, 0]
    rel_pS_sc = [0, 0]
    for (wsb, bcol, scl, scr) in ((wq_sb, bqc, SCALING, qt_scr),
                                  (wk_sb, bkc, 1.0, kt_scr)):
        for hp in range(8):
            for tc in range(8):
                bs = tc % 2
                for cc in range(DC):
                    E("tensor", lambda t, wsb=wsb, hp=hp, tc=tc, cc=cc, bs=bs: t.matmul(
                        pS[bs][:], wsb[:, cc * D + hp * 128: cc * D + hp * 128 + 128],
                        xlnT[:, cc * T + tc * 512: cc * T + tc * 512 + 512],
                        start=(cc == 0), stop=(cc == DC - 1)),
                      waits=[(s_wt, wt_qkv), (s_v, v_xlnT), (s_v, rel_pS_v[bs])],
                      inc=s_pe if cc == DC - 1 else None)
                pe_q = p.now(s_pe)
                E("vector", lambda v, hp=hp, bcol=bcol, scl=scl, bs=bs: v.tensor_scalar(
                    stage[bs][:], pS[bs][:], bcol[:, hp:hp + 1], scl, ALU.add, ALU.mult),
                  waits=[(s_pe, pe_q), (s_ld, ld0), (s_st, rel_stage[bs])],
                  inc=s_v, drain=True)
                vst = p.now(s_v)
                E("gpsimd", lambda g, scr=scr, hp=hp, tc=tc, bs=bs: g.dma_start(
                    out=scr.ap()[hp * 128:(hp + 1) * 128, tc * 512:(tc + 1) * 512],
                    in_=stage[bs][:]),
                  waits=[(s_v, vst)], inc=s_st, amt=16)
                rel_stage[bs] = p.now(s_st)
                rel_pS_v[bs] = p.now(s_v)
    for ti in range(NT):
        for co in range(2):
            bs = co % 2
            for cc in range(DC):
                E("tensor", lambda t, ti=ti, co=co, cc=cc, bs=bs: t.matmul(
                    pS[bs][:], xlnT[:, cc * T + ti * 128: cc * T + ti * 128 + 128],
                    wv_sb[:, cc * D + co * 512: cc * D + co * 512 + 512],
                    start=(cc == 0), stop=(cc == DC - 1)),
                  waits=[(s_v, rel_pS_v[bs])], inc=s_pe if cc == DC - 1 else None)
            pe_v = p.now(s_pe)
            E("vector", lambda v, co=co, bs=bs: v.scalar_tensor_tensor(
                stage[bs][:], pS[bs][:], 1.0, bvb[:, co * 512:(co + 1) * 512],
                ALU.mult, ALU.add),
              waits=[(s_pe, pe_v), (s_st, rel_stage[bs])], inc=s_v, drain=True)
            vst = p.now(s_v)
            rel_pS_v[bs] = vst
            E("gpsimd", lambda g, ti=ti, co=co, bs=bs: g.dma_start(
                out=v_scr.ap()[ti * 128:(ti + 1) * 128, co * 512:(co + 1) * 512],
                in_=stage[bs][:]),
              waits=[(s_v, vst)], inc=s_st, amt=16)
            rel_stage[bs] = p.now(s_st)
    rel_pG_v = p.now(s_v)
    for tc in range(8):
        for (wab, bsr, dst) in ((wa_sb, bsrAb, sga), (wb_sb, bsrBb, sgb)):
            for cc in range(DC):
                E("tensor", lambda t, wab=wab, tc=tc, cc=cc: t.matmul(
                    pG[:], wab[:, cc * 16:(cc + 1) * 16],
                    xlnT[:, cc * T + tc * 512: cc * T + tc * 512 + 512],
                    start=(cc == 0), stop=False),
                  waits=[(s_wt, wt_wab), (s_v, v_xlnT), (s_v, rel_pG_v)],
                  inc=None)
            E("tensor", lambda t, bsr=bsr: t.matmul(
                pG[:], bsr[:], onesrb[:], start=False, stop=True),
              waits=[(s_g, 2)], inc=s_pe)
            pe_g = p.now(s_pe)
            E("scalar", lambda s, dst=dst: s.activation(dst[:], pG[:], AF.Exp,
                                                        scale=-1.0),
              waits=[(s_pe, pe_g)], inc=s_sc, drain=True)
            scg = p.now(s_sc)
            E("vector", lambda v, dst=dst: v.tensor_scalar(dst[:], dst[:], 1.0, 0.0,
                                                           ALU.add, ALU.add),
              waits=[(s_sc, scg)], inc=s_v, drain=True)
            rel_pG_v = p.now(s_v)
            E("vector", lambda v, dst=dst: v.reciprocal(dst[:], dst[:]),
              inc=s_v, drain=True)
        E("vector", lambda v: v.tensor_scalar(gt16[:], sgb[:], gc16[:, 0:1], 1.0,
                                              ALU.mult, ALU.subtract),
          inc=s_v, drain=True)
        E("vector", lambda v: v.tensor_tensor(gt16[:], sga[:], gt16[:], ALU.mult),
          inc=s_v, drain=True)
        E("vector", lambda v: v.tensor_scalar(gt16[:], gt16[:], 2.0, 0.0,
                                              ALU.add, ALU.add), inc=s_v, drain=True)
        vg = p.now(s_v)
        E("gpsimd", lambda g, tc=tc: g.dma_start(
            out=gate_scr.ap()[:, tc * 512:(tc + 1) * 512], in_=gt16[:]),
          waits=[(s_v, vg)], inc=s_st, amt=16)
    a2_st = p.now(s_st)
    a2_pe = p.now(s_pe)
    if stop_after == 'a2':
        E("gpsimd", lambda g: g.dma_start(out=out.ap()[0:128, :], in_=rbuf[0][:]),
          waits=[(s_st, a2_st)], inc=s_st, amt=16)
        fin0 = p.now(s_st)
        E("gpsimd", lambda g: g.nop(), waits=[(s_st, fin0)])
        return p.finalize()

    band_deltas = set(range(-254, 639))
    rel_pbc_v = p.now(s_v)
    rel_rows = 0
    rel_flip = 0
    rel_unit_pe = 0
    rel_P = [0, 0, 0]
    rel_Bt = 0
    rel_Bg = 0
    rel_pO_v = 0
    rel_at = 0
    rel_kv_pe = 0
    for h in range(16):
        E("sync", lambda y, h=h: y.dma_start(
            out=kth[:], in_=kt_scr.ap()[h * 64:(h + 1) * 64, :]),
          waits=[(s_st, a2_st), (s_pe, rel_kv_pe)], inc=s_ld, amt=16)
        vsrc = v_scr.ap()
        vsrc2 = vsrc.copy()
        while len(vsrc2.ap) > 1:
            vsrc2.ap.pop()
        vsrc2.ap[0] = [D, 128]
        vsrc2.ap.append([128 * D, 32])
        vsrc2.ap.append([1, 64])
        vsrc2.offset = h * 64
        vdst = vh.ap().copy() if callable(getattr(vh, "ap", None)) else vh[:].copy()
        pstride = int(vdst.ap[0][0])
        while len(vdst.ap) > 1:
            vdst.ap.pop()
        vdst.ap[0] = [pstride, 128]
        vdst.ap.append([65, 32])
        vdst.ap.append([1, 64])
        E("sync", lambda y, vdst=vdst, vsrc2=vsrc2: y.dma_start(out=vdst, in_=vsrc2),
          waits=[(s_pe, rel_kv_pe)], inc=s_ld, amt=16)
        E("sync", lambda y, h=h: y.dma_start(out=grow[:],
                                             in_=gate_scr.ap()[h:h + 1, :]),
          waits=[(s_v, rel_Bg)], inc=s_ld, amt=16)
        E("sync", lambda y, h=h: y.dma_start(out=crow[:], in_=csc_hbm.ap()[h:h + 1, :]),
          waits=[(s_st, 32)], inc=s_ld, amt=16)
        ld_h = p.now(s_ld)
        for qg in range(8):
            E("sync", lambda y, h=h, qg=qg: y.dma_start(
                out=qsl[:], in_=qt_scr.ap()[h * 64:(h + 1) * 64,
                                            qg * 512:(qg + 1) * 512]),
              waits=[(s_pe, rel_unit_pe)], inc=s_ld, amt=16)
            ld_q = p.now(s_ld)
            E("tensor", lambda t, qg=qg: t.matmul(
                pbc[:], onesf[:], grow[0:1, qg * 512:(qg + 1) * 512],
                start=True, stop=True),
              waits=[(s_ld, ld_h), (s_v, rel_Bg), (s_v, rel_pbc_v)], inc=s_pe)
            peb = p.now(s_pe)
            E("vector", lambda v: v.tensor_copy(gb_sb[:], pbc[:]),
              waits=[(s_pe, peb)], inc=s_v, drain=True)
            rel_Bg = p.now(s_v)
            E("vector", lambda v, qg=qg: v.tensor_scalar(
                rowF[:], grow[0:1, qg * 512:(qg + 1) * 512], crow[0:1, 1:2], 0.0,
                ALU.mult, ALU.add),
              waits=[(s_ld, ld_h), (s_pe, rel_rows)], inc=s_v, drain=True)
            E("vector", lambda v, qg=qg: v.tensor_scalar(
                denr[:], grow[0:1, qg * 512:(qg + 1) * 512], crow[0:1, 0:1], 0.0,
                ALU.mult, ALU.add), inc=s_v, drain=True)
            v_rows = p.now(s_v)
            for kt in range(32):
                delta = kt * 128 - qg * 512
                bs = kt % 2
                E("tensor", lambda t, kt=kt, bs=bs: t.matmul(
                    pS[bs][:], kth[:, kt * 128:(kt + 1) * 128], qsl[:],
                    start=True, stop=False),
                  waits=[(s_ld, ld_q), (s_sc, rel_pS_sc[bs]),
                         (s_v, rel_pS_v[bs])],
                  inc=None)
                if delta in band_deltas:
                    E("sync", lambda y, h=h, delta=delta: y.dma_start(
                        out=Bt[:], in_=toep_ap(tblr_hbm, h, delta)),
                      waits=[(s_st, 48), (s_v, rel_Bt)], inc=s_ld, amt=16)
                    ldb = p.now(s_ld)
                    E("vector", lambda v: v.tensor_tensor(Bg[:], Bt[:], gb_sb[:],
                                                          ALU.mult),
                      waits=[(s_ld, ldb), (s_pe, rel_flip)], inc=s_v, drain=True)
                    rel_Bt = p.now(s_v)
                    vbg = p.now(s_v)
                    E("tensor", lambda t, bs=bs: t.matmul(
                        pS[bs][:], flipb[:], Bg[:], start=False, stop=True),
                      waits=[(s_v, vbg), (s_wt, wt0)], inc=s_pe)
                    rel_flip = p.now(s_pe)
                else:
                    rw = rowF if delta <= -256 else denr
                    E("tensor", lambda t, bs=bs, rw=rw: t.matmul(
                        pS[bs][:], onesf[:], rw[:], start=False, stop=True),
                      waits=[(s_v, v_rows), (s_g, 1)], inc=s_pe)
                pe_s = p.now(s_pe)
                pb = kt % 3
                E("scalar", lambda s, bs=bs, pb=pb: s.activation(
                    Pb[pb][:], pS[bs][:], AF.Exp),
                  waits=[(s_pe, pe_s), (s_pe, rel_P[pb])], inc=s_sc, drain=True)
                sc_p = p.now(s_sc)
                rel_pS_sc[bs] = sc_p
                if dbg_p and h == 0 and ((qg == 0 and kt == 10) or (qg == 7 and kt == 10)):
                    dst = dbg_pfr if qg == 0 else dbg_pfl
                    E("gpsimd", lambda g, dst=dst, pb=pb: g.dma_start(
                        out=dst.ap(), in_=Pb[pb][:]),
                      waits=[(s_sc, sc_p)], inc=s_st, amt=16)
                    if qg == 0:
                        E("gpsimd", lambda g: g.dma_start(out=dbg_row.ap(), in_=denr[:]),
                          waits=[(s_v, v_rows)], inc=s_st, amt=16)
                E("tensor", lambda t, kt=kt, pb=pb: t.matmul(
                    pO[:], vh[:, kt * 65:(kt + 1) * 65], Pb[pb][:],
                    start=(kt == 0), stop=(kt == 31)),
                  waits=[(s_sc, sc_p), (s_ld, ld_h), (s_v, rel_pO_v), (s_g, NGS)],
                  inc=s_pe)
                rel_P[pb] = p.now(s_pe)
            pe_av = p.now(s_pe)
            E("vector", lambda v: v.tensor_copy(denr[:], pO[64:65, 0:512]),
              waits=[(s_pe, pe_av)], inc=s_v, drain=True)
            E("vector", lambda v: v.reciprocal(denr[:], denr[:]), inc=s_v, drain=True)
            vd = p.now(s_v)
            E("tensor", lambda t: t.matmul(pbc[0:64, :], onesf[0:1, 0:64], denr[:],
                                           start=True, stop=True),
              waits=[(s_v, vd), (s_v, rel_Bg)], inc=s_pe)
            ped = p.now(s_pe)
            rel_rows = ped
            rel_unit_pe = pe_av
            E("vector", lambda v: v.tensor_copy(dr_sb[:], pbc[0:64, :]),
              waits=[(s_pe, ped)], inc=s_v, drain=True)
            rel_pbc_v = p.now(s_v)
            E("vector", lambda v: v.tensor_tensor(atst[:], pO[0:64, 0:512], dr_sb[:],
                                                  ALU.mult),
              waits=[(s_st, rel_at)], inc=s_v, drain=True)
            rel_pO_v = p.now(s_v)
            vat = p.now(s_v)
            E("gpsimd", lambda g, h=h, qg=qg: g.dma_start(
                out=at_scr.ap()[h * 64:(h + 1) * 64, qg * 512:(qg + 1) * 512],
                in_=atst[:]),
              waits=[(s_v, vat)], inc=s_st, amt=16)
            rel_at = p.now(s_st)
        rel_kv_pe = p.now(s_pe)
    a4_st = p.now(s_st)
    if stop_after == 'a4':
        E("gpsimd", lambda g: g.dma_start(out=out.ap()[0:128, :], in_=rbuf[0][:]),
          waits=[(s_st, a4_st)], inc=s_st, amt=16)
        fin0 = p.now(s_st)
        E("gpsimd", lambda g: g.nop(), waits=[(s_st, fin0)])
        return p.finalize()

    E("gpsimd", lambda g: g.dma_start(
        out=wo_sb[:], in_=chunkmajor_ap(woD, D, D, D)),
      waits=[(s_pe, p.now(s_pe))], inc=s_wt, amt=16)
    wt_wo = p.now(s_wt)
    onesb1 = onesrb[0:1, 0:128]
    rel_xs2 = [0, 0]
    rel_ob = 0
    rel_pS_v2 = [p.now(s_v), p.now(s_v)]
    for ti in range(NT):
        b = ti % 2
        E("sync", lambda y, ti=ti, b=b: y.dma_start(
            out=xs2[b][:], in_=chunkmajor_ap(at_scr, D, T, 128, col0=ti * 128)),
          waits=[(s_st, a4_st), (s_pe, rel_xs2[b])], inc=s_ld, amt=16)
        E("sync", lambda y, ti=ti, b=b: y.dma_start(
            out=rbuf[b][:], in_=xin.ap()[ti * 128:(ti + 1) * 128, :]),
          waits=[(s_v, rel_rbuf[b])], inc=s_ld, amt=16)
        ld5 = p.now(s_ld)
        for co in range(2):
            bs = co
            for cc in range(DC):
                E("tensor", lambda t, co=co, cc=cc, b=b, bs=bs: t.matmul(
                    pS[bs][:], xs2[b][:, cc * 128:(cc + 1) * 128],
                    wo_sb[:, cc * D + co * 512: cc * D + co * 512 + 512],
                    start=(cc == 0), stop=False),
                  waits=[(s_wt, wt_wo), (s_ld, ld5), (s_v, rel_pS_v2[bs])],
                  inc=None)
            E("tensor", lambda t, co=co, bs=bs: t.matmul(
                pS[bs][:], onesb1, bob[0:1, co * 512:(co + 1) * 512],
                start=False, stop=True),
              waits=[(s_g, 2), (s_wt, 16 * 8)], inc=s_pe)
            pe_o = p.now(s_pe)
            E("vector", lambda v, co=co, b=b, bs=bs: v.tensor_tensor(
                obuf[:, co * 512:(co + 1) * 512], pS[bs][:],
                rbuf[b][:, co * 512:(co + 1) * 512], ALU.add),
              waits=[(s_pe, pe_o), (s_st, rel_ob)], inc=s_v, drain=True)
            rel_pS_v2[bs] = p.now(s_v)
        rel_rbuf[b] = p.now(s_v)
        rel_xs2[b] = p.now(s_pe)
        vo = p.now(s_v)
        E("gpsimd", lambda g, ti=ti: g.dma_start(
            out=out.ap()[ti * 128:(ti + 1) * 128, :], in_=obuf[:]),
          waits=[(s_v, vo)], inc=s_st, amt=16)
        rel_ob = p.now(s_st)

    fin = p.now(s_st)
    E("gpsimd", lambda g: g.nop(), waits=[(s_st, fin)])
    return p.finalize() if fin_own else None


def toep_ap(tbl, h, delta):
    ap = tbl.ap().copy()
    while len(ap.ap) > 1:
        ap.ap.pop()
    ap.ap[0] = [1, 128]
    ap.ap.append([1, 512])
    ap.offset = h * 1280 + (512 - delta)
    return ap


def _rel_buckets_delta(d):
    NUM_BUCKETS, MAX_DIST = 32, 128
    nb = NUM_BUCKETS // 2
    bucket = (1 if d > 0 else 0) * nb
    rp = abs(d)
    max_exact = nb // 2
    if rp < max_exact:
        bucket += rp
    else:
        import math
        large = max_exact + int(math.log(max(rp, 1) / max_exact)
                                / math.log(MAX_DIST / max_exact) * (nb - max_exact))
        bucket += min(large, nb - 1)
    return bucket


def prep_launch_a(inputs):
    oh = np.zeros((32, 1280), np.float32)
    for j in range(1280):
        oh[_rel_buckets_delta(639 - j), j] = 1.0
    flip = np.zeros((128, 128), np.float32)
    for i in range(128):
        flip[i, 127 - i] = 1.0
    def colsfmt(vec):
        return np.ascontiguousarray(np.asarray(vec, np.float32).reshape(8, 128).T)
    d = dict(
        wq=np.ascontiguousarray(np.asarray(inputs["L0_attn_Wq"], np.float32)),
        wk=np.ascontiguousarray(np.asarray(inputs["L0_attn_Wk"], np.float32)),
        wv=np.ascontiguousarray(np.asarray(inputs["L0_attn_Wv"], np.float32)),
        wo=np.ascontiguousarray(np.asarray(inputs["L0_attn_Wo"], np.float32)),
        bq=colsfmt(inputs["L0_attn_bq"]),
        bk=colsfmt(inputs["L0_attn_bk"]),
        bv=np.asarray(inputs["L0_attn_bv"], np.float32).reshape(1, D),
        bo=np.asarray(inputs["L0_attn_bo"], np.float32).reshape(1, D),
        oh=oh,
        rel=np.ascontiguousarray(np.asarray(inputs["L0_attn_rel_embed"], np.float32)),
        gwt=np.ascontiguousarray(np.asarray(inputs["L0_attn_gru_w"], np.float32).T),
        gbt=np.asarray(inputs["L0_attn_gru_b"], np.float32).reshape(8, 1),
        ob2=np.concatenate([(np.arange(8) < 4).astype(np.float32)[:, None],
                            (np.arange(8) >= 4).astype(np.float32)[:, None]], 1),
        gc16=np.asarray(inputs["L0_attn_gru_const"], np.float32).reshape(16, 1),
        flip=flip,
        ident=np.eye(128, dtype=np.float32),
    )
    return d


def run_launch_a(x_full, inputs, stop_after=None, dbg_p=False):
    nc = build_launch_a(stop_after, dbg_p)
    com = prep_launch_a(inputs)
    in_maps = []
    for b in range(8):
        m = dict(com)
        m["xin"] = np.ascontiguousarray(x_full[b])
        in_maps.append(m)
    res = run_bass_kernel_spmd(nc, in_maps, core_ids=list(range(8)))
    if dbg_p:
        return res.results
    return np.stack([r["out"] for r in res.results])



def build_launch_b1(ctx=None):
    nc, p, DT, sb, psum, fin_own = _ctx_or_new(ctx, "b1")

    res1 = DT("res1", [T, D], F32, "ExternalInput")
    wrtD = DT("wrt", [D, 8], F32, "ExternalInput")
    ident = DT("ident", [128, 128], F32, "ExternalInput")
    lg = DT("lg", [T, 8], F32, "ExternalOutput")

    rbuf = [sb(f"rbuf{i}", [128, D], F32) for i in range(2)]
    scratch = sb("scratch", [128, D], F32)
    h32 = sb("h32", [128, D], F32)
    hbf = sb("hbf", [128, D], BF16)
    hlo = sb("hlo", [128, D], BF16)
    hts = sb("hts", [128, D], BF16)
    htl = sb("htl", [128, D], BF16)
    stats = sb("stats", [128, 16], F32)
    epsc = sb("epsc", [128, 1], F32)
    wrf = sb("wrf", [128, 64], F32)
    wrh = sb("wrh", [128, 64], BF16)
    wrl = sb("wrl", [128, 64], BF16)
    identb = sb("identb", [128, 128], BF16)
    lgst = [sb(f"lgst{i}", [128, 8], F32) for i in range(2)]

    ptr = psum("ptr", [128, 1024], BF16)
    ptr2 = psum("ptr2", [128, 1024], BF16)
    pbc = psum("pbc", [128, 512], F32)

    s_ld = p.sem("s_ld")
    s_wt = p.sem("s_wt")
    s_pe = p.sem("s_pe", owner="tensor")
    s_v = p.sem("s_v", owner="vector")
    s_sc = p.sem("s_sc", owner="scalar")
    s_st = p.sem("s_st")
    s_g = p.sem("s_g", owner="gpsimd")
    E = p.emit
    LNI = 1.0 / D

    E("gpsimd", lambda g: g.memset(epsc[:], 1e-5), inc=s_g)
    E("gpsimd", lambda g: g.dma_start(out=identb[:], in_=ident.ap()), inc=s_wt, amt=16)
    E("sync", lambda y: y.dma_start(out=wrf[:], in_=chunkmajor_ap(wrtD, D, 8, 8)),
      inc=s_ld, amt=16)
    ld0 = p.now(s_ld)
    E("vector", lambda v: v.tensor_copy(wrh[:], wrf[:]), waits=[(s_ld, ld0)],
      inc=s_v, drain=True)
    E("vector", lambda v: v.tensor_tensor(wrl[:], wrf[:], wrh[:], ALU.subtract),
      inc=s_v, drain=True)
    v_wr = p.now(s_v)

    rel_rbuf = [0, 0]
    rel_ptr = 0
    rel_ptr2 = 0
    rel_pbc = 0
    hts_mm = 0
    rel_lgst = [0, 0]
    for i in range(NT):
        b = i % 2
        E("sync", lambda y, i=i, b=b: y.dma_start(
            out=rbuf[b][:], in_=res1.ap()[i * 128:(i + 1) * 128, :]),
          waits=[(s_v, rel_rbuf[b])], inc=s_ld, amt=16)
        ldc = p.now(s_ld)
        E("vector", lambda v, b=b: v.tensor_reduce(stats[:, 0:1], rbuf[b][:], AX.X,
                                                   ALU.add),
          waits=[(s_ld, ldc)], inc=s_v, drain=True)
        E("vector", lambda v, b=b: v.tensor_tensor(scratch[:], rbuf[b][:], rbuf[b][:],
                                                   ALU.mult), inc=s_v, drain=True)
        E("vector", lambda v: v.tensor_reduce(stats[:, 2:3], scratch[:], AX.X, ALU.add),
          inc=s_v, drain=True)
        E("vector", lambda v: v.tensor_scalar(stats[:, 1:2], stats[:, 0:1], LNI, 0.0,
                                              ALU.mult, ALU.add), inc=s_v, drain=True)
        E("vector", lambda v: v.tensor_scalar(stats[:, 3:4], stats[:, 2:3], LNI, 0.0,
                                              ALU.mult, ALU.add), inc=s_v, drain=True)
        E("vector", lambda v: v.tensor_tensor(stats[:, 4:5], stats[:, 1:2],
                                              stats[:, 1:2], ALU.mult), inc=s_v, drain=True)
        E("vector", lambda v: v.tensor_tensor(stats[:, 5:6], stats[:, 3:4],
                                              stats[:, 4:5], ALU.subtract),
          inc=s_v, drain=True)
        vc = p.now(s_v)
        E("scalar", lambda s: s.activation(stats[:, 6:7], stats[:, 5:6], AF.Ln,
                                           bias=epsc[:, 0:1]),
          waits=[(s_v, vc), (s_g, 1)], inc=s_sc, drain=True)
        E("scalar", lambda s: s.activation(stats[:, 7:8], stats[:, 6:7], AF.Exp,
                                           scale=-0.5), inc=s_sc, drain=True)
        sc = p.now(s_sc)
        E("vector", lambda v, b=b: v.tensor_scalar(
            h32[:], rbuf[b][:], stats[:, 1:2], stats[:, 7:8], ALU.subtract, ALU.mult),
          waits=[(s_sc, sc)], inc=s_v, drain=True)
        rel_rbuf[b] = p.now(s_v)
        E("vector", lambda v: v.tensor_copy(hbf[:], h32[:]), inc=s_v, drain=True)
        E("vector", lambda v: v.tensor_tensor(hlo[:], h32[:], hbf[:], ALU.subtract),
          inc=s_v, drain=True)
        v_hbf = p.now(s_v)
        for c in range(DC):
            E("tensor", lambda t, c=c: t.transpose(
                ptr[:, c * 128:(c + 1) * 128], hbf[:, c * 128:(c + 1) * 128], identb[:]),
              waits=[(s_v, v_hbf), (s_wt, 16), (s_v, rel_ptr)],
              inc=s_pe if c == DC - 1 else None)
        pe1 = p.now(s_pe)
        E("vector", lambda v: v.tensor_copy(hts[:], ptr[:]),
          waits=[(s_pe, pe1), (s_pe, hts_mm)], inc=s_v, drain=True)
        rel_ptr = p.now(s_v)
        for c in range(DC):
            E("tensor", lambda t, c=c: t.transpose(
                ptr2[:, c * 128:(c + 1) * 128], hlo[:, c * 128:(c + 1) * 128], identb[:]),
              waits=[(s_v, p.now(s_v)), (s_v, rel_ptr2)],
              inc=s_pe if c == DC - 1 else None)
        pe2 = p.now(s_pe)
        E("vector", lambda v: v.tensor_copy(htl[:], ptr2[:]),
          waits=[(s_pe, pe2)], inc=s_v, drain=True)
        rel_ptr2 = p.now(s_v)
        v_ht = p.now(s_v)
        nmm = 0
        for c in range(DC):
            for (lt, rt) in ((hts, wrh), (hts, wrl), (htl, wrh)):
                nmm += 1
                E("tensor", lambda t, c=c, lt=lt, rt=rt, first=(nmm == 1),
                  last=(nmm == 24): t.matmul(
                      pbc[:, 0:8], lt[:, c * 128:(c + 1) * 128],
                      rt[:, c * 8:(c + 1) * 8], start=first, stop=last),
                  waits=[(s_v, v_ht), (s_v, v_wr), (s_v, rel_pbc)],
                  inc=s_pe if nmm == 24 else None)
        hts_mm = p.now(s_pe)
        E("vector", lambda v, b=b: v.tensor_copy(lgst[b][:], pbc[:, 0:8]),
          waits=[(s_pe, hts_mm), (s_st, rel_lgst[b])], inc=s_v, drain=True)
        rel_pbc = p.now(s_v)
        vl = p.now(s_v)
        E("gpsimd", lambda g, i=i, b=b: g.dma_start(
            out=lg.ap()[i * 128:(i + 1) * 128, :], in_=lgst[b][:]),
          waits=[(s_v, vl)], inc=s_st, amt=16)
        rel_lgst[b] = p.now(s_st)
    fin = p.now(s_st)
    E("gpsimd", lambda g: g.nop(), waits=[(s_st, fin)])
    return p.finalize() if fin_own else None


def run_launch_b1(res1_full, inputs):
    nc = build_launch_b1()
    wrt = np.ascontiguousarray(
        np.asarray(inputs["L0_ffn_router_weight"], np.float32).T)
    ident = np.eye(128, dtype=np.float32)
    in_maps = [dict(res1=np.ascontiguousarray(res1_full[b]), wrt=wrt, ident=ident)
               for b in range(8)]
    res = run_bass_kernel_spmd(nc, in_maps, core_ids=list(range(8)))
    return np.stack([r["lg"] for r in res.results])



GT = 8 * T
GNT = GT // 128
MOE_EPS = 1.1920929e-07
CAPACITY = 10240.0

def build_launch_b2(debug=False, ctx=None):
    nc, p, DT, sb, psum, fin_own = _ctx_or_new(ctx, "b2")

    res1 = DT("res1", [T, D], F32, "ExternalInput")
    lgall = DT("lgall", [GT, 8], F32, "ExternalInput")
    preTD = DT("preT", [8, D, 4096], F32, "ExternalInput")
    gateTD = DT("gateT", [8, D, 4096], F32, "ExternalInput")
    postTD = DT("postT", [8, 4096, D], F32, "ExternalInput")
    idx8 = DT("idx8", [128, 8], F32, "ExternalInput")
    idx8m8 = DT("idx8m8", [128, 8], F32, "ExternalInput")
    ident = DT("ident", [128, 128], F32, "ExternalInput")
    out = DT("out", [T, D], F32, "ExternalOutput")
    ht_scr = DT("ht_scr", [D, T], BF16, "Internal")
    acc_scr = DT("acc_scr", [T, D], F32, "Internal")
    if debug:
        dbg_wf = DT("dbg_wf", [128, 512], F32, "ExternalOutput")
        dbg_ef = DT("dbg_ef", [128, 512], F32, "ExternalOutput")
        dbg_kf = DT("dbg_kf", [128, 512], F32, "ExternalOutput")
        dbg_cw = DT("dbg_cw", [128, NT * 8], F32, "ExternalOutput")
        dbg_th = DT("dbg_th", [128, 8], F32, "ExternalOutput")
        dbg_acc = DT("dbg_acc", [T, D], F32, "ExternalOutput")
        dbg_g0 = DT("dbg_g0", [128, 8 * 256], BF16, "ExternalOutput")

    rbuf = [sb(f"rbuf{i}", [128, D], F32) for i in range(2)]
    scratch = sb("scratch", [128, D], F32)
    h32 = sb("h32", [128, D], F32)
    hbf = sb("hbf", [128, D], BF16)
    hts = sb("hts", [128, D], BF16)
    stats = sb("stats", [128, 24], F32)
    epsc = sb("epsc", [128, 1], F32)
    onesf = sb("onesf", [1, 128], F32)
    onescol = sb("onescol", [128, 1], F32)
    identb = sb("identb", [128, 128], BF16)
    idx8s = sb("idx8s", [128, 8], F32)
    idx8m8s = sb("idx8m8s", [128, 8], F32)
    lg8 = [sb(f"lg8_{i}", [128, 8], F32) for i in range(2)]
    e8 = sb("e8", [128, 8], F32)
    pr = sb("pr", [128, 8], F32)
    pr2 = sb("pr2", [128, 8], F32)
    maskA = sb("maskA", [128, 8], F32)
    maskB = sb("maskB", [128, 8], F32)
    tmp8 = sb("tmp8", [128, 8], F32)
    wflat = sb("wflat", [128, 512], F32)
    eflat = sb("eflat", [128, 512], F32)
    kflat = sb("kflat", [128, 512], F32)
    emask = [sb(f"emask{e}", [128, 512], F32) for e in range(8)]
    accb = sb("accb", [128, 512], F32)
    cmpf = sb("cmpf", [128, 512], F32)
    cnt8 = sb("cnt8", [128, 8], F32)
    rowst = sb("rowst", [1, 32], F32)
    midc = sb("midc", [128, 8], F32)
    kw1 = sb("kw1", [128, 256], F32)
    kw2 = sb("kw2", [128, 256], F32)
    cw = sb("cw", [128, NT * 8], F32)
    wpre = sb("wpre", [128, 8 * 1024], BF16)
    wgate = sb("wgate", [128, 8 * 1024], BF16)
    wpost = sb("wpost", [128, 8 * 1024], BF16)
    xsb = [sb(f"xsb{i}", [128, 8 * 256], BF16) for i in range(2)]
    gtb = sb("gtb", [128, 8 * 256], BF16)
    silb = sb("silb", [128, 256], BF16)
    stage = [sb(f"stage{i}", [128, D], F32) for i in range(2)]
    atile = sb("atile", [128, D], F32)
    obuf = sb("obuf", [128, D], F32)

    pm1 = [psum(f"pm1_{i}", [128, 256], F32) for i in range(2)]
    pmg = psum("pmg", [128, 256], F32)
    pm2 = [psum(f"pm2_{i}", [128, 512], F32) for i in range(2)]
    ptr = psum("ptr", [128, 1024], BF16)
    pbc = psum("pbc", [128, 512], F32)

    s_ld = p.sem("s_ld")
    s_wt = p.sem("s_wt")
    s_pe = p.sem("s_pe", owner="tensor")
    s_v = p.sem("s_v", owner="vector")
    s_sc = p.sem("s_sc", owner="scalar")
    s_st = p.sem("s_st")
    s_g = p.sem("s_g", owner="gpsimd")
    E = p.emit
    LNI = 1.0 / D

    E("gpsimd", lambda g: g.memset(epsc[:], 1e-5), inc=s_g)
    E("gpsimd", lambda g: g.memset(onesf[:], 1.0), inc=s_g)
    E("gpsimd", lambda g: g.memset(onescol[:], 1.0), inc=s_g)
    E("gpsimd", lambda g: g.memset(rowst[0:1, 0:8], -1.0), inc=s_g)
    E("gpsimd", lambda g: g.memset(rowst[0:1, 8:16], 1.0), inc=s_g)
    NGS = 5
    E("gpsimd", lambda g: g.dma_start(out=identb[:], in_=ident.ap()), inc=s_wt, amt=16)
    E("sync", lambda y: y.dma_start(out=idx8s[:], in_=idx8.ap()), inc=s_ld, amt=16)
    E("sync", lambda y: y.dma_start(out=idx8m8s[:], in_=idx8m8.ap()), inc=s_ld, amt=16)
    ld0 = p.now(s_ld)

    rel_rbuf = [0, 0]
    rel_ptr = 0
    need_hts = 0
    for i in range(NT):
        b = i % 2
        E("sync", lambda y, i=i, b=b: y.dma_start(
            out=rbuf[b][:], in_=res1.ap()[i * 128:(i + 1) * 128, :]),
          waits=[(s_v, rel_rbuf[b])], inc=s_ld, amt=16)
        ldc = p.now(s_ld)
        E("vector", lambda v, b=b: v.tensor_reduce(stats[:, 0:1], rbuf[b][:], AX.X,
                                                   ALU.add),
          waits=[(s_ld, ldc)], inc=s_v, drain=True)
        E("vector", lambda v, b=b: v.tensor_tensor(scratch[:], rbuf[b][:], rbuf[b][:],
                                                   ALU.mult), inc=s_v, drain=True)
        E("vector", lambda v: v.tensor_reduce(stats[:, 2:3], scratch[:], AX.X, ALU.add),
          inc=s_v, drain=True)
        E("vector", lambda v: v.tensor_scalar(stats[:, 1:2], stats[:, 0:1], LNI, 0.0,
                                              ALU.mult, ALU.add), inc=s_v, drain=True)
        E("vector", lambda v: v.tensor_scalar(stats[:, 3:4], stats[:, 2:3], LNI, 0.0,
                                              ALU.mult, ALU.add), inc=s_v, drain=True)
        E("vector", lambda v: v.tensor_tensor(stats[:, 4:5], stats[:, 1:2],
                                              stats[:, 1:2], ALU.mult), inc=s_v, drain=True)
        E("vector", lambda v: v.tensor_tensor(stats[:, 5:6], stats[:, 3:4],
                                              stats[:, 4:5], ALU.subtract),
          inc=s_v, drain=True)
        vc = p.now(s_v)
        E("scalar", lambda s: s.activation(stats[:, 6:7], stats[:, 5:6], AF.Ln,
                                           bias=epsc[:, 0:1]),
          waits=[(s_v, vc), (s_g, 1)], inc=s_sc, drain=True)
        E("scalar", lambda s: s.activation(stats[:, 7:8], stats[:, 6:7], AF.Exp,
                                           scale=-0.5), inc=s_sc, drain=True)
        sc = p.now(s_sc)
        E("vector", lambda v, b=b: v.tensor_scalar(
            hbf[:], rbuf[b][:], stats[:, 1:2], stats[:, 7:8], ALU.subtract, ALU.mult),
          waits=[(s_sc, sc)], inc=s_v, drain=True)
        rel_rbuf[b] = p.now(s_v)
        v_hbf = p.now(s_v)
        for c in range(DC):
            E("tensor", lambda t, c=c: t.transpose(
                ptr[:, c * 128:(c + 1) * 128], hbf[:, c * 128:(c + 1) * 128], identb[:]),
              waits=[(s_v, v_hbf), (s_wt, 16), (s_v, rel_ptr)],
              inc=s_pe if c == DC - 1 else None)
        pe1 = p.now(s_pe)
        E("vector", lambda v: v.tensor_copy(hts[:], ptr[:]),
          waits=[(s_pe, pe1), (s_st, need_hts)], inc=s_v, drain=True)
        rel_ptr = p.now(s_v)
        v_ht = p.now(s_v)
        E("gpsimd", lambda g, i=i: g.dma_start(
            out=chunkmajor_ap(ht_scr, D, T, 128, col0=i * 128), in_=hts[:]),
          waits=[(s_v, v_ht)], inc=s_st, amt=16)
        need_hts = p.now(s_st)
    p1_st = p.now(s_st)

    rel_lg = [0, 0]
    for i in range(GNT):
        b = i % 2
        E("sync", lambda y, i=i, b=b: y.dma_start(
            out=lg8[b][:], in_=lgall.ap()[i * 128:(i + 1) * 128, :]),
          waits=[(s_v, rel_lg[b])], inc=s_ld, amt=16)
        ldc = p.now(s_ld)
        E("vector", lambda v, b=b: v.tensor_reduce(stats[:, 8:9], lg8[b][:], AX.X,
                                                   ALU.max),
          waits=[(s_ld, ldc), (s_ld, ld0)], inc=s_v, drain=True)
        E("vector", lambda v: v.tensor_scalar(stats[:, 9:10], stats[:, 8:9], -1.0, 0.0,
                                              ALU.mult, ALU.add), inc=s_v, drain=True)
        vb = p.now(s_v)
        E("scalar", lambda s, b=b: s.activation(e8[:], lg8[b][:], AF.Exp,
                                                bias=stats[:, 9:10],
                                                accum_out=stats[:, 10:11]),
          waits=[(s_v, vb)], inc=s_sc, drain=True)
        scb = p.now(s_sc)
        E("vector", lambda v: v.reciprocal(stats[:, 11:12], stats[:, 10:11]),
          waits=[(s_sc, scb)], inc=s_v, drain=True)
        rel_lg[b] = p.now(s_v)
        E("vector", lambda v: v.tensor_scalar(pr[:], e8[:], stats[:, 11:12], 0.0,
                                              ALU.mult, ALU.add), inc=s_v, drain=True)
        E("vector", lambda v: v.tensor_reduce(stats[:, 12:13], pr[:], AX.X, ALU.max),
          inc=s_v, drain=True)
        E("vector", lambda v: v.tensor_scalar(maskA[:], pr[:], stats[:, 12:13], 0.0,
                                              ALU.is_equal, ALU.add), inc=s_v, drain=True)
        E("vector", lambda v: v.tensor_tensor(tmp8[:], maskA[:], idx8m8s[:], ALU.mult),
          inc=s_v, drain=True)
        E("vector", lambda v: v.tensor_scalar(tmp8[:], tmp8[:], 8.0, 0.0, ALU.add,
                                              ALU.add), inc=s_v, drain=True)
        E("vector", lambda v, i=i: v.tensor_reduce(eflat[:, i:i + 1], tmp8[:], AX.X,
                                                   ALU.min), inc=s_v, drain=True)
        E("vector", lambda v, i=i: v.tensor_scalar(maskA[:], idx8s[:],
                                                   eflat[:, i:i + 1], 0.0,
                                                   ALU.is_equal, ALU.add),
          inc=s_v, drain=True)
        E("vector", lambda v: v.scalar_tensor_tensor(pr2[:], maskA[:], -1e30, pr[:],
                                                     ALU.mult, ALU.add),
          inc=s_v, drain=True)
        E("vector", lambda v: v.tensor_reduce(stats[:, 13:14], pr2[:], AX.X, ALU.max),
          inc=s_v, drain=True)
        E("vector", lambda v: v.tensor_scalar(maskB[:], pr2[:], stats[:, 13:14], 0.0,
                                              ALU.is_equal, ALU.add), inc=s_v, drain=True)
        E("vector", lambda v: v.tensor_tensor(tmp8[:], maskB[:], idx8m8s[:], ALU.mult),
          inc=s_v, drain=True)
        E("vector", lambda v: v.tensor_scalar(tmp8[:], tmp8[:], 8.0, 0.0, ALU.add,
                                              ALU.add), inc=s_v, drain=True)
        E("vector", lambda v, i=i: v.tensor_reduce(eflat[:, 256 + i:257 + i], tmp8[:],
                                                   AX.X, ALU.min), inc=s_v, drain=True)
        E("vector", lambda v: v.tensor_tensor(stats[:, 14:15], stats[:, 12:13],
                                              stats[:, 13:14], ALU.add),
          inc=s_v, drain=True)
        E("vector", lambda v: v.tensor_scalar(stats[:, 14:15], stats[:, 14:15],
                                              MOE_EPS, 0.0, ALU.max, ALU.add),
          inc=s_v, drain=True)
        E("vector", lambda v: v.reciprocal(stats[:, 15:16], stats[:, 14:15]),
          inc=s_v, drain=True)
        E("vector", lambda v, i=i: v.tensor_tensor(wflat[:, i:i + 1], stats[:, 12:13],
                                                   stats[:, 15:16], ALU.mult),
          inc=s_v, drain=True)
        E("vector", lambda v, i=i: v.tensor_tensor(wflat[:, 256 + i:257 + i],
                                                   stats[:, 13:14], stats[:, 15:16],
                                                   ALU.mult), inc=s_v, drain=True)
    for e in range(8):
        E("vector", lambda v, e=e: v.tensor_scalar(emask[e][:], eflat[:], float(e), 0.0,
                                                   ALU.is_equal, ALU.add),
          inc=s_v, drain=True)
    rel_midc_pe = 0
    for it in range(42):
        E("vector", lambda v: v.tensor_tensor(rowst[0:1, 16:24], rowst[0:1, 0:8],
                                              rowst[0:1, 8:16], ALU.add),
          inc=s_v, drain=True)
        E("vector", lambda v: v.tensor_scalar(rowst[0:1, 16:24], rowst[0:1, 16:24],
                                              0.5, 0.0, ALU.mult, ALU.add),
          inc=s_v, drain=True)
        vm = p.now(s_v)
        E("tensor", lambda t: t.matmul(pbc[:, 0:8], onesf[:], rowst[0:1, 16:24],
                                       start=True, stop=True),
          waits=[(s_v, vm), (s_g, 2), (s_v, rel_midc_pe)], inc=s_pe)
        pe_m = p.now(s_pe)
        E("vector", lambda v: v.tensor_copy(midc[:], pbc[:, 0:8]),
          waits=[(s_pe, pe_m)], inc=s_v, drain=True)
        E("vector", lambda v: v.tensor_scalar(accb[:], emask[0][:], midc[:, 0:1], 0.0,
                                              ALU.mult, ALU.add), inc=s_v, drain=True)
        for e in range(1, 8):
            E("vector", lambda v, e=e: v.scalar_tensor_tensor(
                accb[:], emask[e][:], midc[:, e:e + 1], accb[:], ALU.mult, ALU.add),
              inc=s_v, drain=True)
        E("vector", lambda v: v.tensor_tensor(cmpf[:], wflat[:], accb[:], ALU.is_gt),
          inc=s_v, drain=True)
        for e in range(8):
            E("vector", lambda v, e=e: v.scalar_tensor_tensor(
                scratch[:, 0:512], cmpf[:], 1.0, emask[e][:], ALU.mult, ALU.mult,
                accum_out=cnt8[:, e:e + 1]), inc=s_v, drain=True)
        vcnt = p.now(s_v)
        E("tensor", lambda t: t.matmul(pbc[0:1, 0:8], onescol[:], cnt8[:],
                                       start=True, stop=True),
          waits=[(s_vcnt := s_v, vcnt), (s_g, 3)], inc=s_pe)
        pe_c = p.now(s_pe)
        rel_midc_pe = pe_c
        E("vector", lambda v: v.tensor_copy(rowst[0:1, 24:32], pbc[0:1, 0:8]),
          waits=[(s_pe, pe_c)], inc=s_v, drain=True)
        E("vector", lambda v: v.tensor_scalar(rowst[0:1, 24:32], rowst[0:1, 24:32],
                                              CAPACITY, 0.0, ALU.is_ge, ALU.add),
          inc=s_v, drain=True)
        E("vector", lambda v: v.tensor_tensor(stats[0:1, 16:24], rowst[0:1, 16:24],
                                              rowst[0:1, 0:8], ALU.subtract),
          inc=s_v, drain=True)
        E("vector", lambda v: v.tensor_tensor(stats[0:1, 16:24], stats[0:1, 16:24],
                                              rowst[0:1, 24:32], ALU.mult),
          inc=s_v, drain=True)
        E("vector", lambda v: v.tensor_tensor(rowst[0:1, 0:8], rowst[0:1, 0:8],
                                              stats[0:1, 16:24], ALU.add),
          inc=s_v, drain=True)
        E("vector", lambda v: v.tensor_scalar(rowst[0:1, 24:32], rowst[0:1, 24:32],
                                              -1.0, 1.0, ALU.mult, ALU.add),
          inc=s_v, drain=True)
        E("vector", lambda v: v.tensor_tensor(stats[0:1, 16:24], rowst[0:1, 16:24],
                                              rowst[0:1, 8:16], ALU.subtract),
          inc=s_v, drain=True)
        E("vector", lambda v: v.tensor_tensor(stats[0:1, 16:24], stats[0:1, 16:24],
                                              rowst[0:1, 24:32], ALU.mult),
          inc=s_v, drain=True)
        E("vector", lambda v: v.tensor_tensor(rowst[0:1, 8:16], rowst[0:1, 8:16],
                                              stats[0:1, 16:24], ALU.add),
          inc=s_v, drain=True)
    E("vector", lambda v: v.tensor_copy(rowst[0:1, 16:24], rowst[0:1, 0:8]),
      inc=s_v, drain=True)
    vth = p.now(s_v)
    E("tensor", lambda t: t.matmul(pbc[:, 0:8], onesf[:], rowst[0:1, 16:24],
                                   start=True, stop=True),
      waits=[(s_v, vth), (s_v, rel_midc_pe if False else 0), (s_pe, rel_midc_pe)],
      inc=s_pe)
    pe_t = p.now(s_pe)
    E("vector", lambda v: v.tensor_copy(midc[:], pbc[:, 0:8]),
      waits=[(s_pe, pe_t)], inc=s_v, drain=True)
    E("vector", lambda v: v.tensor_scalar(accb[:], emask[0][:], midc[:, 0:1], 0.0,
                                          ALU.mult, ALU.add), inc=s_v, drain=True)
    for e in range(1, 8):
        E("vector", lambda v, e=e: v.scalar_tensor_tensor(
            accb[:], emask[e][:], midc[:, e:e + 1], accb[:], ALU.mult, ALU.add),
          inc=s_v, drain=True)
    E("vector", lambda v: v.tensor_tensor(kflat[:], wflat[:], accb[:], ALU.is_gt),
      inc=s_v, drain=True)
    E("vector", lambda v: v.tensor_tensor(kw1[:], kflat[:, 0:256], wflat[:, 0:256],
                                          ALU.mult), inc=s_v, drain=True)
    E("vector", lambda v: v.tensor_tensor(kw2[:], kflat[:, 256:512], wflat[:, 256:512],
                                          ALU.mult), inc=s_v, drain=True)
    E("vector", lambda v: v.tensor_tensor(scratch[:, 0:256], kw1[:], kw2[:], ALU.add),
      inc=s_v, drain=True)
    E("vector", lambda v: v.tensor_scalar(scratch[:, 0:256], scratch[:, 0:256],
                                          MOE_EPS, 0.0, ALU.max, ALU.add),
      inc=s_v, drain=True)
    E("vector", lambda v: v.reciprocal(scratch[:, 256:512], scratch[:, 0:256]),
      inc=s_v, drain=True)
    E("vector", lambda v: v.tensor_tensor(kw1[:], kw1[:], scratch[:, 256:512],
                                          ALU.mult), inc=s_v, drain=True)
    E("vector", lambda v: v.tensor_tensor(kw2[:], kw2[:], scratch[:, 256:512],
                                          ALU.mult), inc=s_v, drain=True)
    for i in range(NT):
        E("vector", lambda v, i=i: v.tensor_scalar(maskA[:], idx8s[:],
                                                   eflat[:, i:i + 1], 0.0,
                                                   ALU.is_equal, ALU.add),
          inc=s_v, drain=True)
        E("vector", lambda v, i=i: v.tensor_scalar(maskB[:], idx8s[:],
                                                   eflat[:, 256 + i:257 + i], 0.0,
                                                   ALU.is_equal, ALU.add),
          inc=s_v, drain=True)
        E("vector", lambda v, i=i: v.tensor_scalar(tmp8[:], maskB[:],
                                                   kw2[:, i:i + 1], 0.0,
                                                   ALU.mult, ALU.add),
          inc=s_v, drain=True)
        E("vector", lambda v, i=i: v.scalar_tensor_tensor(
            cw[:, i * 8:(i + 1) * 8], maskA[:], kw1[:, i:i + 1], tmp8[:],
            ALU.mult, ALU.add), inc=s_v, drain=True)
    cw_done = p.now(s_v)
    if debug:
        for (dst, src) in ((dbg_wf, wflat), (dbg_ef, eflat), (dbg_kf, kflat),
                           (dbg_cw, cw), (dbg_th, midc)):
            E("gpsimd", lambda g, dst=dst, src=src: g.dma_start(out=dst.ap(),
                                                               in_=src[:]),
              waits=[(s_v, cw_done)], inc=s_st, amt=16)

    rel_w_pe = 0
    rel_xsb = [0, 0]
    rel_gtb = 0
    rel_pm1v = [0, 0]
    rel_silb_v = 0
    rel_pm1sc = 0
    rel_pm2v = [0, 0]
    rel_stage = [0, 0]
    prev_acc_st = 0
    NT2 = 16
    for e in range(8):
        for blk in range(4):
            E("gpsimd", lambda g, e=e, blk=blk: g.dma_start(
                out=wpre[:], in_=chunkmajor_ap(preTD, D, 4096, 1024,
                                               col0=e * D * 4096 + blk * 1024)),
              waits=[(s_pe, rel_w_pe)], inc=s_wt, amt=16)
            E("gpsimd", lambda g, e=e, blk=blk: g.dma_start(
                out=wgate[:], in_=chunkmajor_ap(gateTD, D, 4096, 1024,
                                                col0=e * D * 4096 + blk * 1024)),
              inc=s_wt, amt=16)
            E("gpsimd", lambda g, e=e, blk=blk: g.dma_start(
                out=wpost[:], in_=chunkmajor_ap(postTD, 4096, D, D,
                                                col0=e * 4096 * D + blk * 1024 * D,
                                                nrow_chunks=8)),
              inc=s_wt, amt=16)
            wt_eb = p.now(s_wt)
            for ti in range(NT2):
                b2 = ti % 2
                E("sync", lambda y, ti=ti, b2=b2: y.dma_start(
                    out=xsb[b2][:], in_=chunkmajor_ap(ht_scr, D, T, 256,
                                                      col0=ti * 256)),
                  waits=[(s_st, p1_st), (s_pe, rel_xsb[b2])], inc=s_ld, amt=16)
                ld_x = p.now(s_ld)
                for s_ in range(8):
                    bs = s_ % 2
                    for cc in range(DC):
                        E("tensor", lambda t, s_=s_, cc=cc, bs=bs, b2=b2: t.matmul(
                            pm1[bs][:],
                            wpre[:, cc * 1024 + s_ * 128: cc * 1024 + s_ * 128 + 128],
                            xsb[b2][:, cc * 256:(cc + 1) * 256],
                            start=(cc == 0), stop=(cc == DC - 1)),
                          waits=[(s_wt, wt_eb), (s_ld, ld_x), (s_v, rel_pm1v[bs])],
                          inc=s_pe if cc == DC - 1 else None)
                    for cc in range(DC):
                        E("tensor", lambda t, s_=s_, cc=cc, b2=b2: t.matmul(
                            pmg[:],
                            wgate[:, cc * 1024 + s_ * 128: cc * 1024 + s_ * 128 + 128],
                            xsb[b2][:, cc * 256:(cc + 1) * 256],
                            start=(cc == 0), stop=(cc == DC - 1)),
                          waits=[(s_sc, rel_pm1sc)],
                          inc=s_pe if cc == DC - 1 else None)
                    pe_mm1 = p.now(s_pe)
                    E("scalar", lambda s: s.activation(silb[:], pmg[:], AF.Silu),
                      waits=[(s_pe, pe_mm1), (s_v, rel_silb_v)], inc=s_sc, drain=True)
                    rel_pm1sc = p.now(s_sc)
                    E("vector", lambda v, s_=s_, bs=bs: v.tensor_tensor(
                        gtb[:, s_ * 256:(s_ + 1) * 256], pm1[bs][:], silb[:],
                        ALU.mult),
                      waits=[(s_sc, rel_pm1sc), (s_pe, rel_gtb)], inc=s_v, drain=True)
                    rel_pm1v[bs] = p.now(s_v)
                    rel_silb_v = p.now(s_v)
                v_gtb = p.now(s_v)
                if debug and e == 0 and blk == 0 and ti == 0:
                    E("gpsimd", lambda g: g.dma_start(out=dbg_g0.ap(), in_=gtb[:]),
                      waits=[(s_v, v_gtb)], inc=s_st, amt=16)
                for tsub in range(2):
                    gidx = ti * 2 + tsub
                    bst = gidx % 2
                    for co in range(2):
                        for s_ in range(8):
                            E("tensor", lambda t, s_=s_, tsub=tsub, co=co: t.matmul(
                                pm2[co][:],
                                gtb[:, s_ * 256 + tsub * 128: s_ * 256 + tsub * 128 + 128],
                                wpost[:, s_ * D + co * 512: s_ * D + co * 512 + 512],
                                start=(s_ == 0), stop=(s_ == 7)),
                              waits=[(s_v, v_gtb), (s_v, rel_pm2v[co]),
                                     (s_sc, rel_pm1sc if co == 0 else 0)],
                              inc=s_pe if s_ == 7 else None)
                        pe_mm2 = p.now(s_pe)
                        E("vector", lambda v, co=co, gidx=gidx, e=e, bst=bst:
                          v.tensor_scalar(stage[bst][:, co * 512:(co + 1) * 512],
                                          pm2[co][:],
                                          cw[:, gidx * 8 + e: gidx * 8 + e + 1], 0.0,
                                          ALU.mult, ALU.add),
                          waits=[(s_pe, pe_mm2), (s_v, cw_done),
                                 (s_st, rel_stage[bst])], inc=s_v, drain=True)
                        rel_pm2v[co] = p.now(s_v)
                    v_st = p.now(s_v)
                    if e == 0 and blk == 0:
                        E("gpsimd", lambda g, gidx=gidx, bst=bst: g.dma_start(
                            out=acc_scr.ap()[gidx * 128:(gidx + 1) * 128, :],
                            in_=stage[bst][:]),
                          waits=[(s_v, v_st)], inc=s_st, amt=16)
                    else:
                        E("gpsimd", lambda g, gidx=gidx, bst=bst: g.dma_start(
                            out=acc_scr.ap()[gidx * 128:(gidx + 1) * 128, :],
                            in_=stage[bst][:], accum_op=ALU.add),
                          waits=[(s_v, v_st), (s_st, prev_acc_st)], inc=s_st, amt=16)
                    rel_stage[bst] = p.now(s_st)
                rel_gtb = p.now(s_pe)
                rel_xsb[b2] = p.now(s_pe)
            rel_w_pe = p.now(s_pe)
            prev_acc_st = p.now(s_st)
    acc_done = p.now(s_st)
    if debug:
        E("gpsimd", lambda g: g.dma_start(out=dbg_acc.ap(), in_=acc_scr.ap()),
          waits=[(s_st, acc_done)], inc=s_st, amt=16)

    rel_at = 0
    rel_ob = 0
    for i in range(NT):
        b = i % 2
        E("sync", lambda y, i=i, b=b: y.dma_start(
            out=rbuf[b][:], in_=res1.ap()[i * 128:(i + 1) * 128, :]),
          waits=[(s_v, rel_rbuf[b])], inc=s_ld, amt=16)
        E("sync", lambda y, i=i: y.dma_start(
            out=atile[:], in_=acc_scr.ap()[i * 128:(i + 1) * 128, :]),
          waits=[(s_st, acc_done), (s_v, rel_at)], inc=s_ld, amt=16)
        ldc = p.now(s_ld)
        E("vector", lambda v, b=b: v.tensor_tensor(obuf[:], rbuf[b][:], atile[:],
                                                   ALU.add),
          waits=[(s_ld, ldc), (s_st, rel_ob)], inc=s_v, drain=True)
        rel_rbuf[b] = p.now(s_v)
        rel_at = p.now(s_v)
        vo = p.now(s_v)
        E("gpsimd", lambda g, i=i: g.dma_start(
            out=out.ap()[i * 128:(i + 1) * 128, :], in_=obuf[:]),
          waits=[(s_v, vo)], inc=s_st, amt=16)
        rel_ob = p.now(s_st)
    fin = p.now(s_st)
    E("gpsimd", lambda g: g.nop(), waits=[(s_st, fin)])
    return p.finalize() if fin_own else None


def run_launch_b(res1_full, inputs, debug=False, lg=None):
    if lg is None:
        lg = run_launch_b1(res1_full, inputs)
    lgflat = lg.reshape(8 * T, 8)
    nc = build_launch_b2(debug)
    pre = np.asarray(inputs["L0_ffn_ff_pre_act_weight"], np.float32)
    gat = np.asarray(inputs["L0_ffn_gate_weight"], np.float32)
    pos = np.asarray(inputs["L0_ffn_ff_post_act_weight"], np.float32)
    com = dict(
        preT=np.ascontiguousarray(pre.transpose(0, 2, 1)),
        gateT=np.ascontiguousarray(gat.transpose(0, 2, 1)),
        postT=np.ascontiguousarray(pos.transpose(0, 2, 1)),
        idx8=np.tile(np.arange(8, dtype=np.float32), (128, 1)),
        idx8m8=np.tile(np.arange(8, dtype=np.float32) - 8.0, (128, 1)),
        ident=np.eye(128, dtype=np.float32),
    )
    in_maps = []
    for b in range(8):
        m = dict(com)
        m["res1"] = np.ascontiguousarray(res1_full[b])
        m["lgall"] = np.ascontiguousarray(np.roll(lgflat, -b * T, axis=0))
        in_maps.append(m)
    res = run_bass_kernel_spmd(nc, in_maps, core_ids=list(range(8)))
    if debug:
        return res.results
    return np.stack([r["out"] for r in res.results])



ARENA_BYTES = 210944

def emit_gather_rotate(ctx, cc_in, cc_out, lgrot, rotpD):
    nc, p = ctx.nc, ctx.p
    ctx.begin_phase("rot")
    E = p.emit
    s_cc = p.sem("s_cc")
    s_ld = p.sem("s_ld")
    s_pe = p.sem("s_pe", owner="tensor")
    s_v = p.sem("s_v", owner="vector")
    s_st = p.sem("s_st")
    rsb = ctx.sb("rsb", [8, 8], F32)
    chunk = ctx.sb("rchunk", [8, 4096], F32)
    stg = [ctx.sb(f"rstage{i}", [8, 512], F32) for i in range(2)]
    ps = ctx.psum("prot", [8, 512], F32)

    def blockview(dram, col0, width):
        ap = dram.ap().copy()
        while len(ap.ap) > 1:
            ap.ap.pop()
        ap.ap[0] = [8 * T, 8]
        ap.ap.append([1, width])
        ap.offset = col0
        return ap

    E("gpsimd", lambda g: g.collective_compute(
        "AllGather", ALU.bypass, replica_groups=[list(range(8))],
        ins=[cc_in.ap().opt()], outs=[cc_out.ap().opt()]), inc=s_cc, amt=1)
    E("sync", lambda y: y.dma_start(out=rsb, in_=rotpD.ap()), inc=s_ld, amt=16)
    rel_chunk = 0
    rel_stage = [0, 0]
    rel_ps = 0
    for c in range(8):
        E("sync", lambda y, c=c: y.dma_start(out=chunk, in_=blockview(cc_out, c * 4096,
                                                                      4096)),
          waits=[(s_cc, 1), (s_pe, rel_chunk)], inc=s_ld, amt=16)
        ldc = p.now(s_ld)
        for k in range(8):
            bs = k % 2
            E("tensor", lambda t, k=k: t.matmul(ps, rsb, chunk[:, k * 512:(k + 1) * 512],
                                                start=True, stop=True),
              waits=[(s_ld, ldc), (s_v, rel_ps)], inc=s_pe)
            pe = p.now(s_pe)
            E("vector", lambda v, bs=bs: v.tensor_copy(stg[bs][:], ps),
              waits=[(s_pe, pe), (s_st, rel_stage[bs])], inc=s_v, drain=True)
            rel_ps = p.now(s_v)
            vv = p.now(s_v)
            E("gpsimd", lambda g, c=c, k=k, bs=bs: g.dma_start(
                out=blockview(lgrot, c * 4096 + k * 512, 512), in_=stg[bs][:]),
              waits=[(s_v, vv)], inc=s_st, amt=16)
            rel_stage[bs] = p.now(s_st)
        rel_chunk = p.now(s_pe)
    fin = p.now(s_st)
    E("gpsimd", lambda g: g.nop(), waits=[(s_st, fin)])


def build_fused():
    nc = bass.Bass("TRN2", target_bir_lowering=False)
    p = Prog(nc)
    arena = Arena(nc, ARENA_BYTES)
    pool = PsumPool(nc)
    ctx = FusedCtx(nc, p, arena, pool)

    xin = nc.dram_tensor("xin", [T, D], F32, kind="ExternalInput")
    out = nc.dram_tensor("out", [T, D], F32, kind="ExternalOutput")
    rotpD = nc.dram_tensor("rotp", [8, 8], F32, kind="ExternalInput")
    res1_scr = nc.dram_tensor("res1_scr", [T, D], F32, kind="Internal")
    res2_scr = nc.dram_tensor("res2_scr", [T, D], F32, kind="Internal")
    res3_scr = nc.dram_tensor("res3_scr", [T, D], F32, kind="Internal")
    cc_in = nc.dram_tensor("cc_in", [T, 8], F32, kind="Internal")
    cc_out = nc.dram_tensor("cc_out", [8 * T, 8], F32, kind="Internal")
    lgrot_scr = nc.dram_tensor("lgrot_scr", [8 * T, 8], F32, kind="Internal")

    ctx.redirect = {
        ("a", "xin"): xin, ("a", "out"): res1_scr,
        ("b1", "res1"): res1_scr, ("b1", "lg"): cc_in,
        ("b2", "res1"): res1_scr, ("b2", "lgall"): lgrot_scr, ("b2", "out"): res2_scr,
        ("c", "res2"): res2_scr, ("c", "out"): res3_scr,
        ("d", "res3"): res3_scr, ("d", "out"): out,
    }
    build_launch_a(ctx=ctx)
    build_launch_b1(ctx=ctx)
    emit_gather_rotate(ctx, cc_in, cc_out, lgrot_scr, rotpD)
    build_launch_b2(ctx=ctx)
    build_launch_c(ctx=ctx)
    build_launch_d(ctx=ctx)
    p.barrier()
    return p.finalize(), ctx.input_names


def kernel(**inputs):
    x = np.ascontiguousarray(np.asarray(inputs["L0_attn_x"], np.float32))
    nc, input_names = build_fused()
    preps = {
        "a": prep_launch_a(inputs),
        "b1": dict(wrt=np.ascontiguousarray(
            np.asarray(inputs["L0_ffn_router_weight"], np.float32).T),
            ident=np.eye(128, dtype=np.float32)),
        "b2": dict(
            preT=np.ascontiguousarray(np.asarray(
                inputs["L0_ffn_ff_pre_act_weight"], np.float32).transpose(0, 2, 1)),
            gateT=np.ascontiguousarray(np.asarray(
                inputs["L0_ffn_gate_weight"], np.float32).transpose(0, 2, 1)),
            postT=np.ascontiguousarray(np.asarray(
                inputs["L0_ffn_ff_post_act_weight"], np.float32).transpose(0, 2, 1)),
            idx8=np.tile(np.arange(8, dtype=np.float32), (128, 1)),
            idx8m8=np.tile(np.arange(8, dtype=np.float32) - 8.0, (128, 1)),
            ident=np.eye(128, dtype=np.float32),
        ),
        "c": prep_launch_c(inputs),
        "d": prep_launch_d(inputs),
    }
    common = {}
    for fname, phase, orig in input_names:
        common[fname] = np.ascontiguousarray(np.asarray(preps[phase][orig],
                                                        np.float32))
    in_maps = []
    for b in range(8):
        m = dict(common)
        m["xin"] = np.ascontiguousarray(x[b])
        rp = np.zeros((8, 8), np.float32)
        for i in range(8):
            rp[(i + b) % 8, i] = 1.0
        m["rotp"] = rp
        in_maps.append(m)
    res = run_bass_kernel_spmd(nc, in_maps, core_ids=list(range(8)))
    return np.stack([r["out"] for r in res.results]).astype(np.float32)


# version 03 (final)
# speedup vs baseline: 2.0236x; 2.0236x over previous
import contextlib
import os
os.environ.setdefault("BASS_DISABLE_FRAME_TO_TRACEBACK", "1")
import numpy as np
import concourse.bass as bass
import concourse.mybir as mybir
from concourse.bass_utils import run_bass_kernel_spmd

F32 = mybir.dt.float32
BF16 = mybir.dt.bfloat16
AF = mybir.ActivationFunctionType
ALU = mybir.AluOpType
AX = mybir.AxisListType

ENGINES = ("tensor", "vector", "scalar", "gpsimd", "sync")

T = 4096
D = 1024
NT = T // 128
DC = D // 128


class Prog:

    def __init__(self, nc):
        self.nc = nc
        self.q = {e: [] for e in ENGINES}
        self.sem_names = []
        self.counts = {}
        self.last_wait = {e: {} for e in ENGINES}

    def sem(self, name, owner=None):
        name = f"{name}_{len(self.sem_names)}"
        self.sem_names.append(name)
        self.counts[name] = 0
        self.owner = getattr(self, "owner", {})
        self.owner[name] = owner
        return name

    def barrier(self):
        snap = [(sname, c) for sname, c in self.counts.items() if c > 0]
        for eng in ENGINES:
            for sname, c in snap:
                if self.owner.get(sname) == eng:
                    continue
                if c <= self.last_wait[eng].get(sname, 0):
                    continue
                self.last_wait[eng][sname] = c
                self.q[eng].append(([(sname, c)],
                                    (lambda h: h.nop()), None, 1, False))

    def now(self, s):
        return self.counts[s]

    def emit(self, eng, fn, waits=(), inc=None, amt=1, drain=False):
        wl = []
        for s, c in waits:
            assert c <= self.counts[s], f"wait on future count: {s} {c} > {self.counts[s]}"
            if c <= 0 or c <= self.last_wait[eng].get(s, 0):
                continue
            if self.owner.get(s) == eng:
                continue
            self.last_wait[eng][s] = c
            wl.append((s, c))
        if inc is not None:
            self.counts[inc] += amt
        self.q[eng].append((wl, fn, inc, amt, drain))
        return self.counts[inc] if inc is not None else None

    def finalize(self):
        nc = self.nc
        with contextlib.ExitStack() as stack:
            sems = {n: stack.enter_context(nc.semaphore(n)) for n in self.sem_names}
            blk = stack.enter_context(nc.Block())
            for eng in ENGINES:
                ops = self.q[eng]
                if not ops:
                    continue

                def body(h, ops=ops):
                    for wl, fn, inc, amt, drain in ops:
                        for s, c in wl:
                            h.wait_ge(sems[s], c)
                        r = fn(h)
                        if drain:
                            r = h.drain()
                        if inc is not None:
                            r.then_inc(sems[inc], amt)

                getattr(blk, eng)(body)
        return nc


def chunkmajor_ap(dram, rows, row_stride, ncols, col0=0, nrow_chunks=None):
    if nrow_chunks is None:
        nrow_chunks = rows // 128
    ap = dram.ap()
    flat = ap.copy()
    while len(flat.ap) > 1:
        flat.ap.pop()
    flat.ap[0] = [row_stride, 128]
    flat.ap.append([row_stride * 128, nrow_chunks])
    flat.ap.append([1, ncols])
    flat.offset = col0
    return flat




class Arena:

    def __init__(self, nc, nbytes):
        self.t = nc.alloc_sbuf_tensor("arena", [128, nbytes], mybir.dt.uint8)
        self.nbytes = nbytes
        self.off = 0
        self.high = 0

    def reset(self):
        self.off = 0

    def alloc(self, name, shape, dtype):
        assert len(shape) == 2 and shape[0] <= 128, (name, shape)
        per = shape[1] * mybir.dt.size(dtype)
        off = (self.off + 31) // 32 * 32
        self.off = off + per
        self.high = max(self.high, self.off)
        assert self.off <= self.nbytes, f"arena overflow at {name}: {self.off}"
        v = self.t.ap()[0:shape[0], off:off + per].bitcast(dtype)
        return v


class PsumPool:
    def __init__(self, nc):
        self.f32 = [nc.alloc_psum_tensor(f"pp{i}", [128, 512], F32) for i in range(6)]
        self.bf = [nc.alloc_psum_tensor(f"pb{i}", [128, 1024], BF16) for i in range(2)]
        self.reset()

    def reset(self):
        self.i32 = 0
        self.ibf = 0

    def alloc(self, name, shape, dtype):
        if dtype == BF16:
            t = self.bf[self.ibf]
            self.ibf += 1
        else:
            t = self.f32[self.i32]
            self.i32 += 1
        return t[0:shape[0], 0:shape[1]]


class FusedCtx:
    def __init__(self, nc, p, arena, pool):
        self.nc = nc
        self.p = p
        self.arena = arena
        self.pool = pool
        self.phase = ""
        self.redirect = {}
        self.input_names = []

    def begin_phase(self, phase):
        self.phase = phase
        self.arena.reset()
        self.pool.reset()
        self.p.barrier()

    def dram(self, name, shape, dtype, kind):
        key = (self.phase, name)
        if key in self.redirect:
            return self.redirect[key]
        fname = f"{self.phase}_{name}"
        h = self.nc.dram_tensor(fname, shape, dtype, kind=kind)
        if kind == "ExternalInput":
            self.input_names.append((fname, self.phase, name))
        return h

    def sb(self, name, shape, dtype):
        return self.arena.alloc(name, shape, dtype)

    def psum(self, name, shape, dtype):
        return self.pool.alloc(name, shape, dtype)


def _ctx_or_new(ctx, phase):
    if ctx is None:
        nc = bass.Bass("TRN2", target_bir_lowering=False, debug=False,
                       disable_frame_to_traceback=True)
        p = Prog(nc)

        def DT(name, shape, dtype, kind):
            return nc.dram_tensor(name, shape, dtype, kind=kind)

        def psum(name, shape, dtype):
            return nc.alloc_psum_tensor(name, shape, dtype)

        return nc, p, DT, nc.alloc_sbuf_tensor, psum, True
    ctx.begin_phase(phase)
    return ctx.nc, ctx.p, (lambda n, sh, dt, kind: ctx.dram(n, sh, dt, kind)), \
        ctx.sb, ctx.psum, False



def build_launch_d(debug=False, ctx=None):
    nc, p, DT, sb, psum, fin_own = _ctx_or_new(ctx, "d")


    res3 = DT("res3", [T, D], F32, "ExternalInput")
    w1t = DT("w1t", [8, D, 4096], F32, "ExternalInput")
    w2t = DT("w2t", [8, 4096, D], F32, "ExternalInput")
    b1c = DT("b1c", [8, 128, 32], F32, "ExternalInput")
    b2r = DT("b2r", [8, 1, D], F32, "ExternalInput")
    wrt = DT("wrt", [D, 8], F32, "ExternalInput")
    brv = DT("brv", [1, 8], F32, "ExternalInput")
    lnw = DT("lnw", [1, D], F32, "ExternalInput")
    lnb = DT("lnb", [1, D], F32, "ExternalInput")
    idx8 = DT("idx8", [128, 8], F32, "ExternalInput")
    idx8m8 = DT("idx8m8", [128, 8], F32, "ExternalInput")
    ident = DT("ident", [128, 128], F32, "ExternalInput")
    out = DT("out", [T, D], F32, "ExternalOutput")
    ht_scr = DT("ht_scr", [D, T], BF16, "Internal")
    acc_scr = DT("acc_scr", [T, D], F32, "Internal")
    if debug:
        dbg_lg = DT("dbg_lg", [128, NT * 8], F32, "ExternalOutput")
        dbg_cw = DT("dbg_cw", [128, NT * 8], F32, "ExternalOutput")
        dbg_acc = DT("dbg_acc", [T, D], F32, "ExternalOutput")
        dbg_ht = DT("dbg_ht", [D, T], BF16, "ExternalOutput")
        dbg_hts = DT("dbg_hts", [128, D], BF16, "ExternalOutput")
        dbg_hbf = DT("dbg_hbf", [128, D], BF16, "ExternalOutput")
        dbg_h32 = DT("dbg_h32", [128, D], F32, "ExternalOutput")

    rbuf = [sb(f"rbuf{i}", [128, D], F32) for i in range(2)]
    scratch = sb("scratch", [128, D], F32)
    h32 = sb("h32", [128, D], F32)
    hbf = sb("hbf", [128, D], BF16)
    hlo = sb("hlo", [128, D], BF16)
    hts = sb("hts", [128, D], BF16)
    htl = sb("htl", [128, D], BF16)
    stats = sb("stats", [128, 24], F32)
    epsc = sb("epsc", [128, 2], F32)
    lgall = sb("lgall", [128, NT * 8], F32) if debug else None
    cw = sb("cw", [128, NT * 8], F32)
    idx8s = sb("idx8s", [128, 8], F32)
    idx8m8s = sb("idx8m8s", [128, 8], F32)
    l8 = sb("l8", [128, 8], F32)
    tmp8a = sb("tmp8a", [128, 8], F32)
    tmp8b = sb("tmp8b", [128, 8], F32)
    maskA = sb("maskA", [128, 8], F32)
    maskB = sb("maskB", [128, 8], F32)
    wrf = sb("wrf", [128, 64], F32)
    wrh = sb("wrh", [128, 64], BF16)
    wrl = sb("wrl", [128, 64], BF16)
    brb = sb("brb", [128, 8], F32)
    lnwb = sb("lnwb", [128, D], F32)
    lnbb = sb("lnbb", [128, D], F32)
    onesf = sb("onesf", [1, 128], F32)
    onesb = sb("onesb", [1, 128], BF16)
    identb = sb("identb", [128, 128], BF16)
    w1s = sb("w1s", [128, 8 * 4096], BF16)
    w2s = sb("w2s", [128, 32 * D], BF16)
    b1s = sb("b1s", [128, 32], F32)
    b2s = sb("b2s", [1, D], BF16)
    ht2 = [sb(f"ht2_{i}", [128, 8 * 256], BF16) for i in range(2)]
    gt = sb("gt", [128, 32 * 256], BF16)
    stage = [sb(f"stage{i}", [128, D], F32) for i in range(2)]
    atile = [sb("atile0", [128, D], F32)]
    obuf = [stage[1]]

    pm1 = [psum(f"pm1_{i}", [128, 512], F32) for i in range(2)]
    pm2 = [psum(f"pm2_{i}", [128, 512], F32) for i in range(2)]
    ptr = psum("ptr", [128, 1024], BF16)
    ptr2 = psum("ptr2", [128, 1024], BF16)
    pbc = psum("pbc", [128, 512], F32)

    s_ld = p.sem("s_ld")
    s_wt = p.sem("s_wt")
    s_pe = p.sem("s_pe", owner="tensor")
    s_v = p.sem("s_v", owner="vector")
    s_sc = p.sem("s_sc", owner="scalar")
    s_st = p.sem("s_st")
    s_g = p.sem("s_g", owner="gpsimd")

    lnwr = scratch[0:1, :]
    lnbr = h32[0:1, :]
    brs = cw[0:1, 0:8]

    E = p.emit

    E("gpsimd", lambda g: g.memset(onesf[:], 1.0), inc=s_g)
    E("gpsimd", lambda g: g.memset(epsc[:, 0:1], 1e-5), inc=s_g)
    E("gpsimd", lambda g: g.memset(epsc[:, 1:2], 1e-12), inc=s_g)
    E("gpsimd", lambda g: g.memset(onesb[:], 1.0), inc=s_g)
    E("gpsimd", lambda g: g.dma_start(out=identb[:], in_=ident.ap()), inc=s_wt, amt=16)
    E("sync", lambda y: y.dma_start(out=wrf[:], in_=chunkmajor_ap(wrt, D, 8, 8)),
      inc=s_ld, amt=16)
    E("sync", lambda y: y.dma_start(out=brs, in_=brv.ap()), inc=s_ld, amt=16)
    E("sync", lambda y: y.dma_start(out=lnwr, in_=lnw.ap()), inc=s_ld, amt=16)
    E("sync", lambda y: y.dma_start(out=lnbr, in_=lnb.ap()), inc=s_ld, amt=16)
    E("sync", lambda y: y.dma_start(out=idx8s[:], in_=idx8.ap()), inc=s_ld, amt=16)
    E("sync", lambda y: y.dma_start(out=idx8m8s[:], in_=idx8m8.ap()), inc=s_ld, amt=16)
    ld_setup = p.now(s_ld)
    E("vector", lambda v: v.tensor_copy(wrh[:], wrf[:]), waits=[(s_ld, ld_setup)],
      inc=s_v, drain=True)
    E("vector", lambda v: v.tensor_tensor(wrl[:], wrf[:], wrh[:], ALU.subtract),
      inc=s_v, drain=True)
    v_wr = p.now(s_v)
    for (dst, srcrow, wid) in ((lnwb, lnwr, D), (lnbb, lnbr, D), (brb, brs, 8)):
        for c0 in range(0, wid, 512):
            w = min(512, wid - c0)
            E("tensor",
              lambda t, srcrow=srcrow, c0=c0, w=w: t.matmul(
                  pbc[:, 0:w], onesf[:], srcrow[0:1, c0:c0 + w], start=True, stop=True),
              waits=[(s_g, 1), (s_ld, ld_setup), (s_v, p.now(s_v))], inc=s_pe)
            E("vector",
              lambda v, dst=dst, c0=c0, w=w: v.tensor_copy(dst[:, c0:c0 + w], pbc[:, 0:w]),
              waits=[(s_pe, p.now(s_pe))], inc=s_v, drain=True)

    LNI = 1.0 / D
    EPS1 = 1e-5
    rel_rbuf = [0, 0]
    rel_ptr = 0
    rel_ptr2 = 0
    rel_pbc = p.now(s_v)
    need_hts = 0
    hts_mm = 0
    cw_done_v = 0

    def emit_ln_stats(src, eps_ap, mcol, rstdcol, wv, wsc):
        E("vector", lambda v: v.tensor_reduce(stats[:, 0:1], src[:], AX.X, ALU.add),
          waits=wv, inc=s_v, drain=True)
        E("vector", lambda v: v.tensor_tensor(scratch[:], src[:], src[:], ALU.mult), inc=s_v, drain=True)
        E("vector", lambda v: v.tensor_reduce(stats[:, 2:3], scratch[:], AX.X, ALU.add), inc=s_v, drain=True)
        E("vector", lambda v: v.tensor_scalar(stats[:, mcol:mcol + 1], stats[:, 0:1],
                                              LNI, 0.0, ALU.mult, ALU.add), inc=s_v, drain=True)
        E("vector", lambda v: v.tensor_scalar(stats[:, 3:4], stats[:, 2:3],
                                              LNI, 0.0, ALU.mult, ALU.add), inc=s_v, drain=True)
        E("vector", lambda v: v.tensor_tensor(stats[:, 4:5], stats[:, mcol:mcol + 1],
                                              stats[:, mcol:mcol + 1], ALU.mult), inc=s_v, drain=True)
        E("vector", lambda v: v.tensor_tensor(stats[:, 5:6], stats[:, 3:4],
                                              stats[:, 4:5], ALU.subtract), inc=s_v, drain=True)
        vc = p.now(s_v)
        E("scalar", lambda s: s.activation(stats[:, 6:7], stats[:, 5:6], AF.Ln,
                                           bias=eps_ap),
          waits=[(s_v, vc), (s_g, 4)] + wsc, inc=s_sc, drain=True)
        E("scalar", lambda s: s.activation(stats[:, rstdcol:rstdcol + 1], stats[:, 6:7],
                                           AF.Exp, scale=-0.5), inc=s_sc, drain=True)
        return p.now(s_sc)

    for i in range(NT):
        b = i % 2
        E("sync", lambda y, i=i, b=b: y.dma_start(
            out=rbuf[b][:], in_=res3.ap()[i * 128:(i + 1) * 128, :]),
          waits=[(s_v, rel_rbuf[b])], inc=s_ld, amt=16)
        ldc = p.now(s_ld)
        sc = emit_ln_stats(rbuf[b], epsc[:, 0:1], 1, 7, [(s_ld, ldc)], [])
        E("vector", lambda v, b=b: v.tensor_scalar(
            h32[:], rbuf[b][:], stats[:, 1:2], stats[:, 7:8], ALU.subtract, ALU.mult),
          waits=[(s_sc, sc)], inc=s_v, drain=True)
        rel_rbuf[b] = p.now(s_v)
        E("vector", lambda v: v.tensor_copy(hbf[:], h32[:]), inc=s_v, drain=True)
        E("vector", lambda v: v.tensor_tensor(hlo[:], h32[:], hbf[:], ALU.subtract),
          inc=s_v, drain=True)
        v_hbf = p.now(s_v)
        for c in range(DC):
            E("tensor", lambda t, c=c: t.transpose(
                ptr[:, c * 128:(c + 1) * 128], hbf[:, c * 128:(c + 1) * 128], identb[:]),
              waits=[(s_v, v_hbf), (s_wt, 16), (s_v, rel_ptr)],
              inc=s_pe if c == DC - 1 else None)
        pe_tr1 = p.now(s_pe)
        E("vector", lambda v: v.tensor_copy(hts[:], ptr[:]),
          waits=[(s_pe, pe_tr1), (s_st, need_hts), (s_pe, hts_mm)], inc=s_v, drain=True)
        rel_ptr = p.now(s_v)
        for c in range(DC):
            E("tensor", lambda t, c=c: t.transpose(
                ptr2[:, c * 128:(c + 1) * 128], hlo[:, c * 128:(c + 1) * 128], identb[:]),
              waits=[(s_v, p.now(s_v)), (s_v, rel_ptr2)],
              inc=s_pe if c == DC - 1 else None)
        pe_tr2 = p.now(s_pe)
        E("vector", lambda v: v.tensor_copy(htl[:], ptr2[:]),
          waits=[(s_pe, pe_tr2)], inc=s_v, drain=True)
        rel_ptr2 = p.now(s_v)
        v_ht = p.now(s_v)
        E("sync", lambda y, i=i: y.dma_start(
            out=chunkmajor_ap(ht_scr, D, T, 128, col0=i * 128), in_=hts[:]),
          waits=[(s_v, v_ht)], inc=s_st, amt=16)
        need_hts = p.now(s_st)
        if debug and i == 0:
            E("sync", lambda y: y.dma_start(out=dbg_hts.ap(), in_=hts[:]),
              waits=[(s_v, v_ht)], inc=s_st, amt=16)
            E("sync", lambda y: y.dma_start(out=dbg_hbf.ap(), in_=hbf[:]),
              waits=[(s_v, v_ht)], inc=s_st, amt=16)
            E("sync", lambda y: y.dma_start(out=dbg_h32.ap(), in_=h32[:]),
              waits=[(s_v, v_ht)], inc=s_st, amt=16)
            need_hts = p.now(s_st)
        nmm = 0
        for c in range(DC):
            for (lt, rt) in ((hts, wrh), (hts, wrl), (htl, wrh)):
                nmm += 1
                E("tensor", lambda t, c=c, lt=lt, rt=rt, first=(nmm == 1), last=(nmm == 24):
                  t.matmul(pbc[:, 0:8], lt[:, c * 128:(c + 1) * 128],
                           rt[:, c * 8:(c + 1) * 8], start=first, stop=last),
                  waits=[(s_v, v_ht), (s_v, v_wr), (s_v, rel_pbc)],
                  inc=s_pe if nmm == 24 else None)
        hts_mm = p.now(s_pe)
        E("vector", lambda v: v.tensor_tensor(l8[:], pbc[:, 0:8], brb[:], ALU.add),
          waits=[(s_pe, hts_mm)], inc=s_v, drain=True)
        rel_pbc = p.now(s_v)
        E("vector", lambda v: v.tensor_reduce(stats[:, 8:9], l8[:], AX.X, ALU.max), inc=s_v, drain=True)
        E("vector", lambda v: v.tensor_scalar(maskA[:], l8[:], stats[:, 8:9], 0.0,
                                              ALU.is_equal, ALU.add), inc=s_v, drain=True)
        E("vector", lambda v: v.tensor_tensor(tmp8a[:], maskA[:], idx8m8s[:], ALU.mult), inc=s_v, drain=True)
        E("vector", lambda v: v.tensor_scalar(tmp8a[:], tmp8a[:], 8.0, 0.0,
                                              ALU.add, ALU.add), inc=s_v, drain=True)
        E("vector", lambda v: v.tensor_reduce(stats[:, 10:11], tmp8a[:], AX.X, ALU.min), inc=s_v, drain=True)
        E("vector", lambda v: v.tensor_scalar(maskA[:], idx8s[:], stats[:, 10:11], 0.0,
                                              ALU.is_equal, ALU.add), inc=s_v, drain=True)
        E("vector", lambda v: v.scalar_tensor_tensor(tmp8b[:], maskA[:], -1e30, l8[:],
                                                     ALU.mult, ALU.add), inc=s_v, drain=True)
        E("vector", lambda v: v.tensor_reduce(stats[:, 9:10], tmp8b[:], AX.X, ALU.max), inc=s_v, drain=True)
        E("vector", lambda v: v.tensor_scalar(maskB[:], tmp8b[:], stats[:, 9:10], 0.0,
                                              ALU.is_equal, ALU.add), inc=s_v, drain=True)
        E("vector", lambda v: v.tensor_tensor(tmp8a[:], maskB[:], idx8m8s[:], ALU.mult), inc=s_v, drain=True)
        E("vector", lambda v: v.tensor_scalar(tmp8a[:], tmp8a[:], 8.0, 0.0,
                                              ALU.add, ALU.add), inc=s_v, drain=True)
        E("vector", lambda v: v.tensor_reduce(stats[:, 11:12], tmp8a[:], AX.X, ALU.min), inc=s_v, drain=True)
        E("vector", lambda v: v.tensor_scalar(maskB[:], idx8s[:], stats[:, 11:12], 0.0,
                                              ALU.is_equal, ALU.add), inc=s_v, drain=True)
        E("vector", lambda v: v.tensor_tensor(stats[:, 12:13], stats[:, 8:9],
                                              stats[:, 9:10], ALU.subtract), inc=s_v, drain=True)
        vsw = p.now(s_v)
        E("scalar", lambda s: s.activation(stats[:, 13:14], stats[:, 12:13], AF.Exp,
                                           scale=-1.0), waits=[(s_v, vsw)], inc=s_sc, drain=True)
        E("vector", lambda v: v.tensor_scalar(stats[:, 14:15], stats[:, 13:14], 1.0, 0.0,
                                              ALU.add, ALU.add),
          waits=[(s_sc, p.now(s_sc))], inc=s_v, drain=True)
        E("vector", lambda v: v.reciprocal(stats[:, 15:16], stats[:, 14:15]), inc=s_v, drain=True)
        E("vector", lambda v: v.tensor_tensor(stats[:, 16:17], stats[:, 13:14],
                                              stats[:, 15:16], ALU.mult), inc=s_v, drain=True)
        E("vector", lambda v: v.tensor_scalar(tmp8a[:], maskB[:], stats[:, 16:17], 0.0,
                                              ALU.mult, ALU.add), inc=s_v, drain=True)
        E("vector", lambda v, i=i: v.scalar_tensor_tensor(
            cw[:, i * 8:(i + 1) * 8], maskA[:], stats[:, 15:16], tmp8a[:],
            ALU.mult, ALU.add), inc=s_v, drain=True)
        if debug:
            E("vector", lambda v, i=i: v.tensor_copy(lgall[:, i * 8:(i + 1) * 8], l8[:]),
              inc=s_v, drain=True)
        cw_done_v = p.now(s_v)

    phaseA_st = p.now(s_st)
    phaseA_pe = p.now(s_pe)

    rel_w1s = 0
    rel_w2s = 0
    rel_b = 0
    rel_ht2 = [0, 0]
    rel_gt = 0
    rel_pm1 = [0, 0]
    rel_pm2 = [0, 0]
    rel_stage = [0, 0]
    prev_acc_st = 0
    TT = 16

    for e in range(8):
        E("gpsimd", lambda g, e=e: g.dma_start(
            out=w1s[:], in_=chunkmajor_ap(w1t, D, 4096, 4096,
                                          col0=e * D * 4096)),
          waits=[(s_pe, rel_w1s)], inc=s_wt, amt=16)
        E("gpsimd", lambda g, e=e: g.dma_start(
            out=w2s[:], in_=chunkmajor_ap(w2t, 4096, D, D,
                                          col0=e * 4096 * D, nrow_chunks=32)),
          waits=[(s_pe, rel_w2s)], inc=s_wt, amt=16)
        E("gpsimd", lambda g, e=e: g.dma_start(
            out=b1s[:], in_=b1c.ap()[e]), waits=[(s_sc, rel_b)], inc=s_wt, amt=16)
        E("gpsimd", lambda g, e=e: g.dma_start(
            out=b2s[:], in_=b2r.ap()[e]), waits=[(s_pe, rel_w2s)], inc=s_wt, amt=16)
        wt_e = p.now(s_wt)
        for j in range(TT):
            bj = j % 2
            E("sync", lambda y, j=j, bj=bj: y.dma_start(
                out=ht2[bj][:], in_=chunkmajor_ap(ht_scr, D, T, 256, col0=j * 256)),
              waits=[(s_st, phaseA_st), (s_pe, rel_ht2[bj])], inc=s_ld, amt=16)
            ld_ht = p.now(s_ld)
            for s in range(32):
                bs = s % 2
                for c in range(DC):
                    E("tensor", lambda t, s=s, c=c, bs=bs, bj=bj:
                      t.matmul(pm1[bs][:, 0:256],
                               w1s[:, c * 4096 + s * 128: c * 4096 + (s + 1) * 128],
                               ht2[bj][:, c * 256:(c + 1) * 256],
                               start=(c == 0), stop=(c == DC - 1)),
                      waits=[(s_wt, wt_e), (s_ld, ld_ht), (s_sc, rel_pm1[bs])],
                      inc=s_pe if c == DC - 1 else None)
                pe_mm1 = p.now(s_pe)
                E("scalar", lambda sc_, s=s, bs=bs: sc_.activation(
                    gt[:, s * 256:(s + 1) * 256], pm1[bs][:, 0:256], AF.Gelu,
                    bias=b1s[:, s:s + 1]),
                  waits=[(s_pe, pe_mm1), (s_wt, wt_e), (s_pe, rel_gt)], inc=s_sc)
                rel_pm1[bs] = p.now(s_sc)
            sc_gelu = p.now(s_sc)
            for ts_ in range(2):
                for co in range(2):
                    for s in range(32):
                        E("tensor", lambda t, s=s, ts_=ts_, co=co:
                          t.matmul(pm2[co][:],
                                   gt[:, s * 256 + ts_ * 128: s * 256 + (ts_ + 1) * 128],
                                   w2s[:, s * D + co * 512: s * D + (co + 1) * 512],
                                   start=(s == 0), stop=False),
                          waits=[(s_sc, sc_gelu), (s_wt, wt_e), (s_v, rel_pm2[co])],
                          inc=None)
                    E("tensor", lambda t, ts_=ts_, co=co:
                      t.matmul(pm2[co][:], onesb[:, 0:128],
                               b2s[:, co * 512:(co + 1) * 512],
                               start=False, stop=True),
                      waits=[(s_g, 4)], inc=s_pe)
                    pe_mm2 = p.now(s_pe)
                    gidx = j * 2 + ts_
                    E("vector", lambda v, co=co, gidx=gidx, e=e, bst=(j * 2 + ts_) % 2:
                      v.tensor_scalar(stage[bst][:, co * 512:(co + 1) * 512],
                                      pm2[co][:], cw[:, gidx * 8 + e: gidx * 8 + e + 1],
                                      0.0, ALU.mult, ALU.add),
                      waits=[(s_pe, pe_mm2), (s_v, cw_done_v),
                             (s_st, rel_stage[(j * 2 + ts_) % 2])], inc=s_v)
                    rel_pm2[co] = p.now(s_v)
                v_stage = p.now(s_v)
                bst = (j * 2 + ts_) % 2
                gidx = j * 2 + ts_
                if e == 0:
                    E("gpsimd", lambda g, gidx=gidx, bst=bst: g.dma_start(
                        out=acc_scr.ap()[gidx * 128:(gidx + 1) * 128, :],
                        in_=stage[bst][:]),
                      waits=[(s_v, v_stage)], inc=s_st, amt=16)
                else:
                    E("gpsimd", lambda g, gidx=gidx, bst=bst: g.dma_start(
                        out=acc_scr.ap()[gidx * 128:(gidx + 1) * 128, :],
                        in_=stage[bst][:], accum_op=ALU.add),
                      waits=[(s_v, v_stage), (s_st, prev_acc_st)], inc=s_st, amt=16)
                rel_stage[bst] = p.now(s_st)
            rel_gt = p.now(s_pe)
            rel_ht2[bj] = p.now(s_pe)
        rel_w1s = p.now(s_pe)
        rel_w2s = p.now(s_pe)
        rel_b = p.now(s_sc)
        prev_acc_st = p.now(s_st)

    acc_done = p.now(s_st)

    rel_stage_f = max(rel_stage)
    rel_atile = [rel_pbc]
    rel_obuf = [0]
    for i in range(NT):
        b = i % 2
        E("sync", lambda y, i=i, b=b: y.dma_start(
            out=rbuf[b][:], in_=res3.ap()[i * 128:(i + 1) * 128, :]),
          waits=[(s_v, rel_rbuf[b])], inc=s_ld, amt=16)
        E("sync", lambda y, i=i: y.dma_start(
            out=atile[0][:], in_=acc_scr.ap()[i * 128:(i + 1) * 128, :]),
          waits=[(s_st, acc_done), (s_v, rel_atile[0]), (s_st, rel_stage_f)], inc=s_ld, amt=16)
        ldc = p.now(s_ld)
        sc = emit_ln_stats(rbuf[b], epsc[:, 0:1], 1, 7, [(s_ld, ldc)], [])
        E("vector", lambda v, b=b: v.tensor_scalar(
            h32[:], rbuf[b][:], stats[:, 1:2], stats[:, 7:8], ALU.subtract, ALU.mult),
          waits=[(s_sc, sc)], inc=s_v, drain=True)
        E("vector", lambda v: v.tensor_tensor(atile[0][:], h32[:], atile[0][:],
                                                   ALU.add), inc=s_v, drain=True)
        sc2 = emit_ln_stats(atile[0], epsc[:, 1:2], 1, 7, [(s_v, p.now(s_v))], [])
        E("vector", lambda v: v.tensor_scalar(
            scratch[:], atile[0][:], stats[:, 1:2], stats[:, 7:8], ALU.subtract, ALU.mult),
          waits=[(s_sc, sc2)], inc=s_v, drain=True)
        rel_atile[0] = p.now(s_v)
        E("vector", lambda v: v.tensor_tensor(scratch[:], scratch[:], lnwb[:], ALU.mult),
          inc=s_v, drain=True)
        E("vector", lambda v, b=b: v.tensor_tensor(scratch[:], scratch[:], lnbb[:], ALU.add),
          inc=s_v, drain=True)
        E("vector", lambda v, b=b: v.tensor_tensor(obuf[0][:], scratch[:], rbuf[b][:],
                                                   ALU.add),
          waits=[(s_st, rel_obuf[0])], inc=s_v, drain=True)
        rel_rbuf[b] = p.now(s_v)
        vo = p.now(s_v)
        E("gpsimd", lambda g, i=i, b=b: g.dma_start(
            out=out.ap()[i * 128:(i + 1) * 128, :], in_=obuf[0][:]),
          waits=[(s_v, vo)], inc=s_st, amt=16)
        rel_obuf[0] = p.now(s_st)

    if debug:
        vv = p.now(s_v)
        E("gpsimd", lambda g: g.dma_start(out=dbg_lg.ap(), in_=lgall[:]),
          waits=[(s_v, vv)], inc=s_st, amt=16)
        E("gpsimd", lambda g: g.dma_start(out=dbg_cw.ap(), in_=cw[:]),
          waits=[(s_v, vv)], inc=s_st, amt=16)
        E("gpsimd", lambda g: g.dma_start(out=dbg_acc.ap(), in_=acc_scr.ap()),
          waits=[(s_st, acc_done)], inc=s_st, amt=16)
        E("gpsimd", lambda g: g.dma_start(out=dbg_ht.ap(), in_=ht_scr.ap()),
          waits=[(s_st, phaseA_st)], inc=s_st, amt=16)

    fin = p.now(s_st)
    E("gpsimd", lambda g: g.nop(), waits=[(s_st, fin)])

    return p.finalize() if fin_own else None



def prep_launch_d(inputs):
    W1 = np.ascontiguousarray(inputs["L1_ffn_W1"])
    W2 = np.ascontiguousarray(inputs["L1_ffn_W2"])
    d = dict(
        w1t=np.ascontiguousarray(W1.transpose(0, 2, 1)),
        w2t=np.ascontiguousarray(W2.transpose(0, 2, 1)),
        b1c=np.ascontiguousarray(
            np.asarray(inputs["L1_ffn_b1"]).reshape(8, 32, 128).transpose(0, 2, 1)),
        b2r=np.ascontiguousarray(np.asarray(inputs["L1_ffn_b2"]).reshape(8, 1, D)),
        wrt=np.ascontiguousarray(np.asarray(inputs["L1_ffn_Wr"]).T),
        brv=np.asarray(inputs["L1_ffn_br"]).reshape(1, 8),
        lnw=np.asarray(inputs["L1_ffn_ln_w"]).reshape(1, D),
        lnb=np.asarray(inputs["L1_ffn_ln_b"]).reshape(1, D),
        idx8=np.tile(np.arange(8, dtype=np.float32), (128, 1)),
        idx8m8=np.tile(np.arange(8, dtype=np.float32) - 8.0, (128, 1)),
        ident=np.eye(128, dtype=np.float32),
    )
    return {k: np.ascontiguousarray(np.asarray(v, dtype=np.float32)) for k, v in d.items()}


def run_launch_d(res3_full, inputs, debug=False):
    nc = build_launch_d(debug)
    com = prep_launch_d(inputs)
    in_maps = []
    for b in range(8):
        m = dict(com)
        m["res3"] = np.ascontiguousarray(res3_full[b])
        in_maps.append(m)
    res = run_bass_kernel_spmd(nc, in_maps, core_ids=list(range(8)))
    if debug:
        return res.results
    return np.stack([r["out"] for r in res.results])



def build_launch_c(ctx=None):
    nc, p, DT, sb, psum, fin_own = _ctx_or_new(ctx, "c")


    res2 = DT("res2", [T, D], F32, "ExternalInput")
    ln1g = DT("ln1g", [1, D], F32, "ExternalInput")
    ln1b = DT("ln1b", [1, D], F32, "ExternalInput")
    ln2g = DT("ln2g", [1, D], F32, "ExternalInput")
    ln2b = DT("ln2b", [1, D], F32, "ExternalInput")
    tmcD = DT("tmc", [128, 8], F32, "ExternalInput")
    cmcD = DT("cmc", [128, 8], F32, "ExternalInput")
    ftmcD = DT("ftmc", [128, 8], F32, "ExternalInput")
    fcmcD = DT("fcmc", [128, 8], F32, "ExternalInput")
    tdcD = DT("tdc", [128, 8], F32, "ExternalInput")
    tfcD = DT("tfc", [128, 8], F32, "ExternalInput")
    mloD = DT("mlo", [128, 8], F32, "ExternalInput")
    wktD = DT("wkt", [D, D], F32, "ExternalInput")
    wvtD = DT("wvt", [D, D], F32, "ExternalInput")
    wrtD = DT("wrt", [D, D], F32, "ExternalInput")
    wotD = DT("wot", [D, D], F32, "ExternalInput")
    fktD = DT("fkt", [D, 4096], F32, "ExternalInput")
    fvtD = DT("fvt", [4096, D], F32, "ExternalInput")
    frtD = DT("frt", [D, D], F32, "ExternalInput")
    ident = DT("ident", [128, 128], F32, "ExternalInput")
    out = DT("out", [T, D], F32, "ExternalOutput")

    x1_scr = DT("x1_scr", [T, D], F32, "Internal")
    x1t_scr = DT("x1t_scr", [D, T + 2], BF16, "Internal")
    x3_scr = DT("x3_scr", [T, D], F32, "Internal")
    x3t_scr = DT("x3t_scr", [D, T + 2], BF16, "Internal")
    xm2t_scr = DT("xm2t_scr", [D, T], BF16, "Internal")
    kv_scr = DT("kv_scr", [T, D], F32, "Internal")

    rbuf = [sb(f"rbuf{i}", [128, D], F32) for i in range(2)]
    x1til = [sb(f"x1til{i}", [128, D], F32) for i in range(2)]
    scratch = sb("scratch", [128, D], F32)
    h32 = sb("h32", [128, D], F32)
    x3w = sb("x3w", [128, D], F32)
    hbf = sb("hbf", [128, D], BF16)
    hts = sb("hts", [128, D], BF16)
    obuf = sb("obuf", [128, D], F32)
    stats = sb("stats", [128, 16], F32)
    epsc = sb("epsc", [128, 1], F32)
    lnbc = [sb(f"lnbc{i}", [128, D], F32) for i in range(4)]
    onesf = sb("onesf", [1, 128], F32)
    ones512 = sb("ones512", [128, 512], F32)
    identb = sb("identb", [128, 128], BF16)
    zcolb = sb("zcolb", [128, 8], BF16)
    cols = sb("cols", [128, 8 * 12], F32)
    def C(i):
        return cols[:, i * 8:(i + 1) * 8]
    wk_sb = sb("wk_sb", [128, 8 * D], BF16)
    wv_sb = sb("wv_sb", [128, 8 * D], BF16)
    wr_sb = sb("wr_sb", [128, 8 * D], BF16)
    wo_sb = sb("wo_sb", [128, 8 * D], BF16)
    fkb, fvb, frt_sb = wk_sb, wv_sb, wr_sb
    x1t_c = sb("x1t_c", [128, 8 * 514], BF16)
    xm_c = sb("xm_c", [128, 8 * 512], BF16)
    P_c = sb("P_c", [128, 8 * 512], BF16)
    gk = sb("gk", [128, 512], F32)
    gv = sb("gv", [128, 512], F32)
    gkv = sb("gkv", [128, 512], F32)
    gy = sb("gy", [128, 512], F32)
    gsr = sb("gsr", [128, 512], F32)
    gt1 = sb("gt1", [128, 512], F32)
    gt2 = sb("gt2", [128, 512], F32)
    edb = sb("edb", [128, 512], F32)
    Abuf = sb("Abuf", [128, 513], F32)
    Bbuf = sb("Bbuf", [128, 513], F32)
    carryA = sb("carryA", [128, 8], F32)
    carryB = sb("carryB", [128, 8], F32)
    xs = [sb(f"xs{i}", [128, 8 * 256], BF16) for i in range(2)]
    k2sq = sb("k2sq", [128, 8 * 256], BF16)
    sq_in = sb("sq_in", [128, 256], BF16)
    kstage = [sb(f"kstage{i}", [128, D], F32) for i in range(2)]
    srt = sb("srt", [128, D], F32)

    pmA = psum("pmA", [128, 512], F32)
    pmB = psum("pmB", [128, 512], F32)
    pmC = psum("pmC", [128, 512], F32)
    pmD = psum("pmD", [128, 512], F32)
    ptr = psum("ptr", [128, 1024], BF16)
    pbc = psum("pbc", [128, 512], F32)

    s_ld = p.sem("s_ld")
    s_wt = p.sem("s_wt")
    s_pe = p.sem("s_pe", owner="tensor")
    s_v = p.sem("s_v", owner="vector")
    s_sc = p.sem("s_sc", owner="scalar")
    s_st = p.sem("s_st")
    s_g = p.sem("s_g", owner="gpsimd")

    E = p.emit
    LNI = 1.0 / D

    E("gpsimd", lambda g: g.memset(onesf[:], 1.0), inc=s_g)
    E("gpsimd", lambda g: g.memset(ones512[:], 1.0), inc=s_g)
    E("gpsimd", lambda g: g.memset(epsc[:], 1e-5), inc=s_g)
    E("gpsimd", lambda g: g.memset(carryA[:], 0.0), inc=s_g)
    E("gpsimd", lambda g: g.memset(carryB[:], 0.0), inc=s_g)
    E("gpsimd", lambda g: g.memset(zcolb[:], 0.0), inc=s_g)
    NGS = 6
    E("gpsimd", lambda g: g.dma_start(out=identb[:], in_=ident.ap()), inc=s_wt, amt=16)
    for (dst, src) in ((C(0), tmcD), (C(1), cmcD), (C(2), ftmcD), (C(3), fcmcD),
                       (C(4), tdcD), (C(5), tfcD), (C(6), mloD)):
        E("sync", lambda y, dst=dst, src=src: y.dma_start(out=dst, in_=src.ap()),
          inc=s_ld, amt=16)
    rowbuf = scratch[0:1, :]
    for i, src in enumerate((ln1g, ln1b, ln2g, ln2b)):
        E("sync", lambda y, src=src: y.dma_start(out=rowbuf, in_=src.ap()),
          waits=[(s_v, p.now(s_v))], inc=s_ld, amt=16)
        ldc = p.now(s_ld)
        for c0 in range(0, D, 512):
            E("tensor", lambda t, c0=c0: t.matmul(
                pbc[:], onesf[:], rowbuf[0:1, c0:c0 + 512], start=True, stop=True),
              waits=[(s_g, 1), (s_ld, ldc), (s_v, p.now(s_v))], inc=s_pe)
            E("vector", lambda v, i=i, c0=c0: v.tensor_copy(
                lnbc[i][:, c0:c0 + 512], pbc[:]),
              waits=[(s_pe, p.now(s_pe))], inc=s_v, drain=True)
    for scr in (x1t_scr, x3t_scr):
        for col in (0, T + 1):
            E("gpsimd", lambda g, scr=scr, col=col: g.dma_start(
                out=chunkmajor_ap(scr, D, T + 2, 1, col0=col), in_=zcolb[:],
                allow_slow_non_contiguous=True),
              waits=[(s_g, NGS)], inc=s_st, amt=16)
    ld7 = p.now(s_ld)
    E("scalar", lambda s: s.activation(C(11), C(4), AF.Exp),
      waits=[(s_ld, ld7)], inc=s_sc, drain=True)
    E("scalar", lambda s: s.activation(C(7), C(11), AF.Exp, scale=-1.0),
      inc=s_sc, drain=True)
    E("scalar", lambda s: s.activation(C(8), C(5), AF.Exp), inc=s_sc, drain=True)
    sc0 = p.now(s_sc)
    E("vector", lambda v: v.tensor_scalar(C(9), C(0), -1.0, 1.0, ALU.mult, ALU.add),
      waits=[(s_ld, ld7), (s_sc, sc0)], inc=s_v, drain=True)
    E("vector", lambda v: v.tensor_tensor(C(11), C(1), C(6), ALU.mult), inc=s_v, drain=True)
    E("vector", lambda v: v.tensor_tensor(C(9), C(9), C(11), ALU.add), inc=s_v, drain=True)
    E("vector", lambda v: v.tensor_scalar(C(11), C(6), -1.0, 1.0, ALU.mult, ALU.add),
      inc=s_v, drain=True)
    E("vector", lambda v: v.tensor_tensor(C(10), C(1), C(11), ALU.mult), inc=s_v, drain=True)
    for (dst, src) in ((wk_sb, wktD), (wv_sb, wvtD), (wr_sb, wrtD), (wo_sb, wotD)):
        E("gpsimd", lambda g, dst=dst, src=src: g.dma_start(
            out=dst[:], in_=chunkmajor_ap(src, D, D, D)), inc=s_wt, amt=16)
    wt_tm = p.now(s_wt)

    def ln_stats(src, eps_ap, wv):
        E("vector", lambda v: v.tensor_reduce(stats[:, 0:1], src[:], AX.X, ALU.add),
          waits=wv, inc=s_v, drain=True)
        E("vector", lambda v: v.tensor_tensor(scratch[:], src[:], src[:], ALU.mult),
          inc=s_v, drain=True)
        E("vector", lambda v: v.tensor_reduce(stats[:, 2:3], scratch[:], AX.X, ALU.add),
          inc=s_v, drain=True)
        E("vector", lambda v: v.tensor_scalar(stats[:, 1:2], stats[:, 0:1],
                                              LNI, 0.0, ALU.mult, ALU.add),
          inc=s_v, drain=True)
        E("vector", lambda v: v.tensor_scalar(stats[:, 3:4], stats[:, 2:3],
                                              LNI, 0.0, ALU.mult, ALU.add),
          inc=s_v, drain=True)
        E("vector", lambda v: v.tensor_tensor(stats[:, 4:5], stats[:, 1:2],
                                              stats[:, 1:2], ALU.mult), inc=s_v, drain=True)
        E("vector", lambda v: v.tensor_tensor(stats[:, 5:6], stats[:, 3:4],
                                              stats[:, 4:5], ALU.subtract),
          inc=s_v, drain=True)
        vc = p.now(s_v)
        E("scalar", lambda s: s.activation(stats[:, 6:7], stats[:, 5:6], AF.Ln,
                                           bias=epsc[:, 0:1]),
          waits=[(s_v, vc), (s_g, NGS)], inc=s_sc, drain=True)
        E("scalar", lambda s: s.activation(stats[:, 7:8], stats[:, 6:7],
                                           AF.Exp, scale=-0.5), inc=s_sc, drain=True)
        return p.now(s_sc)

    rel_ptr = 0
    rel_hts_pe = 0
    need_hts = 0

    def transpose_store(srcbf, scr, col0, extra_pe_wait=0):
        nonlocal rel_ptr, need_hts
        vsrc = p.now(s_v)
        for c in range(DC):
            E("tensor", lambda t, c=c, srcbf=srcbf: t.transpose(
                ptr[:, c * 128:(c + 1) * 128], srcbf[:, c * 128:(c + 1) * 128], identb[:]),
              waits=[(s_v, vsrc), (s_wt, 16), (s_v, rel_ptr)],
              inc=s_pe if c == DC - 1 else None)
        pe_tr = p.now(s_pe)
        E("vector", lambda v: v.tensor_copy(hts[:], ptr[:]),
          waits=[(s_pe, pe_tr), (s_st, need_hts)], inc=s_v, drain=True)
        rel_ptr = p.now(s_v)
        E("gpsimd", lambda g, scr=scr, col0=col0: g.dma_start(
            out=chunkmajor_ap(scr, D, T + 2, 128, col0=col0), in_=hts[:]),
          waits=[(s_v, rel_ptr)], inc=s_st, amt=16)
        need_hts = p.now(s_st)

    rel_rbuf = [0, 0]
    rel_x1til = [p.now(s_st), p.now(s_st)]
    for i in range(NT):
        b = i % 2
        E("sync", lambda y, i=i, b=b: y.dma_start(
            out=rbuf[b][:], in_=res2.ap()[i * 128:(i + 1) * 128, :]),
          waits=[(s_v, rel_rbuf[b])], inc=s_ld, amt=16)
        ldc = p.now(s_ld)
        sc = ln_stats(rbuf[b], epsc[:, 0:1], [(s_ld, ldc)])
        E("vector", lambda v, b=b: v.tensor_scalar(
            h32[:], rbuf[b][:], stats[:, 1:2], stats[:, 7:8], ALU.subtract, ALU.mult),
          waits=[(s_sc, sc)], inc=s_v, drain=True)
        rel_rbuf[b] = p.now(s_v)
        sc = ln_stats(h32, epsc[:, 0:1], [])
        E("vector", lambda v: v.tensor_scalar(
            x3w[:], h32[:], stats[:, 1:2], stats[:, 7:8], ALU.subtract, ALU.mult),
          waits=[(s_sc, sc)], inc=s_v, drain=True)
        E("vector", lambda v: v.tensor_tensor(x3w[:], x3w[:], lnbc[0][:], ALU.mult),
          inc=s_v, drain=True)
        E("vector", lambda v, b=b: v.tensor_tensor(x1til[b][:], x3w[:], lnbc[1][:],
                                                   ALU.add),
          waits=[(s_st, rel_x1til[b])], inc=s_v, drain=True)
        vx1 = p.now(s_v)
        E("gpsimd", lambda g, i=i, b=b: g.dma_start(
            out=x1_scr.ap()[i * 128:(i + 1) * 128, :], in_=x1til[b][:]),
          waits=[(s_v, vx1)], inc=s_st, amt=16)
        rel_x1til[b] = p.now(s_st)
        E("vector", lambda v, b=b: v.tensor_copy(hbf[:], x1til[b][:]), inc=s_v, drain=True)
        transpose_store(hbf, x1t_scr, 1 + i * 128)
    p1_st = p.now(s_st)

    rel_x1tc = 0
    rel_xm = 0
    rel_Pc = 0
    rel_pmA_sc = 0
    rel_pmA_v = 0
    rel_pmB_v = 0
    rel_pmC_v = 0

    def ln_aff2_store(x2src, tile_idx, b):
        sc = ln_stats(x2src, epsc[:, 0:1], [])
        E("vector", lambda v: v.tensor_scalar(
            x3w[:], x2src[:], stats[:, 1:2], stats[:, 7:8], ALU.subtract, ALU.mult),
          waits=[(s_sc, sc)], inc=s_v, drain=True)
        E("vector", lambda v: v.tensor_tensor(x3w[:], x3w[:], lnbc[2][:], ALU.mult),
          inc=s_v, drain=True)
        E("vector", lambda v, b=b: v.tensor_tensor(x1til[b][:], x3w[:], lnbc[3][:],
                                                   ALU.add),
          waits=[(s_st, rel_x1til[b])], inc=s_v, drain=True)
        vx3 = p.now(s_v)
        E("gpsimd", lambda g, tile_idx=tile_idx, b=b: g.dma_start(
            out=x3_scr.ap()[tile_idx * 128:(tile_idx + 1) * 128, :], in_=x1til[b][:]),
          waits=[(s_v, vx3)], inc=s_st, amt=16)
        E("vector", lambda v, b=b: v.tensor_copy(hbf[:], x1til[b][:]), inc=s_v, drain=True)
        transpose_store(hbf, x3t_scr, 1 + tile_idx * 128)

    for c in range(8):
        E("sync", lambda y, c=c: y.dma_start(
            out=x1t_c[:], in_=chunkmajor_ap(x1t_scr, D, T + 2, 514, col0=c * 512)),
          waits=[(s_st, p1_st if c == 0 else need_hts), (s_v, rel_x1tc),
                 (s_pe, rel_xm)],
          inc=s_ld, amt=16)
        ld_x1t = p.now(s_ld)
        for g_ in range(8):
            base = g_ * 514
            E("vector", lambda v, g_=g_, base=base: v.tensor_scalar(
                gt1[:], x1t_c[:, base + 1: base + 513], C(0)[:, g_:g_ + 1], 0.0,
                ALU.mult, ALU.add),
              waits=[(s_ld, ld_x1t), (s_pe, rel_xm)], inc=s_v, drain=True)
            E("vector", lambda v, g_=g_, base=base: v.scalar_tensor_tensor(
                gt1[:], x1t_c[:, base: base + 512], C(9)[:, g_:g_ + 1], gt1[:],
                ALU.mult, ALU.add), inc=s_v, drain=True)
            E("vector", lambda v, g_=g_, base=base: v.scalar_tensor_tensor(
                xm_c[:, g_ * 512:(g_ + 1) * 512], x1t_c[:, base + 2: base + 514],
                C(10)[:, g_:g_ + 1], gt1[:], ALU.mult, ALU.add), inc=s_v, drain=True)
        rel_x1tc = p.now(s_v)
        v_xm = p.now(s_v)
        for g_ in range(8):
            for cc in range(DC):
                E("tensor", lambda t, g_=g_, cc=cc: t.matmul(
                    pmA[:], wk_sb[:, cc * D + g_ * 128: cc * D + g_ * 128 + 128],
                    xm_c[:, cc * 512:(cc + 1) * 512],
                    start=(cc == 0), stop=(cc == DC - 1)),
                  waits=[(s_wt, wt_tm), (s_v, v_xm), (s_v, rel_pmA_v),
                         (s_sc, rel_pmA_sc)],
                  inc=s_pe if cc == DC - 1 else None)
            pe_k = p.now(s_pe)
            E("vector", lambda v: v.tensor_scalar(gt1[:], pmA[:], 60.0, 0.0,
                                                  ALU.min, ALU.add),
              waits=[(s_pe, pe_k)], inc=s_v, drain=True)
            rel_pmA_v = p.now(s_v)
            E("scalar", lambda s: s.activation(gk[:], gt1[:], AF.Exp),
              waits=[(s_v, rel_pmA_v)], inc=s_sc, drain=True)
            sc_k = p.now(s_sc)
            for cc in range(DC):
                E("tensor", lambda t, g_=g_, cc=cc: t.matmul(
                    pmB[:], wv_sb[:, cc * D + g_ * 128: cc * D + g_ * 128 + 128],
                    xm_c[:, cc * 512:(cc + 1) * 512],
                    start=(cc == 0), stop=(cc == DC - 1)),
                  waits=[(s_v, rel_pmB_v)], inc=s_pe if cc == DC - 1 else None)
            pe_v = p.now(s_pe)
            E("vector", lambda v: v.tensor_copy(gv[:], pmB[:]),
              waits=[(s_pe, pe_v)], inc=s_v, drain=True)
            rel_pmB_v = p.now(s_v)
            for cc in range(DC):
                E("tensor", lambda t, g_=g_, cc=cc: t.matmul(
                    pmC[:], wr_sb[:, cc * D + g_ * 128: cc * D + g_ * 128 + 128],
                    xm_c[:, cc * 512:(cc + 1) * 512],
                    start=(cc == 0), stop=(cc == DC - 1)),
                  waits=[(s_v, rel_pmC_v)], inc=s_pe if cc == DC - 1 else None)
            pe_r = p.now(s_pe)
            E("scalar", lambda s: s.activation(gt2[:], pmC[:], AF.Exp, scale=-1.0),
              waits=[(s_pe, pe_r), (s_sc, sc_k), (s_v, p.now(s_v))],
              inc=s_sc, drain=True)
            sc_r = p.now(s_sc)
            E("vector", lambda v: v.tensor_scalar(gt2[:], gt2[:], 1.0, 0.0,
                                                  ALU.add, ALU.add),
              waits=[(s_sc, sc_r)], inc=s_v, drain=True)
            rel_pmC_v = p.now(s_v)
            E("vector", lambda v: v.reciprocal(gsr[:], gt2[:]), inc=s_v, drain=True)
            E("vector", lambda v: v.tensor_tensor(gkv[:], gk[:], gv[:], ALU.mult),
              waits=[(s_sc, sc_k)], inc=s_v, drain=True)
            E("vector", lambda v, g_=g_: v.tensor_copy(Abuf[:, 0:1], carryA[:, g_:g_ + 1]),
              inc=s_v, drain=True)
            E("vector", lambda v, g_=g_: v.tensor_copy(Bbuf[:, 0:1], carryB[:, g_:g_ + 1]),
              inc=s_v, drain=True)
            E("vector", lambda v, g_=g_: v.tensor_scalar(
                edb[:], ones512[:], C(7)[:, g_:g_ + 1], 0.0, ALU.mult, ALU.add),
              waits=[(s_g, 2)], inc=s_v, drain=True)
            E("vector", lambda v, g_=g_: v.tensor_tensor_scan(
                Abuf[:, 1:513], edb[:], gkv[:], carryA[:, g_:g_ + 1],
                ALU.mult, ALU.add), inc=s_v, drain=True)
            E("vector", lambda v, g_=g_: v.tensor_tensor_scan(
                Bbuf[:, 1:513], edb[:], gk[:], carryB[:, g_:g_ + 1],
                ALU.mult, ALU.add), inc=s_v, drain=True)
            E("vector", lambda v, g_=g_: v.tensor_copy(carryA[:, g_:g_ + 1],
                                                       Abuf[:, 512:513]),
              inc=s_v, drain=True)
            E("vector", lambda v, g_=g_: v.tensor_copy(carryB[:, g_:g_ + 1],
                                                       Bbuf[:, 512:513]),
              inc=s_v, drain=True)
            E("vector", lambda v, g_=g_: v.scalar_tensor_tensor(
                gt1[:], gkv[:], C(8)[:, g_:g_ + 1], Abuf[:, 0:512], ALU.mult, ALU.add),
              inc=s_v, drain=True)
            E("vector", lambda v, g_=g_: v.scalar_tensor_tensor(
                gt2[:], gk[:], C(8)[:, g_:g_ + 1], Bbuf[:, 0:512], ALU.mult, ALU.add),
              inc=s_v, drain=True)
            E("vector", lambda v: v.reciprocal(gt2[:], gt2[:]), inc=s_v, drain=True)
            E("vector", lambda v: v.tensor_tensor(gy[:], gt1[:], gt2[:], ALU.mult),
              inc=s_v, drain=True)
            E("vector", lambda v, g_=g_: v.tensor_tensor(
                P_c[:, g_ * 512:(g_ + 1) * 512], gy[:], gsr[:], ALU.mult),
              waits=[(s_pe, rel_Pc)], inc=s_v, drain=True)
        v_P = p.now(s_v)
        for ts4 in range(4):
            ti = c * 4 + ts4
            b = ts4 % 2
            E("sync", lambda y, ti=ti, b=b: y.dma_start(
                out=rbuf[b][:], in_=x1_scr.ap()[ti * 128:(ti + 1) * 128, :]),
              waits=[(s_st, p1_st), (s_v, rel_rbuf[b])], inc=s_ld, amt=16)
            ld_x1 = p.now(s_ld)
            for co in range(2):
                for cc in range(DC):
                    E("tensor", lambda t, ts4=ts4, co=co, cc=cc: t.matmul(
                        pmD[:], P_c[:, cc * 512 + ts4 * 128: cc * 512 + ts4 * 128 + 128],
                        wo_sb[:, cc * D + co * 512: cc * D + co * 512 + 512],
                        start=(cc == 0), stop=(cc == DC - 1)),
                      waits=[(s_v, v_P), (s_v, rel_pmA_v)],
                      inc=s_pe if cc == DC - 1 else None)
                pe_wo = p.now(s_pe)
                E("vector", lambda v, co=co, b=b: v.tensor_tensor(
                    h32[:, co * 512:(co + 1) * 512], pmD[:],
                    rbuf[b][:, co * 512:(co + 1) * 512], ALU.add),
                  waits=[(s_pe, pe_wo), (s_ld, ld_x1)], inc=s_v, drain=True)
                rel_pmA_v = p.now(s_v)
            rel_rbuf[b] = p.now(s_v)
            ln_aff2_store(h32, ti, b)
        rel_Pc = p.now(s_pe)
        rel_xm = p.now(s_pe)
    p2_st = p.now(s_st)
    p2_pe = p.now(s_pe)

    vnow = p.now(s_v)
    E("vector", lambda v: v.tensor_scalar(C(9), C(2), -1.0, 1.0, ALU.mult, ALU.add),
      waits=[(s_v, vnow)], inc=s_v, drain=True)
    E("vector", lambda v: v.tensor_tensor(C(11), C(3), C(6), ALU.mult), inc=s_v, drain=True)
    E("vector", lambda v: v.tensor_tensor(C(9), C(9), C(11), ALU.add), inc=s_v, drain=True)
    E("vector", lambda v: v.tensor_scalar(C(11), C(6), -1.0, 1.0, ALU.mult, ALU.add),
      inc=s_v, drain=True)
    E("vector", lambda v: v.tensor_tensor(C(10), C(3), C(11), ALU.mult), inc=s_v, drain=True)
    rel_xmst = 0
    for c in range(8):
        E("sync", lambda y, c=c: y.dma_start(
            out=x1t_c[:], in_=chunkmajor_ap(x3t_scr, D, T + 2, 514, col0=c * 512)),
          waits=[(s_st, p2_st), (s_v, rel_x1tc)], inc=s_ld, amt=16)
        ld_x3t = p.now(s_ld)
        for g_ in range(8):
            base = g_ * 514
            E("vector", lambda v, g_=g_, base=base: v.tensor_scalar(
                gt1[:], x1t_c[:, base + 1: base + 513], C(2)[:, g_:g_ + 1], 0.0,
                ALU.mult, ALU.add),
              waits=[(s_ld, ld_x3t), (s_st, rel_xmst)], inc=s_v, drain=True)
            E("vector", lambda v, g_=g_, base=base: v.scalar_tensor_tensor(
                gt1[:], x1t_c[:, base: base + 512], C(9)[:, g_:g_ + 1], gt1[:],
                ALU.mult, ALU.add), inc=s_v, drain=True)
            E("vector", lambda v, g_=g_, base=base: v.scalar_tensor_tensor(
                xm_c[:, g_ * 512:(g_ + 1) * 512], x1t_c[:, base + 2: base + 514],
                C(10)[:, g_:g_ + 1], gt1[:], ALU.mult, ALU.add), inc=s_v, drain=True)
        rel_x1tc = p.now(s_v)
        v_xm2 = p.now(s_v)
        E("gpsimd", lambda g, c=c: g.dma_start(
            out=chunkmajor_ap(xm2t_scr, D, T, 512, col0=c * 512), in_=xm_c[:]),
          waits=[(s_v, v_xm2)], inc=s_st, amt=16)
        rel_xmst = p.now(s_st)
    p3a_st = p.now(s_st)

    rel_xs = [0, 0]
    rel_k2 = 0
    rel_kst = [p3a_st, p3a_st]
    rel_pm1sc = 0
    prev_kv_st = 0
    for blk in range(4):
        E("gpsimd", lambda g, blk=blk: g.dma_start(
            out=fkb[:], in_=chunkmajor_ap(fktD, D, 4096, 1024, col0=blk * 1024)),
          waits=[(s_pe, p2_pe if blk == 0 else p.now(s_pe))], inc=s_wt, amt=16)
        E("gpsimd", lambda g, blk=blk: g.dma_start(
            out=fvb[:], in_=chunkmajor_ap(fvtD, 4096, D, D, col0=blk * 1024 * D,
                                          nrow_chunks=8)),
          waits=[], inc=s_wt, amt=16)
        wt_cm = p.now(s_wt)
        for ti in range(NT // 2):
            b2 = ti % 2
            E("sync", lambda y, ti=ti, b2=b2: y.dma_start(
                out=xs[b2][:], in_=chunkmajor_ap(xm2t_scr, D, T, 256, col0=ti * 256)),
              waits=[(s_st, p3a_st), (s_pe, rel_xs[b2])], inc=s_ld, amt=16)
            ld_xs = p.now(s_ld)
            for s_ in range(8):
                for cc in range(DC):
                    E("tensor", lambda t, s_=s_, cc=cc, b2=b2: t.matmul(
                        pmA[:, 0:256],
                        fkb[:, cc * 1024 + s_ * 128: cc * 1024 + s_ * 128 + 128],
                        xs[b2][:, cc * 256:(cc + 1) * 256],
                        start=(cc == 0), stop=(cc == DC - 1)),
                      waits=[(s_wt, wt_cm), (s_ld, ld_xs), (s_v, rel_pmA_v),
                             (s_sc, rel_pm1sc)],
                      inc=s_pe if cc == DC - 1 else None)
                pe1 = p.now(s_pe)
                E("vector", lambda v: v.tensor_scalar(sq_in[:], pmA[:, 0:256], 0.0, 0.0,
                                                      ALU.max, ALU.add),
                  waits=[(s_pe, pe1), (s_sc, rel_pm1sc)], inc=s_v, drain=True)
                rel_pmA_v = p.now(s_v)
                E("scalar", lambda s, s_=s_: s.activation(
                    k2sq[:, s_ * 256:(s_ + 1) * 256], sq_in[:], AF.Square),
                  waits=[(s_v, rel_pmA_v), (s_pe, rel_k2)], inc=s_sc, drain=True)
                rel_pm1sc = p.now(s_sc)
            sc_k2 = p.now(s_sc)
            for tsub in range(2):
                gi = ti * 2 + tsub
                for co in range(2):
                    for s_ in range(8):
                        E("tensor", lambda t, s_=s_, co=co, tsub=tsub: t.matmul(
                            pmB[:],
                            k2sq[:, s_ * 256 + tsub * 128: s_ * 256 + tsub * 128 + 128],
                            fvb[:, s_ * D + co * 512: s_ * D + co * 512 + 512],
                            start=(s_ == 0), stop=(s_ == 7)),
                          waits=[(s_sc, sc_k2), (s_v, rel_pmB_v)],
                          inc=s_pe if s_ == 7 else None)
                    pe2 = p.now(s_pe)
                    E("vector", lambda v, co=co, tsub=tsub: v.tensor_copy(
                        kstage[tsub][:, co * 512:(co + 1) * 512], pmB[:]),
                      waits=[(s_pe, pe2), (s_st, rel_kst[tsub])], inc=s_v, drain=True)
                    rel_pmB_v = p.now(s_v)
                v_kst = p.now(s_v)
                if blk == 0:
                    E("gpsimd", lambda g, gi=gi, tsub=tsub: g.dma_start(
                        out=kv_scr.ap()[gi * 128:(gi + 1) * 128, :], in_=kstage[tsub][:]),
                      waits=[(s_v, v_kst)], inc=s_st, amt=16)
                else:
                    E("gpsimd", lambda g, gi=gi, tsub=tsub: g.dma_start(
                        out=kv_scr.ap()[gi * 128:(gi + 1) * 128, :], in_=kstage[tsub][:],
                        accum_op=ALU.add),
                      waits=[(s_v, v_kst), (s_st, prev_kv_st)], inc=s_st, amt=16)
                rel_kst[tsub] = p.now(s_st)
            rel_k2 = p.now(s_pe)
            rel_xs[b2] = p.now(s_pe)
        prev_kv_st = p.now(s_st)
    kv_done = p.now(s_st)

    E("gpsimd", lambda g: g.dma_start(
        out=frt_sb[:], in_=chunkmajor_ap(frtD, D, D, D)),
      waits=[(s_pe, p.now(s_pe))], inc=s_wt, amt=16)
    wt_fr = p.now(s_wt)
    rel_obuf = 0
    rel_srv = 0
    for ti in range(NT):
        b2 = ti % 2
        b = ti % 2
        E("sync", lambda y, ti=ti, b2=b2: y.dma_start(
            out=xs[b2][:, 0:1024], in_=chunkmajor_ap(xm2t_scr, D, T, 128,
                                                     col0=ti * 128)),
          waits=[(s_st, p3a_st), (s_pe, rel_xs[b2])], inc=s_ld, amt=16)
        E("sync", lambda y, ti=ti, b=b: y.dma_start(
            out=rbuf[b][:], in_=x3_scr.ap()[ti * 128:(ti + 1) * 128, :]),
          waits=[(s_v, rel_rbuf[b])], inc=s_ld, amt=16)
        E("sync", lambda y, ti=ti, b=b: y.dma_start(
            out=x1til[b][:], in_=res2.ap()[ti * 128:(ti + 1) * 128, :]),
          waits=[(s_v, rel_x1til[b] if isinstance(rel_x1til[b], int) else 0)],
          inc=s_ld, amt=16)
        E("sync", lambda y, ti=ti, b=b: y.dma_start(
            out=kstage[b][:], in_=kv_scr.ap()[ti * 128:(ti + 1) * 128, :]),
          waits=[(s_st, kv_done), (s_v, rel_srv)], inc=s_ld, amt=16)
        ld3 = p.now(s_ld)
        for co in range(2):
            for cc in range(DC):
                E("tensor", lambda t, co=co, cc=cc, b2=b2: t.matmul(
                    pmC[:],
                    xs[b2][:, cc * 128:(cc + 1) * 128],
                    frt_sb[:, cc * D + co * 512: cc * D + co * 512 + 512],
                    start=(cc == 0), stop=(cc == DC - 1)),
                  waits=[(s_wt, wt_fr), (s_ld, ld3), (s_v, rel_pmC_v)],
                  inc=s_pe if cc == DC - 1 else None)
            pe_fr = p.now(s_pe)
            E("scalar", lambda s, co=co: s.activation(
                srt[:, co * 512:(co + 1) * 512], pmC[:], AF.Exp, scale=-1.0),
              waits=[(s_pe, pe_fr), (s_v, p.now(s_v))], inc=s_sc, drain=True)
            scx = p.now(s_sc)
            E("vector", lambda v, co=co: v.tensor_scalar(
                srt[:, co * 512:(co + 1) * 512], srt[:, co * 512:(co + 1) * 512],
                1.0, 0.0, ALU.add, ALU.add),
              waits=[(s_sc, scx)], inc=s_v, drain=True)
            rel_pmC_v = p.now(s_v)
            E("vector", lambda v, co=co: v.reciprocal(
                srt[:, co * 512:(co + 1) * 512], srt[:, co * 512:(co + 1) * 512]),
              inc=s_v, drain=True)
        rel_xs[b2] = p.now(s_pe)
        E("vector", lambda v, b=b: v.tensor_tensor(srt[:], srt[:], kstage[b][:],
                                                   ALU.mult), inc=s_v, drain=True)
        rel_srv = p.now(s_v)
        E("vector", lambda v, b=b: v.tensor_tensor(srt[:], srt[:], rbuf[b][:], ALU.add),
          inc=s_v, drain=True)
        rel_rbuf[b] = p.now(s_v)
        E("vector", lambda v, b=b: v.tensor_tensor(obuf[:], srt[:], x1til[b][:], ALU.add),
          waits=[(s_st, rel_obuf)], inc=s_v, drain=True)
        rel_x1til[b] = p.now(s_v)
        vo = p.now(s_v)
        E("gpsimd", lambda g, ti=ti: g.dma_start(
            out=out.ap()[ti * 128:(ti + 1) * 128, :], in_=obuf[:]),
          waits=[(s_v, vo)], inc=s_st, amt=16)
        rel_obuf = p.now(s_st)

    fin = p.now(s_st)
    E("gpsimd", lambda g: g.nop(), waits=[(s_st, fin)])
    return p.finalize() if fin_own else None


def prep_launch_c(inputs):
    def colsfmt(vec):
        return np.ascontiguousarray(
            np.asarray(vec, np.float32).reshape(8, 128).T)
    d = dict(
        ln1g=np.asarray(inputs["L1_attn_ln1_g"], np.float32).reshape(1, D),
        ln1b=np.asarray(inputs["L1_attn_ln1_b"], np.float32).reshape(1, D),
        ln2g=np.asarray(inputs["L1_attn_ln2_g"], np.float32).reshape(1, D),
        ln2b=np.asarray(inputs["L1_attn_ln2_b"], np.float32).reshape(1, D),
        tmc=colsfmt(inputs["L1_attn_att_time_mix"]),
        cmc=colsfmt(inputs["L1_attn_att_combined_mix"]),
        ftmc=colsfmt(inputs["L1_attn_ffn_time_mix"]),
        fcmc=colsfmt(inputs["L1_attn_ffn_combined_mix"]),
        tdc=colsfmt(inputs["L1_attn_time_decay"]),
        tfc=colsfmt(inputs["L1_attn_time_first"]),
        mlo=colsfmt((np.arange(1024) < 512).astype(np.float32)),
        wkt=np.ascontiguousarray(np.asarray(inputs["L1_attn_Wk"], np.float32).T),
        wvt=np.ascontiguousarray(np.asarray(inputs["L1_attn_Wv"], np.float32).T),
        wrt=np.ascontiguousarray(np.asarray(inputs["L1_attn_Wr"], np.float32).T),
        wot=np.ascontiguousarray(np.asarray(inputs["L1_attn_Wo"], np.float32).T),
        fkt=np.ascontiguousarray(np.asarray(inputs["L1_attn_Fk"], np.float32).T),
        fvt=np.ascontiguousarray(np.asarray(inputs["L1_attn_Fv"], np.float32).T),
        frt=np.ascontiguousarray(np.asarray(inputs["L1_attn_Fr"], np.float32).T),
        ident=np.eye(128, dtype=np.float32),
    )
    return d


def run_launch_c(res2_full, inputs):
    nc = build_launch_c()
    com = prep_launch_c(inputs)
    in_maps = []
    for b in range(8):
        m = dict(com)
        m["res2"] = np.ascontiguousarray(res2_full[b])
        in_maps.append(m)
    res = run_bass_kernel_spmd(nc, in_maps, core_ids=list(range(8)))
    return np.stack([r["out"] for r in res.results])



SCALING = 64.0 ** -0.5

def build_launch_a(stop_after=None, dbg_p=False, ctx=None):
    nc, p, DT, sb, psum, fin_own = _ctx_or_new(ctx, "a")


    xin = DT("xin", [T, D], F32, "ExternalInput")
    wqD = DT("wq", [D, D], F32, "ExternalInput")
    wkD = DT("wk", [D, D], F32, "ExternalInput")
    wvD = DT("wv", [D, D], F32, "ExternalInput")
    woD = DT("wo", [D, D], F32, "ExternalInput")
    bqD = DT("bq", [128, 8], F32, "ExternalInput")
    bkD = DT("bk", [128, 8], F32, "ExternalInput")
    bvD = DT("bv", [1, D], F32, "ExternalInput")
    boD = DT("bo", [1, D], F32, "ExternalInput")
    ohD = DT("oh", [32, 1280], F32, "ExternalInput")
    relD = DT("rel", [32, 16], F32, "ExternalInput")
    gwtD = DT("gwt", [8, 64], F32, "ExternalInput")
    gbtD = DT("gbt", [8, 1], F32, "ExternalInput")
    ob2D = DT("ob2", [8, 2], F32, "ExternalInput")
    gc16D = DT("gc16", [16, 1], F32, "ExternalInput")
    flipD = DT("flip", [128, 128], F32, "ExternalInput")
    ident = DT("ident", [128, 128], F32, "ExternalInput")
    out = DT("out", [T, D], F32, "ExternalOutput")

    if dbg_p:
        dbg_pfr = DT("dbg_pfr", [128, 512], BF16, "ExternalOutput")
        dbg_pfl = DT("dbg_pfl", [128, 512], BF16, "ExternalOutput")
        dbg_row = DT("dbg_row", [1, 512], F32, "ExternalOutput")
    qt_scr = DT("qt_scr", [D, T], BF16, "Internal")
    kt_scr = DT("kt_scr", [D, T], BF16, "Internal")
    v_scr = DT("v_scr", [T, D], BF16, "Internal")
    at_scr = DT("at_scr", [D, T], BF16, "Internal")
    gate_scr = DT("gate_scr", [16, T], F32, "Internal")
    tblr_hbm = DT("tblr_hbm", [16, 1280], BF16, "Internal")
    csc_hbm = DT("csc_hbm", [16, 2], F32, "Internal")

    rbuf = [sb(f"rbuf{i}", [128, D], F32) for i in range(2)]
    scratch = sb("scratch", [128, D], F32)
    hbf = sb("hbf", [128, D], BF16)
    hts = sb("hts", [128, D], BF16)
    obuf = sb("obuf", [128, D], F32)
    stats = sb("stats", [128, 16], F32)
    epsc = sb("epsc", [128, 1], F32)
    onesf = sb("onesf", [1, 128], F32)
    onesrb = sb("onesrb", [1, 512], BF16)
    identb = sb("identb", [128, 128], BF16)
    flipb = sb("flipb", [128, 128], BF16)
    xlnT = sb("xlnT", [128, 8 * T], BF16)
    wq_sb = sb("wq_sb", [128, 8 * D], BF16)
    wk_sb = sb("wk_sb", [128, 8 * D], BF16)
    wv_sb = sb("wv_sb", [128, 8 * D], BF16)
    wo_sb = wq_sb
    bqc = sb("bqc", [128, 8], F32)
    bkc = sb("bkc", [128, 8], F32)
    bvb = sb("bvb", [128, D], F32)
    bob = sb("bob", [1, D], BF16)
    stage = [sb(f"stage{i}", [128, 512], BF16) for i in range(2)]
    relb = sb("relb", [32, 16], BF16)
    ohb = sb("ohb", [32, 1280], BF16)
    tblf = sb("tblf", [16, 1280], F32)
    tblb = sb("tblb", [16, 1280], BF16)
    gwtb = sb("gwtb", [8, 64], BF16)
    gbtb = sb("gbtb", [8, 1], BF16)
    ob2b = sb("ob2b", [8, 2], BF16)
    gw2 = sb("gw2", [64, 2], F32)
    gbs = sb("gbs", [1, 2], F32)
    wa_sb = sb("wa_sb", [128, 8 * 16], BF16)
    wb_sb = sb("wb_sb", [128, 8 * 16], BF16)
    bsrA = sb("bsrA", [1, 16], F32)
    bsrB = sb("bsrB", [1, 16], F32)
    bsrAb = sb("bsrAb", [1, 16], BF16)
    bsrBb = sb("bsrBb", [1, 16], BF16)
    gc16 = sb("gc16s", [16, 1], F32)
    sga = sb("sga", [16, 512], F32)
    sgb = sb("sgb", [16, 512], F32)
    gt16 = sb("gt16", [16, 512], F32)
    kth = sb("kth", [64, T], BF16)
    vh = sb("vh", [128, 32 * 65], BF16)
    qsl = sb("qsl", [64, 512], BF16)
    grow = sb("grow", [1, T], F32)
    crow = sb("crow", [1, 2], F32)
    rowF = sb("rowF", [1, 512], F32)
    gb_sb = sb("gb_sb", [128, 512], BF16)
    Bt = sb("Bt", [128, 512], BF16)
    Bg = sb("Bg", [128, 512], BF16)
    Pb = [sb(f"Pb{i}", [128, 512], BF16) for i in range(3)]
    denr = sb("denr", [1, 512], F32)
    dr_sb = sb("dr_sb", [64, 512], F32)
    atst = sb("atst", [64, 512], BF16)
    xs2 = [sb(f"xs2{i}", [128, 8 * 128], BF16) for i in range(2)]

    pS = [psum(f"pS{i}", [128, 512], F32) for i in range(2)]
    pO = psum("pO", [65, 512], F32)
    pbc = psum("pbc", [128, 512], F32)
    ptr = psum("ptr", [128, 1024], BF16)
    pG = psum("pG", [16, 512], F32)

    s_ld = p.sem("s_ld")
    s_wt = p.sem("s_wt")
    s_pe = p.sem("s_pe", owner="tensor")
    s_v = p.sem("s_v", owner="vector")
    s_sc = p.sem("s_sc", owner="scalar")
    s_st = p.sem("s_st")
    s_g = p.sem("s_g", owner="gpsimd")

    E = p.emit
    LNI = 1.0 / D

    E("gpsimd", lambda g: g.memset(onesf[:], 1.0), inc=s_g)
    E("gpsimd", lambda g: g.memset(onesrb[:], 1.0), inc=s_g)
    E("gpsimd", lambda g: g.memset(epsc[:], 1e-5), inc=s_g)
    E("gpsimd", lambda g: g.memset(vh[:], 1.0), inc=s_g)
    NGS = 4
    for (dst, src) in ((identb, ident), (flipb, flipD), (relb, relD), (ohb, ohD),
                       (gwtb, gwtD), (gbtb, gbtD), (ob2b, ob2D), (bob, boD)):
        E("gpsimd", lambda g, dst=dst, src=src: g.dma_start(out=dst[:], in_=src.ap()),
          inc=s_wt, amt=16)
    for (dst, src) in ((bqc, bqD), (bkc, bkD), (gc16, gc16D)):
        E("sync", lambda y, dst=dst, src=src: y.dma_start(out=dst[:], in_=src.ap()),
          inc=s_ld, amt=16)
    if stop_after == 's0':
        E("gpsimd", lambda g: g.dma_start(out=out.ap()[0:128, :], in_=rbuf[0][:]),
          waits=[(s_wt, p.now(s_wt)), (s_ld, p.now(s_ld)), (s_g, p.now(s_g))],
          inc=s_st, amt=16)
        fin0 = p.now(s_st)
        E("gpsimd", lambda g: g.nop(), waits=[(s_st, fin0)])
        return p.finalize()
    rowbuf = scratch[0:1, :]
    E("sync", lambda y: y.dma_start(out=rowbuf, in_=bvD.ap()), inc=s_ld, amt=16)
    ld0 = p.now(s_ld)
    if stop_after == 's0a':
        E("gpsimd", lambda g: g.dma_start(out=out.ap()[0:128, :], in_=rbuf[0][:]),
          waits=[(s_ld, ld0), (s_wt, p.now(s_wt))], inc=s_st, amt=16)
        fin0 = p.now(s_st)
        E("gpsimd", lambda g: g.nop(), waits=[(s_st, fin0)])
        return p.finalize()
    for c0 in range(0, D, 512):
        E("tensor", lambda t, c0=c0: t.matmul(
            pbc[:], onesf[:], rowbuf[0:1, c0:c0 + 512], start=True, stop=True),
          waits=[(s_g, 1), (s_ld, ld0), (s_v, p.now(s_v))], inc=s_pe)
        E("vector", lambda v, c0=c0: v.tensor_copy(bvb[:, c0:c0 + 512], pbc[:]),
          waits=[(s_pe, p.now(s_pe))], inc=s_v, drain=True)
    if stop_after == 's1':
        E("gpsimd", lambda g: g.dma_start(out=out.ap()[0:128, :], in_=bvb[:]),
          waits=[(s_v, p.now(s_v))], inc=s_st, amt=16)
        fin0 = p.now(s_st)
        E("gpsimd", lambda g: g.nop(), waits=[(s_st, fin0)])
        return p.finalize()
    wt0 = p.now(s_wt)
    for c0 in range(0, 1280, 512):
        w = min(512, 1280 - c0)
        E("tensor", lambda t, c0=c0, w=w: t.matmul(
            pG[:, 0:w], relb[:], ohb[:, c0:c0 + w], start=True, stop=True),
          waits=[(s_wt, wt0), (s_v, p.now(s_v))], inc=s_pe)
        E("vector", lambda v, c0=c0, w=w: v.tensor_copy(tblf[:, c0:c0 + w], pG[:, 0:w]),
          waits=[(s_pe, p.now(s_pe))], inc=s_v, drain=True)
    E("vector", lambda v: v.tensor_copy(tblb[:], tblf[:]), inc=s_v, drain=True)
    vt = p.now(s_v)
    E("gpsimd", lambda g: g.dma_start(out=tblr_hbm.ap(), in_=tblb[:]),
      waits=[(s_v, vt)], inc=s_st, amt=16)
    E("vector", lambda v: v.tensor_copy(stats[0:16, 8:9], tblf[:, 0:1]), inc=s_v, drain=True)
    E("vector", lambda v: v.tensor_copy(stats[0:16, 9:10], tblf[:, 1279:1280]),
      inc=s_v, drain=True)
    vt = p.now(s_v)
    E("gpsimd", lambda g: g.dma_start(out=csc_hbm.ap(), in_=stats[0:16, 8:10]),
      waits=[(s_v, vt)], inc=s_st, amt=16)
    if stop_after == 's2':
        E("gpsimd", lambda g: g.dma_start(out=out.ap()[0:128, :], in_=bvb[:]),
          waits=[(s_st, p.now(s_st))], inc=s_st, amt=16)
        fin0 = p.now(s_st)
        E("gpsimd", lambda g: g.nop(), waits=[(s_st, fin0)])
        return p.finalize()
    E("tensor", lambda t: t.matmul(pbc[0:64, 0:2], gwtb[:], ob2b[:], start=True,
                                   stop=True),
      waits=[(s_wt, wt0), (s_v, p.now(s_v))], inc=s_pe)
    E("vector", lambda v: v.tensor_copy(gw2[:], pbc[0:64, 0:2]),
      waits=[(s_pe, p.now(s_pe))], inc=s_v, drain=True)
    E("tensor", lambda t: t.matmul(pbc[0:1, 0:2], gbtb[:], ob2b[:], start=True,
                                   stop=True),
      waits=[(s_v, p.now(s_v))], inc=s_pe)
    E("vector", lambda v: v.tensor_copy(gbs[:], pbc[0:1, 0:2]),
      waits=[(s_pe, p.now(s_pe))], inc=s_v, drain=True)
    E("vector", lambda v: v.tensor_scalar(bsrA[:], onesf[0:1, 0:16], gbs[0:1, 0:1],
                                          0.0, ALU.mult, ALU.add), inc=s_v, drain=True)
    E("vector", lambda v: v.tensor_scalar(bsrB[:], onesf[0:1, 0:16], gbs[0:1, 1:2],
                                          0.0, ALU.mult, ALU.add), inc=s_v, drain=True)
    E("vector", lambda v: v.tensor_copy(bsrAb[:], bsrA[:]), inc=s_v, drain=True)
    E("vector", lambda v: v.tensor_copy(bsrBb[:], bsrB[:]), inc=s_v, drain=True)
    v_gp = p.now(s_v)
    E("gpsimd", lambda g: g.memset(wa_sb[:], 0.0), inc=s_g)
    E("gpsimd", lambda g: g.memset(wb_sb[:], 0.0), inc=s_g)
    for h in range(16):
        p0 = (h % 2) * 64
        cc = h // 2
        E("gpsimd", lambda g, h=h, p0=p0, cc=cc: g.dma_start(
            out=wa_sb[p0:p0 + 64, cc * 16 + h: cc * 16 + h + 1], in_=gw2[:, 0:1]),
          waits=[(s_v, v_gp), (s_g, NGS + 2)], inc=s_wt, amt=16)
        E("gpsimd", lambda g, h=h, p0=p0, cc=cc: g.dma_start(
            out=wb_sb[p0:p0 + 64, cc * 16 + h: cc * 16 + h + 1], in_=gw2[:, 1:2]),
          inc=s_wt, amt=16)
    wt_wab = p.now(s_wt)
    if stop_after == 's3':
        E("gpsimd", lambda g: g.dma_start(out=out.ap()[0:128, :], in_=bvb[:]),
          waits=[(s_wt, p.now(s_wt))], inc=s_st, amt=16)
        fin0 = p.now(s_st)
        E("gpsimd", lambda g: g.nop(), waits=[(s_st, fin0)])
        return p.finalize()
    for (dst, src) in ((wq_sb, wqD), (wk_sb, wkD), (wv_sb, wvD)):
        E("gpsimd", lambda g, dst=dst, src=src: g.dma_start(
            out=dst[:], in_=chunkmajor_ap(src, D, D, D)), inc=s_wt, amt=16)
    wt_qkv = p.now(s_wt)

    def ln_stats(src, wv):
        E("vector", lambda v: v.tensor_reduce(stats[:, 0:1], src[:], AX.X, ALU.add),
          waits=wv, inc=s_v, drain=True)
        E("vector", lambda v: v.tensor_tensor(scratch[:], src[:], src[:], ALU.mult),
          inc=s_v, drain=True)
        E("vector", lambda v: v.tensor_reduce(stats[:, 2:3], scratch[:], AX.X, ALU.add),
          inc=s_v, drain=True)
        E("vector", lambda v: v.tensor_scalar(stats[:, 1:2], stats[:, 0:1],
                                              LNI, 0.0, ALU.mult, ALU.add),
          inc=s_v, drain=True)
        E("vector", lambda v: v.tensor_scalar(stats[:, 3:4], stats[:, 2:3],
                                              LNI, 0.0, ALU.mult, ALU.add),
          inc=s_v, drain=True)
        E("vector", lambda v: v.tensor_tensor(stats[:, 4:5], stats[:, 1:2],
                                              stats[:, 1:2], ALU.mult), inc=s_v, drain=True)
        E("vector", lambda v: v.tensor_tensor(stats[:, 5:6], stats[:, 3:4],
                                              stats[:, 4:5], ALU.subtract),
          inc=s_v, drain=True)
        vc = p.now(s_v)
        E("scalar", lambda s: s.activation(stats[:, 6:7], stats[:, 5:6], AF.Ln,
                                           bias=epsc[:, 0:1]),
          waits=[(s_v, vc), (s_g, NGS)], inc=s_sc, drain=True)
        E("scalar", lambda s: s.activation(stats[:, 7:8], stats[:, 6:7],
                                           AF.Exp, scale=-0.5), inc=s_sc, drain=True)
        return p.now(s_sc)

    rel_rbuf = [0, 0]
    rel_ptr = 0
    for i in range(NT):
        b = i % 2
        E("sync", lambda y, i=i, b=b: y.dma_start(
            out=rbuf[b][:], in_=xin.ap()[i * 128:(i + 1) * 128, :]),
          waits=[(s_v, rel_rbuf[b])], inc=s_ld, amt=16)
        ldc = p.now(s_ld)
        sc = ln_stats(rbuf[b], [(s_ld, ldc)])
        E("vector", lambda v, b=b: v.tensor_scalar(
            hbf[:], rbuf[b][:], stats[:, 1:2], stats[:, 7:8], ALU.subtract, ALU.mult),
          waits=[(s_sc, sc)], inc=s_v, drain=True)
        rel_rbuf[b] = p.now(s_v)
        v_hbf = p.now(s_v)
        for c in range(DC):
            E("tensor", lambda t, c=c: t.transpose(
                ptr[:, c * 128:(c + 1) * 128], hbf[:, c * 128:(c + 1) * 128], identb[:]),
              waits=[(s_v, v_hbf), (s_wt, 16), (s_v, rel_ptr)],
              inc=s_pe if c == DC - 1 else None)
        pe_tr = p.now(s_pe)
        for c in range(DC):
            E("vector", lambda v, c=c, i=i: v.tensor_copy(
                xlnT[:, c * T + i * 128: c * T + (i + 1) * 128],
                ptr[:, c * 128:(c + 1) * 128]),
              waits=[(s_pe, pe_tr)], inc=s_v, drain=(c == DC - 1))
        rel_ptr = p.now(s_v)
    v_xlnT = p.now(s_v)
    if stop_after == 'a1':
        E("gpsimd", lambda g: g.dma_start(out=out.ap()[0:128, :], in_=rbuf[0][:]),
          waits=[(s_v, p.now(s_v))], inc=s_st, amt=16)
        fin0 = p.now(s_st)
        E("gpsimd", lambda g: g.nop(), waits=[(s_st, fin0)])
        return p.finalize()

    rel_stage = [0, 0]
    rel_pS_v = [0, 0]
    rel_pS_sc = [0, 0]
    for (wsb, bcol, scl, scr) in ((wq_sb, bqc, SCALING, qt_scr),
                                  (wk_sb, bkc, 1.0, kt_scr)):
        for hp in range(8):
            for tc in range(8):
                bs = tc % 2
                for cc in range(DC):
                    E("tensor", lambda t, wsb=wsb, hp=hp, tc=tc, cc=cc, bs=bs: t.matmul(
                        pS[bs][:], wsb[:, cc * D + hp * 128: cc * D + hp * 128 + 128],
                        xlnT[:, cc * T + tc * 512: cc * T + tc * 512 + 512],
                        start=(cc == 0), stop=(cc == DC - 1)),
                      waits=[(s_wt, wt_qkv), (s_v, v_xlnT), (s_v, rel_pS_v[bs])],
                      inc=s_pe if cc == DC - 1 else None)
                pe_q = p.now(s_pe)
                E("vector", lambda v, hp=hp, bcol=bcol, scl=scl, bs=bs: v.tensor_scalar(
                    stage[bs][:], pS[bs][:], bcol[:, hp:hp + 1], scl, ALU.add, ALU.mult),
                  waits=[(s_pe, pe_q), (s_ld, ld0), (s_st, rel_stage[bs])],
                  inc=s_v, drain=True)
                vst = p.now(s_v)
                E("gpsimd", lambda g, scr=scr, hp=hp, tc=tc, bs=bs: g.dma_start(
                    out=scr.ap()[hp * 128:(hp + 1) * 128, tc * 512:(tc + 1) * 512],
                    in_=stage[bs][:]),
                  waits=[(s_v, vst)], inc=s_st, amt=16)
                rel_stage[bs] = p.now(s_st)
                rel_pS_v[bs] = p.now(s_v)
    for ti in range(NT):
        for co in range(2):
            bs = co % 2
            for cc in range(DC):
                E("tensor", lambda t, ti=ti, co=co, cc=cc, bs=bs: t.matmul(
                    pS[bs][:], xlnT[:, cc * T + ti * 128: cc * T + ti * 128 + 128],
                    wv_sb[:, cc * D + co * 512: cc * D + co * 512 + 512],
                    start=(cc == 0), stop=(cc == DC - 1)),
                  waits=[(s_v, rel_pS_v[bs])], inc=s_pe if cc == DC - 1 else None)
            pe_v = p.now(s_pe)
            E("vector", lambda v, co=co, bs=bs: v.scalar_tensor_tensor(
                stage[bs][:], pS[bs][:], 1.0, bvb[:, co * 512:(co + 1) * 512],
                ALU.mult, ALU.add),
              waits=[(s_pe, pe_v), (s_st, rel_stage[bs])], inc=s_v, drain=True)
            vst = p.now(s_v)
            rel_pS_v[bs] = vst
            E("gpsimd", lambda g, ti=ti, co=co, bs=bs: g.dma_start(
                out=v_scr.ap()[ti * 128:(ti + 1) * 128, co * 512:(co + 1) * 512],
                in_=stage[bs][:]),
              waits=[(s_v, vst)], inc=s_st, amt=16)
            rel_stage[bs] = p.now(s_st)
    rel_pG_v = p.now(s_v)
    for tc in range(8):
        for (wab, bsr, dst) in ((wa_sb, bsrAb, sga), (wb_sb, bsrBb, sgb)):
            for cc in range(DC):
                E("tensor", lambda t, wab=wab, tc=tc, cc=cc: t.matmul(
                    pG[:], wab[:, cc * 16:(cc + 1) * 16],
                    xlnT[:, cc * T + tc * 512: cc * T + tc * 512 + 512],
                    start=(cc == 0), stop=False),
                  waits=[(s_wt, wt_wab), (s_v, v_xlnT), (s_v, rel_pG_v)],
                  inc=None)
            E("tensor", lambda t, bsr=bsr: t.matmul(
                pG[:], bsr[:], onesrb[:], start=False, stop=True),
              waits=[(s_g, 2)], inc=s_pe)
            pe_g = p.now(s_pe)
            E("scalar", lambda s, dst=dst: s.activation(dst[:], pG[:], AF.Exp,
                                                        scale=-1.0),
              waits=[(s_pe, pe_g), (s_v, p.now(s_v))], inc=s_sc, drain=True)
            scg = p.now(s_sc)
            E("vector", lambda v, dst=dst: v.tensor_scalar(dst[:], dst[:], 1.0, 0.0,
                                                           ALU.add, ALU.add),
              waits=[(s_sc, scg)], inc=s_v, drain=True)
            rel_pG_v = p.now(s_v)
            E("vector", lambda v, dst=dst: v.reciprocal(dst[:], dst[:]),
              inc=s_v, drain=True)
        E("vector", lambda v: v.tensor_scalar(gt16[:], sgb[:], gc16[:, 0:1], 1.0,
                                              ALU.mult, ALU.subtract),
          inc=s_v, drain=True)
        E("vector", lambda v: v.tensor_tensor(gt16[:], sga[:], gt16[:], ALU.mult),
          inc=s_v, drain=True)
        E("vector", lambda v: v.tensor_scalar(gt16[:], gt16[:], 2.0, 0.0,
                                              ALU.add, ALU.add), inc=s_v, drain=True)
        vg = p.now(s_v)
        E("gpsimd", lambda g, tc=tc: g.dma_start(
            out=gate_scr.ap()[:, tc * 512:(tc + 1) * 512], in_=gt16[:]),
          waits=[(s_v, vg)], inc=s_st, amt=16)
    a2_st = p.now(s_st)
    a2_pe = p.now(s_pe)
    if stop_after == 'a2':
        E("gpsimd", lambda g: g.dma_start(out=out.ap()[0:128, :], in_=rbuf[0][:]),
          waits=[(s_st, a2_st)], inc=s_st, amt=16)
        fin0 = p.now(s_st)
        E("gpsimd", lambda g: g.nop(), waits=[(s_st, fin0)])
        return p.finalize()

    band_deltas = set(range(-254, 639))
    rel_pbc_v = p.now(s_v)
    rel_rows = 0
    rel_flip = 0
    rel_unit_pe = 0
    rel_P = [0, 0, 0]
    rel_Bt = 0
    rel_Bg = 0
    rel_pO_v = 0
    rel_at = 0
    rel_kv_pe = 0
    for h in range(16):
        E("sync", lambda y, h=h: y.dma_start(
            out=kth[:], in_=kt_scr.ap()[h * 64:(h + 1) * 64, :]),
          waits=[(s_st, a2_st), (s_pe, rel_kv_pe)], inc=s_ld, amt=16)
        vsrc = v_scr.ap()
        vsrc2 = vsrc.copy()
        while len(vsrc2.ap) > 1:
            vsrc2.ap.pop()
        vsrc2.ap[0] = [D, 128]
        vsrc2.ap.append([128 * D, 32])
        vsrc2.ap.append([1, 64])
        vsrc2.offset = h * 64
        vdst = vh.ap().copy() if callable(getattr(vh, "ap", None)) else vh[:].copy()
        pstride = int(vdst.ap[0][0])
        while len(vdst.ap) > 1:
            vdst.ap.pop()
        vdst.ap[0] = [pstride, 128]
        vdst.ap.append([65, 32])
        vdst.ap.append([1, 64])
        E("sync", lambda y, vdst=vdst, vsrc2=vsrc2: y.dma_start(out=vdst, in_=vsrc2),
          waits=[(s_pe, rel_kv_pe)], inc=s_ld, amt=16)
        E("sync", lambda y, h=h: y.dma_start(out=grow[:],
                                             in_=gate_scr.ap()[h:h + 1, :]),
          waits=[(s_v, rel_Bg)], inc=s_ld, amt=16)
        E("sync", lambda y, h=h: y.dma_start(out=crow[:], in_=csc_hbm.ap()[h:h + 1, :]),
          waits=[(s_st, 32)], inc=s_ld, amt=16)
        ld_h = p.now(s_ld)
        for qg in range(8):
            E("sync", lambda y, h=h, qg=qg: y.dma_start(
                out=qsl[:], in_=qt_scr.ap()[h * 64:(h + 1) * 64,
                                            qg * 512:(qg + 1) * 512]),
              waits=[(s_pe, rel_unit_pe)], inc=s_ld, amt=16)
            ld_q = p.now(s_ld)
            E("tensor", lambda t, qg=qg: t.matmul(
                pbc[:], onesf[:], grow[0:1, qg * 512:(qg + 1) * 512],
                start=True, stop=True),
              waits=[(s_ld, ld_h), (s_v, rel_Bg), (s_v, rel_pbc_v)], inc=s_pe)
            peb = p.now(s_pe)
            E("vector", lambda v: v.tensor_copy(gb_sb[:], pbc[:]),
              waits=[(s_pe, peb)], inc=s_v, drain=True)
            rel_Bg = p.now(s_v)
            E("vector", lambda v, qg=qg: v.tensor_scalar(
                rowF[:], grow[0:1, qg * 512:(qg + 1) * 512], crow[0:1, 1:2], 0.0,
                ALU.mult, ALU.add),
              waits=[(s_ld, ld_h), (s_pe, rel_rows)], inc=s_v, drain=True)
            E("vector", lambda v, qg=qg: v.tensor_scalar(
                denr[:], grow[0:1, qg * 512:(qg + 1) * 512], crow[0:1, 0:1], 0.0,
                ALU.mult, ALU.add), inc=s_v, drain=True)
            v_rows = p.now(s_v)
            for kt in range(32):
                delta = kt * 128 - qg * 512
                bs = kt % 2
                E("tensor", lambda t, kt=kt, bs=bs: t.matmul(
                    pS[bs][:], kth[:, kt * 128:(kt + 1) * 128], qsl[:],
                    start=True, stop=False),
                  waits=[(s_ld, ld_q), (s_sc, rel_pS_sc[bs]),
                         (s_v, rel_pS_v[bs])],
                  inc=None)
                if delta in band_deltas:
                    E("sync", lambda y, h=h, delta=delta: y.dma_start(
                        out=Bt[:], in_=toep_ap(tblr_hbm, h, delta)),
                      waits=[(s_st, 48), (s_v, rel_Bt)], inc=s_ld, amt=16)
                    ldb = p.now(s_ld)
                    E("vector", lambda v: v.tensor_tensor(Bg[:], Bt[:], gb_sb[:],
                                                          ALU.mult),
                      waits=[(s_ld, ldb), (s_pe, rel_flip)], inc=s_v, drain=True)
                    rel_Bt = p.now(s_v)
                    vbg = p.now(s_v)
                    E("tensor", lambda t, bs=bs: t.matmul(
                        pS[bs][:], flipb[:], Bg[:], start=False, stop=True),
                      waits=[(s_v, vbg), (s_wt, wt0)], inc=s_pe)
                    rel_flip = p.now(s_pe)
                else:
                    rw = rowF if delta <= -256 else denr
                    E("tensor", lambda t, bs=bs, rw=rw: t.matmul(
                        pS[bs][:], onesf[:], rw[:], start=False, stop=True),
                      waits=[(s_v, v_rows), (s_g, 1)], inc=s_pe)
                pe_s = p.now(s_pe)
                pb = kt % 3
                E("scalar", lambda s, bs=bs, pb=pb: s.activation(
                    Pb[pb][:], pS[bs][:], AF.Exp),
                  waits=[(s_pe, pe_s), (s_pe, rel_P[pb])], inc=s_sc, drain=True)
                sc_p = p.now(s_sc)
                rel_pS_sc[bs] = sc_p
                if dbg_p and h == 0 and ((qg == 0 and kt == 10) or (qg == 7 and kt == 10)):
                    dst = dbg_pfr if qg == 0 else dbg_pfl
                    E("gpsimd", lambda g, dst=dst, pb=pb: g.dma_start(
                        out=dst.ap(), in_=Pb[pb][:]),
                      waits=[(s_sc, sc_p)], inc=s_st, amt=16)
                    if qg == 0:
                        E("gpsimd", lambda g: g.dma_start(out=dbg_row.ap(), in_=denr[:]),
                          waits=[(s_v, v_rows)], inc=s_st, amt=16)
                E("tensor", lambda t, kt=kt, pb=pb: t.matmul(
                    pO[:], vh[:, kt * 65:(kt + 1) * 65], Pb[pb][:],
                    start=(kt == 0), stop=(kt == 31)),
                  waits=[(s_sc, sc_p), (s_ld, ld_h), (s_v, rel_pO_v), (s_g, NGS)],
                  inc=s_pe)
                rel_P[pb] = p.now(s_pe)
            pe_av = p.now(s_pe)
            E("vector", lambda v: v.tensor_copy(denr[:], pO[64:65, 0:512]),
              waits=[(s_pe, pe_av)], inc=s_v, drain=True)
            E("vector", lambda v: v.reciprocal(denr[:], denr[:]), inc=s_v, drain=True)
            vd = p.now(s_v)
            E("tensor", lambda t: t.matmul(pbc[0:64, :], onesf[0:1, 0:64], denr[:],
                                           start=True, stop=True),
              waits=[(s_v, vd), (s_v, rel_Bg)], inc=s_pe)
            ped = p.now(s_pe)
            rel_rows = ped
            rel_unit_pe = pe_av
            E("vector", lambda v: v.tensor_copy(dr_sb[:], pbc[0:64, :]),
              waits=[(s_pe, ped)], inc=s_v, drain=True)
            rel_pbc_v = p.now(s_v)
            E("vector", lambda v: v.tensor_tensor(atst[:], pO[0:64, 0:512], dr_sb[:],
                                                  ALU.mult),
              waits=[(s_st, rel_at)], inc=s_v, drain=True)
            rel_pO_v = p.now(s_v)
            vat = p.now(s_v)
            E("gpsimd", lambda g, h=h, qg=qg: g.dma_start(
                out=at_scr.ap()[h * 64:(h + 1) * 64, qg * 512:(qg + 1) * 512],
                in_=atst[:]),
              waits=[(s_v, vat)], inc=s_st, amt=16)
            rel_at = p.now(s_st)
        rel_kv_pe = p.now(s_pe)
    a4_st = p.now(s_st)
    if stop_after == 'a4':
        E("gpsimd", lambda g: g.dma_start(out=out.ap()[0:128, :], in_=rbuf[0][:]),
          waits=[(s_st, a4_st)], inc=s_st, amt=16)
        fin0 = p.now(s_st)
        E("gpsimd", lambda g: g.nop(), waits=[(s_st, fin0)])
        return p.finalize()

    E("gpsimd", lambda g: g.dma_start(
        out=wo_sb[:], in_=chunkmajor_ap(woD, D, D, D)),
      waits=[(s_pe, p.now(s_pe))], inc=s_wt, amt=16)
    wt_wo = p.now(s_wt)
    onesb1 = onesrb[0:1, 0:128]
    rel_xs2 = [0, 0]
    rel_ob = 0
    rel_pS_v2 = [p.now(s_v), p.now(s_v)]
    for ti in range(NT):
        b = ti % 2
        E("sync", lambda y, ti=ti, b=b: y.dma_start(
            out=xs2[b][:], in_=chunkmajor_ap(at_scr, D, T, 128, col0=ti * 128)),
          waits=[(s_st, a4_st), (s_pe, rel_xs2[b])], inc=s_ld, amt=16)
        E("sync", lambda y, ti=ti, b=b: y.dma_start(
            out=rbuf[b][:], in_=xin.ap()[ti * 128:(ti + 1) * 128, :]),
          waits=[(s_v, rel_rbuf[b])], inc=s_ld, amt=16)
        ld5 = p.now(s_ld)
        for co in range(2):
            bs = co
            for cc in range(DC):
                E("tensor", lambda t, co=co, cc=cc, b=b, bs=bs: t.matmul(
                    pS[bs][:], xs2[b][:, cc * 128:(cc + 1) * 128],
                    wo_sb[:, cc * D + co * 512: cc * D + co * 512 + 512],
                    start=(cc == 0), stop=False),
                  waits=[(s_wt, wt_wo), (s_ld, ld5), (s_v, rel_pS_v2[bs])],
                  inc=None)
            E("tensor", lambda t, co=co, bs=bs: t.matmul(
                pS[bs][:], onesb1, bob[0:1, co * 512:(co + 1) * 512],
                start=False, stop=True),
              waits=[(s_g, 2), (s_wt, 16 * 8)], inc=s_pe)
            pe_o = p.now(s_pe)
            E("vector", lambda v, co=co, b=b, bs=bs: v.tensor_tensor(
                obuf[:, co * 512:(co + 1) * 512], pS[bs][:],
                rbuf[b][:, co * 512:(co + 1) * 512], ALU.add),
              waits=[(s_pe, pe_o), (s_st, rel_ob)], inc=s_v, drain=True)
            rel_pS_v2[bs] = p.now(s_v)
        rel_rbuf[b] = p.now(s_v)
        rel_xs2[b] = p.now(s_pe)
        vo = p.now(s_v)
        E("gpsimd", lambda g, ti=ti: g.dma_start(
            out=out.ap()[ti * 128:(ti + 1) * 128, :], in_=obuf[:]),
          waits=[(s_v, vo)], inc=s_st, amt=16)
        rel_ob = p.now(s_st)

    fin = p.now(s_st)
    E("gpsimd", lambda g: g.nop(), waits=[(s_st, fin)])
    return p.finalize() if fin_own else None


def toep_ap(tbl, h, delta):
    ap = tbl.ap().copy()
    while len(ap.ap) > 1:
        ap.ap.pop()
    ap.ap[0] = [1, 128]
    ap.ap.append([1, 512])
    ap.offset = h * 1280 + (512 - delta)
    return ap


def _rel_buckets_delta(d):
    NUM_BUCKETS, MAX_DIST = 32, 128
    nb = NUM_BUCKETS // 2
    bucket = (1 if d > 0 else 0) * nb
    rp = abs(d)
    max_exact = nb // 2
    if rp < max_exact:
        bucket += rp
    else:
        import math
        large = max_exact + int(math.log(max(rp, 1) / max_exact)
                                / math.log(MAX_DIST / max_exact) * (nb - max_exact))
        bucket += min(large, nb - 1)
    return bucket


def prep_launch_a(inputs):
    oh = np.zeros((32, 1280), np.float32)
    for j in range(1280):
        oh[_rel_buckets_delta(639 - j), j] = 1.0
    flip = np.zeros((128, 128), np.float32)
    for i in range(128):
        flip[i, 127 - i] = 1.0
    def colsfmt(vec):
        return np.ascontiguousarray(np.asarray(vec, np.float32).reshape(8, 128).T)
    d = dict(
        wq=np.ascontiguousarray(np.asarray(inputs["L0_attn_Wq"], np.float32)),
        wk=np.ascontiguousarray(np.asarray(inputs["L0_attn_Wk"], np.float32)),
        wv=np.ascontiguousarray(np.asarray(inputs["L0_attn_Wv"], np.float32)),
        wo=np.ascontiguousarray(np.asarray(inputs["L0_attn_Wo"], np.float32)),
        bq=colsfmt(inputs["L0_attn_bq"]),
        bk=colsfmt(inputs["L0_attn_bk"]),
        bv=np.asarray(inputs["L0_attn_bv"], np.float32).reshape(1, D),
        bo=np.asarray(inputs["L0_attn_bo"], np.float32).reshape(1, D),
        oh=oh,
        rel=np.ascontiguousarray(np.asarray(inputs["L0_attn_rel_embed"], np.float32)),
        gwt=np.ascontiguousarray(np.asarray(inputs["L0_attn_gru_w"], np.float32).T),
        gbt=np.asarray(inputs["L0_attn_gru_b"], np.float32).reshape(8, 1),
        ob2=np.concatenate([(np.arange(8) < 4).astype(np.float32)[:, None],
                            (np.arange(8) >= 4).astype(np.float32)[:, None]], 1),
        gc16=np.asarray(inputs["L0_attn_gru_const"], np.float32).reshape(16, 1),
        flip=flip,
        ident=np.eye(128, dtype=np.float32),
    )
    return d


def run_launch_a(x_full, inputs, stop_after=None, dbg_p=False):
    nc = build_launch_a(stop_after, dbg_p)
    com = prep_launch_a(inputs)
    in_maps = []
    for b in range(8):
        m = dict(com)
        m["xin"] = np.ascontiguousarray(x_full[b])
        in_maps.append(m)
    res = run_bass_kernel_spmd(nc, in_maps, core_ids=list(range(8)))
    if dbg_p:
        return res.results
    return np.stack([r["out"] for r in res.results])



def build_launch_b1(ctx=None):
    nc, p, DT, sb, psum, fin_own = _ctx_or_new(ctx, "b1")

    res1 = DT("res1", [T, D], F32, "ExternalInput")
    wrtD = DT("wrt", [D, 8], F32, "ExternalInput")
    ident = DT("ident", [128, 128], F32, "ExternalInput")
    lg = DT("lg", [T, 8], F32, "ExternalOutput")

    rbuf = [sb(f"rbuf{i}", [128, D], F32) for i in range(2)]
    scratch = sb("scratch", [128, D], F32)
    h32 = sb("h32", [128, D], F32)
    hbf = sb("hbf", [128, D], BF16)
    hlo = sb("hlo", [128, D], BF16)
    hts = sb("hts", [128, D], BF16)
    htl = sb("htl", [128, D], BF16)
    stats = sb("stats", [128, 16], F32)
    epsc = sb("epsc", [128, 1], F32)
    wrf = sb("wrf", [128, 64], F32)
    wrh = sb("wrh", [128, 64], BF16)
    wrl = sb("wrl", [128, 64], BF16)
    identb = sb("identb", [128, 128], BF16)
    lgst = [sb(f"lgst{i}", [128, 8], F32) for i in range(2)]

    ptr = psum("ptr", [128, 1024], BF16)
    ptr2 = psum("ptr2", [128, 1024], BF16)
    pbc = psum("pbc", [128, 512], F32)

    s_ld = p.sem("s_ld")
    s_wt = p.sem("s_wt")
    s_pe = p.sem("s_pe", owner="tensor")
    s_v = p.sem("s_v", owner="vector")
    s_sc = p.sem("s_sc", owner="scalar")
    s_st = p.sem("s_st")
    s_g = p.sem("s_g", owner="gpsimd")
    E = p.emit
    LNI = 1.0 / D

    E("gpsimd", lambda g: g.memset(epsc[:], 1e-5), inc=s_g)
    E("gpsimd", lambda g: g.dma_start(out=identb[:], in_=ident.ap()), inc=s_wt, amt=16)
    E("sync", lambda y: y.dma_start(out=wrf[:], in_=chunkmajor_ap(wrtD, D, 8, 8)),
      inc=s_ld, amt=16)
    ld0 = p.now(s_ld)
    E("vector", lambda v: v.tensor_copy(wrh[:], wrf[:]), waits=[(s_ld, ld0)],
      inc=s_v, drain=True)
    E("vector", lambda v: v.tensor_tensor(wrl[:], wrf[:], wrh[:], ALU.subtract),
      inc=s_v, drain=True)
    v_wr = p.now(s_v)

    rel_rbuf = [0, 0]
    rel_ptr = 0
    rel_ptr2 = 0
    rel_pbc = 0
    hts_mm = 0
    rel_lgst = [0, 0]
    for i in range(NT):
        b = i % 2
        E("sync", lambda y, i=i, b=b: y.dma_start(
            out=rbuf[b][:], in_=res1.ap()[i * 128:(i + 1) * 128, :]),
          waits=[(s_v, rel_rbuf[b])], inc=s_ld, amt=16)
        ldc = p.now(s_ld)
        E("vector", lambda v, b=b: v.tensor_reduce(stats[:, 0:1], rbuf[b][:], AX.X,
                                                   ALU.add),
          waits=[(s_ld, ldc)], inc=s_v, drain=True)
        E("vector", lambda v, b=b: v.tensor_tensor(scratch[:], rbuf[b][:], rbuf[b][:],
                                                   ALU.mult), inc=s_v, drain=True)
        E("vector", lambda v: v.tensor_reduce(stats[:, 2:3], scratch[:], AX.X, ALU.add),
          inc=s_v, drain=True)
        E("vector", lambda v: v.tensor_scalar(stats[:, 1:2], stats[:, 0:1], LNI, 0.0,
                                              ALU.mult, ALU.add), inc=s_v, drain=True)
        E("vector", lambda v: v.tensor_scalar(stats[:, 3:4], stats[:, 2:3], LNI, 0.0,
                                              ALU.mult, ALU.add), inc=s_v, drain=True)
        E("vector", lambda v: v.tensor_tensor(stats[:, 4:5], stats[:, 1:2],
                                              stats[:, 1:2], ALU.mult), inc=s_v, drain=True)
        E("vector", lambda v: v.tensor_tensor(stats[:, 5:6], stats[:, 3:4],
                                              stats[:, 4:5], ALU.subtract),
          inc=s_v, drain=True)
        vc = p.now(s_v)
        E("scalar", lambda s: s.activation(stats[:, 6:7], stats[:, 5:6], AF.Ln,
                                           bias=epsc[:, 0:1]),
          waits=[(s_v, vc), (s_g, 1)], inc=s_sc, drain=True)
        E("scalar", lambda s: s.activation(stats[:, 7:8], stats[:, 6:7], AF.Exp,
                                           scale=-0.5), inc=s_sc, drain=True)
        sc = p.now(s_sc)
        E("vector", lambda v, b=b: v.tensor_scalar(
            h32[:], rbuf[b][:], stats[:, 1:2], stats[:, 7:8], ALU.subtract, ALU.mult),
          waits=[(s_sc, sc)], inc=s_v, drain=True)
        rel_rbuf[b] = p.now(s_v)
        E("vector", lambda v: v.tensor_copy(hbf[:], h32[:]), inc=s_v, drain=True)
        E("vector", lambda v: v.tensor_tensor(hlo[:], h32[:], hbf[:], ALU.subtract),
          inc=s_v, drain=True)
        v_hbf = p.now(s_v)
        for c in range(DC):
            E("tensor", lambda t, c=c: t.transpose(
                ptr[:, c * 128:(c + 1) * 128], hbf[:, c * 128:(c + 1) * 128], identb[:]),
              waits=[(s_v, v_hbf), (s_wt, 16), (s_v, rel_ptr)],
              inc=s_pe if c == DC - 1 else None)
        pe1 = p.now(s_pe)
        E("vector", lambda v: v.tensor_copy(hts[:], ptr[:]),
          waits=[(s_pe, pe1), (s_pe, hts_mm)], inc=s_v, drain=True)
        rel_ptr = p.now(s_v)
        for c in range(DC):
            E("tensor", lambda t, c=c: t.transpose(
                ptr2[:, c * 128:(c + 1) * 128], hlo[:, c * 128:(c + 1) * 128], identb[:]),
              waits=[(s_v, p.now(s_v)), (s_v, rel_ptr2)],
              inc=s_pe if c == DC - 1 else None)
        pe2 = p.now(s_pe)
        E("vector", lambda v: v.tensor_copy(htl[:], ptr2[:]),
          waits=[(s_pe, pe2)], inc=s_v, drain=True)
        rel_ptr2 = p.now(s_v)
        v_ht = p.now(s_v)
        nmm = 0
        for c in range(DC):
            for (lt, rt) in ((hts, wrh), (hts, wrl), (htl, wrh)):
                nmm += 1
                E("tensor", lambda t, c=c, lt=lt, rt=rt, first=(nmm == 1),
                  last=(nmm == 24): t.matmul(
                      pbc[:, 0:8], lt[:, c * 128:(c + 1) * 128],
                      rt[:, c * 8:(c + 1) * 8], start=first, stop=last),
                  waits=[(s_v, v_ht), (s_v, v_wr), (s_v, rel_pbc)],
                  inc=s_pe if nmm == 24 else None)
        hts_mm = p.now(s_pe)
        E("vector", lambda v, b=b: v.tensor_copy(lgst[b][:], pbc[:, 0:8]),
          waits=[(s_pe, hts_mm), (s_st, rel_lgst[b])], inc=s_v, drain=True)
        rel_pbc = p.now(s_v)
        vl = p.now(s_v)
        E("gpsimd", lambda g, i=i, b=b: g.dma_start(
            out=lg.ap()[i * 128:(i + 1) * 128, :], in_=lgst[b][:]),
          waits=[(s_v, vl)], inc=s_st, amt=16)
        rel_lgst[b] = p.now(s_st)
    fin = p.now(s_st)
    E("gpsimd", lambda g: g.nop(), waits=[(s_st, fin)])
    return p.finalize() if fin_own else None


def run_launch_b1(res1_full, inputs):
    nc = build_launch_b1()
    wrt = np.ascontiguousarray(
        np.asarray(inputs["L0_ffn_router_weight"], np.float32).T)
    ident = np.eye(128, dtype=np.float32)
    in_maps = [dict(res1=np.ascontiguousarray(res1_full[b]), wrt=wrt, ident=ident)
               for b in range(8)]
    res = run_bass_kernel_spmd(nc, in_maps, core_ids=list(range(8)))
    return np.stack([r["lg"] for r in res.results])



GT = 8 * T
GNT = GT // 128
MOE_EPS = 1.1920929e-07
CAPACITY = 10240.0

def run_launch_b1(res1_full, inputs):
    nc = build_launch_b1()
    wrt = np.ascontiguousarray(
        np.asarray(inputs["L0_ffn_router_weight"], np.float32).T)
    ident = np.eye(128, dtype=np.float32)
    in_maps = [dict(res1=np.ascontiguousarray(res1_full[b]), wrt=wrt, ident=ident)
               for b in range(8)]
    res = run_bass_kernel_spmd(nc, in_maps, core_ids=list(range(8)))
    return np.stack([r["lg"] for r in res.results])



GT = 8 * T
GNT = GT // 128
MOE_EPS = 1.1920929e-07
CAPACITY = 10240.0

def build_launch_b2(debug=False, ctx=None):
    nc, p, DT, sb, psum, fin_own = _ctx_or_new(ctx, "b2")

    res1 = DT("res1", [T, D], F32, "ExternalInput")
    lgall = DT("lgall", [GT, 8], F32, "ExternalInput")
    preTD = DT("preT", [8, D, 4096], F32, "ExternalInput")
    gateTD = DT("gateT", [8, D, 4096], F32, "ExternalInput")
    postTD = DT("postT", [8, 4096, D], F32, "ExternalInput")
    idx8 = DT("idx8", [128, 8], F32, "ExternalInput")
    idx8m8 = DT("idx8m8", [128, 8], F32, "ExternalInput")
    ident = DT("ident", [128, 128], F32, "ExternalInput")
    out = DT("out", [T, D], F32, "ExternalOutput")
    ht_scr = DT("ht_scr", [D, T], BF16, "Internal")
    acc_scr = DT("acc_scr", [T, D], F32, "Internal")
    if debug:
        dbg_wf = DT("dbg_wf", [128, 512], F32, "ExternalOutput")
        dbg_ef = DT("dbg_ef", [128, 512], F32, "ExternalOutput")
        dbg_kf = DT("dbg_kf", [128, 512], F32, "ExternalOutput")
        dbg_cw = DT("dbg_cw", [128, NT * 8], F32, "ExternalOutput")
        dbg_th = DT("dbg_th", [128, 8], F32, "ExternalOutput")
        dbg_acc = DT("dbg_acc", [T, D], F32, "ExternalOutput")
        dbg_g0 = DT("dbg_g0", [128, 8 * 256], BF16, "ExternalOutput")

    rbuf = [sb(f"rbuf{i}", [128, D], F32) for i in range(2)]
    scratch = sb("scratch", [128, D], F32)
    h32 = sb("h32", [128, D], F32)
    hbf = sb("hbf", [128, D], BF16)
    hts = sb("hts", [128, D], BF16)
    stats = sb("stats", [128, 24], F32)
    epsc = sb("epsc", [128, 1], F32)
    onesf = sb("onesf", [1, 128], F32)
    onescol = sb("onescol", [128, 1], F32)
    identb = sb("identb", [128, 128], BF16)
    idx8s = sb("idx8s", [128, 8], F32)
    idx8m8s = sb("idx8m8s", [128, 8], F32)
    lg8 = [sb(f"lg8_{i}", [128, 8], F32) for i in range(2)]
    e8 = sb("e8", [128, 8], F32)
    pr = sb("pr", [128, 8], F32)
    pr2 = sb("pr2", [128, 8], F32)
    maskA = sb("maskA", [128, 8], F32)
    maskB = sb("maskB", [128, 8], F32)
    tmp8 = sb("tmp8", [128, 8], F32)
    wflat = sb("wflat", [128, 512], F32)
    eflat = sb("eflat", [128, 512], F32)
    kflat = sb("kflat", [128, 512], F32)
    emask = [sb(f"emask{e}", [128, 512], F32) for e in range(8)]
    accb = sb("accb", [128, 512], F32)
    cmpf = sb("cmpf", [128, 512], F32)
    cnt8 = sb("cnt8", [128, 8], F32)
    rowst = sb("rowst", [1, 32], F32)
    midc = sb("midc", [128, 8], F32)
    kw1 = sb("kw1", [128, 256], F32)
    kw2 = sb("kw2", [128, 256], F32)
    cw = sb("cw", [128, NT * 8], F32)
    wpre = sb("wpre", [128, 8 * 1024], BF16)
    wgate = sb("wgate", [128, 8 * 1024], BF16)
    wpost = sb("wpost", [128, 8 * 1024], BF16)
    xsb = [sb(f"xsb{i}", [128, 8 * 256], BF16) for i in range(2)]
    gtb = sb("gtb", [128, 8 * 256], BF16)
    silb = sb("silb", [128, 256], BF16)
    stage = [sb(f"stage{i}", [128, D], F32) for i in range(2)]
    atile = sb("atile", [128, D], F32)
    obuf = sb("obuf", [128, D], F32)

    pm1 = [psum(f"pm1_{i}", [128, 256], F32) for i in range(2)]
    pmg = psum("pmg", [128, 256], F32)
    pm2 = [psum(f"pm2_{i}", [128, 512], F32) for i in range(2)]
    ptr = psum("ptr", [128, 1024], BF16)
    pbc = psum("pbc", [128, 512], F32)

    s_ld = p.sem("s_ld")
    s_wt = p.sem("s_wt")
    s_pe = p.sem("s_pe", owner="tensor")
    s_v = p.sem("s_v", owner="vector")
    s_sc = p.sem("s_sc", owner="scalar")
    s_st = p.sem("s_st")
    s_g = p.sem("s_g", owner="gpsimd")
    E = p.emit
    LNI = 1.0 / D

    E("gpsimd", lambda g: g.memset(epsc[:], 1e-5), inc=s_g)
    E("gpsimd", lambda g: g.memset(onesf[:], 1.0), inc=s_g)
    E("gpsimd", lambda g: g.memset(onescol[:], 1.0), inc=s_g)
    E("gpsimd", lambda g: g.memset(rowst[0:1, 0:8], -1.0), inc=s_g)
    E("gpsimd", lambda g: g.memset(rowst[0:1, 8:16], 1.0), inc=s_g)
    NGS = 5
    E("gpsimd", lambda g: g.dma_start(out=identb[:], in_=ident.ap()), inc=s_wt, amt=16)
    E("sync", lambda y: y.dma_start(out=idx8s[:], in_=idx8.ap()), inc=s_ld, amt=16)
    E("sync", lambda y: y.dma_start(out=idx8m8s[:], in_=idx8m8.ap()), inc=s_ld, amt=16)
    ld0 = p.now(s_ld)

    rel_rbuf = [0, 0]
    rel_ptr = 0
    need_hts = 0
    for i in range(NT):
        b = i % 2
        E("sync", lambda y, i=i, b=b: y.dma_start(
            out=rbuf[b][:], in_=res1.ap()[i * 128:(i + 1) * 128, :]),
          waits=[(s_v, rel_rbuf[b])], inc=s_ld, amt=16)
        ldc = p.now(s_ld)
        E("vector", lambda v, b=b: v.tensor_reduce(stats[:, 0:1], rbuf[b][:], AX.X,
                                                   ALU.add),
          waits=[(s_ld, ldc)], inc=s_v, drain=True)
        E("vector", lambda v, b=b: v.tensor_tensor(scratch[:], rbuf[b][:], rbuf[b][:],
                                                   ALU.mult), inc=s_v, drain=True)
        E("vector", lambda v: v.tensor_reduce(stats[:, 2:3], scratch[:], AX.X, ALU.add),
          inc=s_v, drain=True)
        E("vector", lambda v: v.tensor_scalar(stats[:, 1:2], stats[:, 0:1], LNI, 0.0,
                                              ALU.mult, ALU.add), inc=s_v, drain=True)
        E("vector", lambda v: v.tensor_scalar(stats[:, 3:4], stats[:, 2:3], LNI, 0.0,
                                              ALU.mult, ALU.add), inc=s_v, drain=True)
        E("vector", lambda v: v.tensor_tensor(stats[:, 4:5], stats[:, 1:2],
                                              stats[:, 1:2], ALU.mult), inc=s_v, drain=True)
        E("vector", lambda v: v.tensor_tensor(stats[:, 5:6], stats[:, 3:4],
                                              stats[:, 4:5], ALU.subtract),
          inc=s_v, drain=True)
        vc = p.now(s_v)
        E("scalar", lambda s: s.activation(stats[:, 6:7], stats[:, 5:6], AF.Ln,
                                           bias=epsc[:, 0:1]),
          waits=[(s_v, vc), (s_g, 1)], inc=s_sc, drain=True)
        E("scalar", lambda s: s.activation(stats[:, 7:8], stats[:, 6:7], AF.Exp,
                                           scale=-0.5), inc=s_sc, drain=True)
        sc = p.now(s_sc)
        E("vector", lambda v, b=b: v.tensor_scalar(
            hbf[:], rbuf[b][:], stats[:, 1:2], stats[:, 7:8], ALU.subtract, ALU.mult),
          waits=[(s_sc, sc)], inc=s_v, drain=True)
        rel_rbuf[b] = p.now(s_v)
        v_hbf = p.now(s_v)
        for c in range(DC):
            E("tensor", lambda t, c=c: t.transpose(
                ptr[:, c * 128:(c + 1) * 128], hbf[:, c * 128:(c + 1) * 128], identb[:]),
              waits=[(s_v, v_hbf), (s_wt, 16), (s_v, rel_ptr)],
              inc=s_pe if c == DC - 1 else None)
        pe1 = p.now(s_pe)
        E("vector", lambda v: v.tensor_copy(hts[:], ptr[:]),
          waits=[(s_pe, pe1), (s_st, need_hts)], inc=s_v, drain=True)
        rel_ptr = p.now(s_v)
        v_ht = p.now(s_v)
        E("gpsimd", lambda g, i=i: g.dma_start(
            out=chunkmajor_ap(ht_scr, D, T, 128, col0=i * 128), in_=hts[:]),
          waits=[(s_v, v_ht)], inc=s_st, amt=16)
        need_hts = p.now(s_st)
    p1_st = p.now(s_st)

    rel_lg = [0, 0]
    for i in range(GNT):
        b = i % 2
        E("sync", lambda y, i=i, b=b: y.dma_start(
            out=lg8[b][:], in_=lgall.ap()[i * 128:(i + 1) * 128, :]),
          waits=[(s_v, rel_lg[b])], inc=s_ld, amt=16)
        ldc = p.now(s_ld)
        E("vector", lambda v, b=b: v.tensor_reduce(stats[:, 8:9], lg8[b][:], AX.X,
                                                   ALU.max),
          waits=[(s_ld, ldc), (s_ld, ld0)], inc=s_v, drain=True)
        E("vector", lambda v: v.tensor_scalar(stats[:, 9:10], stats[:, 8:9], -1.0, 0.0,
                                              ALU.mult, ALU.add), inc=s_v, drain=True)
        vb = p.now(s_v)
        E("scalar", lambda s, b=b: s.activation(e8[:], lg8[b][:], AF.Exp,
                                                bias=stats[:, 9:10],
                                                accum_out=stats[:, 10:11]),
          waits=[(s_v, vb)], inc=s_sc, drain=True)
        scb = p.now(s_sc)
        E("vector", lambda v: v.reciprocal(stats[:, 11:12], stats[:, 10:11]),
          waits=[(s_sc, scb)], inc=s_v, drain=True)
        rel_lg[b] = p.now(s_v)
        E("vector", lambda v: v.tensor_scalar(pr[:], e8[:], stats[:, 11:12], 0.0,
                                              ALU.mult, ALU.add), inc=s_v, drain=True)
        E("vector", lambda v: v.tensor_reduce(stats[:, 12:13], pr[:], AX.X, ALU.max),
          inc=s_v, drain=True)
        E("vector", lambda v: v.tensor_scalar(maskA[:], pr[:], stats[:, 12:13], 0.0,
                                              ALU.is_equal, ALU.add), inc=s_v, drain=True)
        E("vector", lambda v: v.tensor_tensor(tmp8[:], maskA[:], idx8m8s[:], ALU.mult),
          inc=s_v, drain=True)
        E("vector", lambda v: v.tensor_scalar(tmp8[:], tmp8[:], 8.0, 0.0, ALU.add,
                                              ALU.add), inc=s_v, drain=True)
        E("vector", lambda v, i=i: v.tensor_reduce(eflat[:, i:i + 1], tmp8[:], AX.X,
                                                   ALU.min), inc=s_v, drain=True)
        E("vector", lambda v, i=i: v.tensor_scalar(maskA[:], idx8s[:],
                                                   eflat[:, i:i + 1], 0.0,
                                                   ALU.is_equal, ALU.add),
          inc=s_v, drain=True)
        E("vector", lambda v: v.scalar_tensor_tensor(pr2[:], maskA[:], -1e30, pr[:],
                                                     ALU.mult, ALU.add),
          inc=s_v, drain=True)
        E("vector", lambda v: v.tensor_reduce(stats[:, 13:14], pr2[:], AX.X, ALU.max),
          inc=s_v, drain=True)
        E("vector", lambda v: v.tensor_scalar(maskB[:], pr2[:], stats[:, 13:14], 0.0,
                                              ALU.is_equal, ALU.add), inc=s_v, drain=True)
        E("vector", lambda v: v.tensor_tensor(tmp8[:], maskB[:], idx8m8s[:], ALU.mult),
          inc=s_v, drain=True)
        E("vector", lambda v: v.tensor_scalar(tmp8[:], tmp8[:], 8.0, 0.0, ALU.add,
                                              ALU.add), inc=s_v, drain=True)
        E("vector", lambda v, i=i: v.tensor_reduce(eflat[:, 256 + i:257 + i], tmp8[:],
                                                   AX.X, ALU.min), inc=s_v, drain=True)
        E("vector", lambda v: v.tensor_tensor(stats[:, 14:15], stats[:, 12:13],
                                              stats[:, 13:14], ALU.add),
          inc=s_v, drain=True)
        E("vector", lambda v: v.tensor_scalar(stats[:, 14:15], stats[:, 14:15],
                                              MOE_EPS, 0.0, ALU.max, ALU.add),
          inc=s_v, drain=True)
        E("vector", lambda v: v.reciprocal(stats[:, 15:16], stats[:, 14:15]),
          inc=s_v, drain=True)
        E("vector", lambda v, i=i: v.tensor_tensor(wflat[:, i:i + 1], stats[:, 12:13],
                                                   stats[:, 15:16], ALU.mult),
          inc=s_v, drain=True)
        E("vector", lambda v, i=i: v.tensor_tensor(wflat[:, 256 + i:257 + i],
                                                   stats[:, 13:14], stats[:, 15:16],
                                                   ALU.mult), inc=s_v, drain=True)
    for e in range(8):
        E("vector", lambda v, e=e: v.tensor_scalar(emask[e][:], eflat[:], float(e), 0.0,
                                                   ALU.is_equal, ALU.add),
          inc=s_v, drain=True)
    rel_midc_pe = 0
    for it in range(42):
        E("vector", lambda v: v.tensor_tensor(rowst[0:1, 16:24], rowst[0:1, 0:8],
                                              rowst[0:1, 8:16], ALU.add),
          inc=s_v, drain=True)
        E("vector", lambda v: v.tensor_scalar(rowst[0:1, 16:24], rowst[0:1, 16:24],
                                              0.5, 0.0, ALU.mult, ALU.add),
          inc=s_v, drain=True)
        vm = p.now(s_v)
        E("tensor", lambda t: t.matmul(pbc[:, 0:8], onesf[:], rowst[0:1, 16:24],
                                       start=True, stop=True),
          waits=[(s_v, vm), (s_g, 2), (s_v, rel_midc_pe)], inc=s_pe)
        pe_m = p.now(s_pe)
        E("vector", lambda v: v.tensor_copy(midc[:], pbc[:, 0:8]),
          waits=[(s_pe, pe_m)], inc=s_v, drain=True)
        E("vector", lambda v: v.tensor_scalar(accb[:], emask[0][:], midc[:, 0:1], 0.0,
                                              ALU.mult, ALU.add), inc=s_v, drain=True)
        for e in range(1, 8):
            E("vector", lambda v, e=e: v.scalar_tensor_tensor(
                accb[:], emask[e][:], midc[:, e:e + 1], accb[:], ALU.mult, ALU.add),
              inc=s_v, drain=True)
        E("vector", lambda v: v.tensor_tensor(cmpf[:], wflat[:], accb[:], ALU.is_gt),
          inc=s_v, drain=True)
        for e in range(8):
            E("vector", lambda v, e=e: v.scalar_tensor_tensor(
                scratch[:, 0:512], cmpf[:], 1.0, emask[e][:], ALU.mult, ALU.mult,
                accum_out=cnt8[:, e:e + 1]), inc=s_v, drain=True)
        vcnt = p.now(s_v)
        E("tensor", lambda t: t.matmul(pbc[0:1, 0:8], onescol[:], cnt8[:],
                                       start=True, stop=True),
          waits=[(s_vcnt := s_v, vcnt), (s_g, 3)], inc=s_pe)
        pe_c = p.now(s_pe)
        rel_midc_pe = pe_c
        E("vector", lambda v: v.tensor_copy(rowst[0:1, 24:32], pbc[0:1, 0:8]),
          waits=[(s_pe, pe_c)], inc=s_v, drain=True)
        E("vector", lambda v: v.tensor_scalar(rowst[0:1, 24:32], rowst[0:1, 24:32],
                                              CAPACITY, 0.0, ALU.is_ge, ALU.add),
          inc=s_v, drain=True)
        E("vector", lambda v: v.tensor_tensor(stats[0:1, 16:24], rowst[0:1, 16:24],
                                              rowst[0:1, 0:8], ALU.subtract),
          inc=s_v, drain=True)
        E("vector", lambda v: v.tensor_tensor(stats[0:1, 16:24], stats[0:1, 16:24],
                                              rowst[0:1, 24:32], ALU.mult),
          inc=s_v, drain=True)
        E("vector", lambda v: v.tensor_tensor(rowst[0:1, 0:8], rowst[0:1, 0:8],
                                              stats[0:1, 16:24], ALU.add),
          inc=s_v, drain=True)
        E("vector", lambda v: v.tensor_scalar(rowst[0:1, 24:32], rowst[0:1, 24:32],
                                              -1.0, 1.0, ALU.mult, ALU.add),
          inc=s_v, drain=True)
        E("vector", lambda v: v.tensor_tensor(stats[0:1, 16:24], rowst[0:1, 16:24],
                                              rowst[0:1, 8:16], ALU.subtract),
          inc=s_v, drain=True)
        E("vector", lambda v: v.tensor_tensor(stats[0:1, 16:24], stats[0:1, 16:24],
                                              rowst[0:1, 24:32], ALU.mult),
          inc=s_v, drain=True)
        E("vector", lambda v: v.tensor_tensor(rowst[0:1, 8:16], rowst[0:1, 8:16],
                                              stats[0:1, 16:24], ALU.add),
          inc=s_v, drain=True)
    E("vector", lambda v: v.tensor_copy(rowst[0:1, 16:24], rowst[0:1, 0:8]),
      inc=s_v, drain=True)
    vth = p.now(s_v)
    E("tensor", lambda t: t.matmul(pbc[:, 0:8], onesf[:], rowst[0:1, 16:24],
                                   start=True, stop=True),
      waits=[(s_v, vth), (s_v, rel_midc_pe if False else 0), (s_pe, rel_midc_pe)],
      inc=s_pe)
    pe_t = p.now(s_pe)
    E("vector", lambda v: v.tensor_copy(midc[:], pbc[:, 0:8]),
      waits=[(s_pe, pe_t)], inc=s_v, drain=True)
    E("vector", lambda v: v.tensor_scalar(accb[:], emask[0][:], midc[:, 0:1], 0.0,
                                          ALU.mult, ALU.add), inc=s_v, drain=True)
    for e in range(1, 8):
        E("vector", lambda v, e=e: v.scalar_tensor_tensor(
            accb[:], emask[e][:], midc[:, e:e + 1], accb[:], ALU.mult, ALU.add),
          inc=s_v, drain=True)
    E("vector", lambda v: v.tensor_tensor(kflat[:], wflat[:], accb[:], ALU.is_gt),
      inc=s_v, drain=True)
    E("vector", lambda v: v.tensor_tensor(kw1[:], kflat[:, 0:256], wflat[:, 0:256],
                                          ALU.mult), inc=s_v, drain=True)
    E("vector", lambda v: v.tensor_tensor(kw2[:], kflat[:, 256:512], wflat[:, 256:512],
                                          ALU.mult), inc=s_v, drain=True)
    E("vector", lambda v: v.tensor_tensor(scratch[:, 0:256], kw1[:], kw2[:], ALU.add),
      inc=s_v, drain=True)
    E("vector", lambda v: v.tensor_scalar(scratch[:, 0:256], scratch[:, 0:256],
                                          MOE_EPS, 0.0, ALU.max, ALU.add),
      inc=s_v, drain=True)
    E("vector", lambda v: v.reciprocal(scratch[:, 256:512], scratch[:, 0:256]),
      inc=s_v, drain=True)
    E("vector", lambda v: v.tensor_tensor(kw1[:], kw1[:], scratch[:, 256:512],
                                          ALU.mult), inc=s_v, drain=True)
    E("vector", lambda v: v.tensor_tensor(kw2[:], kw2[:], scratch[:, 256:512],
                                          ALU.mult), inc=s_v, drain=True)
    for i in range(NT):
        E("vector", lambda v, i=i: v.tensor_scalar(maskA[:], idx8s[:],
                                                   eflat[:, i:i + 1], 0.0,
                                                   ALU.is_equal, ALU.add),
          inc=s_v, drain=True)
        E("vector", lambda v, i=i: v.tensor_scalar(maskB[:], idx8s[:],
                                                   eflat[:, 256 + i:257 + i], 0.0,
                                                   ALU.is_equal, ALU.add),
          inc=s_v, drain=True)
        E("vector", lambda v, i=i: v.tensor_scalar(tmp8[:], maskB[:],
                                                   kw2[:, i:i + 1], 0.0,
                                                   ALU.mult, ALU.add),
          inc=s_v, drain=True)
        E("vector", lambda v, i=i: v.scalar_tensor_tensor(
            cw[:, i * 8:(i + 1) * 8], maskA[:], kw1[:, i:i + 1], tmp8[:],
            ALU.mult, ALU.add), inc=s_v, drain=True)
    cw_done = p.now(s_v)
    if debug:
        for (dst, src) in ((dbg_wf, wflat), (dbg_ef, eflat), (dbg_kf, kflat),
                           (dbg_cw, cw), (dbg_th, midc)):
            E("gpsimd", lambda g, dst=dst, src=src: g.dma_start(out=dst.ap(),
                                                               in_=src[:]),
              waits=[(s_v, cw_done)], inc=s_st, amt=16)

    rel_w_pe = 0
    rel_xsb = [0, 0]
    rel_gtb = 0
    rel_pm1v = [0, 0]
    rel_silb_v = 0
    rel_pm1sc = 0
    rel_pm2v = [0, 0]
    rel_stage = [0, 0]
    prev_acc_st = 0
    NT2 = 16
    for e in range(8):
        for blk in range(4):
            E("gpsimd", lambda g, e=e, blk=blk: g.dma_start(
                out=wpre[:], in_=chunkmajor_ap(preTD, D, 4096, 1024,
                                               col0=e * D * 4096 + blk * 1024)),
              waits=[(s_pe, rel_w_pe)], inc=s_wt, amt=16)
            E("gpsimd", lambda g, e=e, blk=blk: g.dma_start(
                out=wgate[:], in_=chunkmajor_ap(gateTD, D, 4096, 1024,
                                                col0=e * D * 4096 + blk * 1024)),
              inc=s_wt, amt=16)
            E("gpsimd", lambda g, e=e, blk=blk: g.dma_start(
                out=wpost[:], in_=chunkmajor_ap(postTD, 4096, D, D,
                                                col0=e * 4096 * D + blk * 1024 * D,
                                                nrow_chunks=8)),
              inc=s_wt, amt=16)
            wt_eb = p.now(s_wt)
            for ti in range(NT2):
                b2 = ti % 2
                E("sync", lambda y, ti=ti, b2=b2: y.dma_start(
                    out=xsb[b2][:], in_=chunkmajor_ap(ht_scr, D, T, 256,
                                                      col0=ti * 256)),
                  waits=[(s_st, p1_st), (s_pe, rel_xsb[b2])], inc=s_ld, amt=16)
                ld_x = p.now(s_ld)
                for s_ in range(8):
                    bs = s_ % 2
                    for cc in range(DC):
                        E("tensor", lambda t, s_=s_, cc=cc, bs=bs, b2=b2: t.matmul(
                            pm1[bs][:],
                            wpre[:, cc * 1024 + s_ * 128: cc * 1024 + s_ * 128 + 128],
                            xsb[b2][:, cc * 256:(cc + 1) * 256],
                            start=(cc == 0), stop=(cc == DC - 1)),
                          waits=[(s_wt, wt_eb), (s_ld, ld_x), (s_v, rel_pm1v[bs])],
                          inc=s_pe if cc == DC - 1 else None)
                    for cc in range(DC):
                        E("tensor", lambda t, s_=s_, cc=cc, b2=b2: t.matmul(
                            pmg[:],
                            wgate[:, cc * 1024 + s_ * 128: cc * 1024 + s_ * 128 + 128],
                            xsb[b2][:, cc * 256:(cc + 1) * 256],
                            start=(cc == 0), stop=(cc == DC - 1)),
                          waits=[(s_sc, rel_pm1sc)],
                          inc=s_pe if cc == DC - 1 else None)
                    pe_mm1 = p.now(s_pe)
                    E("scalar", lambda s: s.activation(silb[:], pmg[:], AF.Silu),
                      waits=[(s_pe, pe_mm1), (s_v, rel_silb_v)], inc=s_sc, drain=True)
                    rel_pm1sc = p.now(s_sc)
                    E("vector", lambda v, s_=s_, bs=bs: v.tensor_tensor(
                        gtb[:, s_ * 256:(s_ + 1) * 256], pm1[bs][:], silb[:],
                        ALU.mult),
                      waits=[(s_sc, rel_pm1sc), (s_pe, rel_gtb)], inc=s_v, drain=True)
                    rel_pm1v[bs] = p.now(s_v)
                    rel_silb_v = p.now(s_v)
                v_gtb = p.now(s_v)
                if debug and e == 0 and blk == 0 and ti == 0:
                    E("gpsimd", lambda g: g.dma_start(out=dbg_g0.ap(), in_=gtb[:]),
                      waits=[(s_v, v_gtb)], inc=s_st, amt=16)
                for tsub in range(2):
                    gidx = ti * 2 + tsub
                    bst = gidx % 2
                    for co in range(2):
                        for s_ in range(8):
                            E("tensor", lambda t, s_=s_, tsub=tsub, co=co: t.matmul(
                                pm2[co][:],
                                gtb[:, s_ * 256 + tsub * 128: s_ * 256 + tsub * 128 + 128],
                                wpost[:, s_ * D + co * 512: s_ * D + co * 512 + 512],
                                start=(s_ == 0), stop=(s_ == 7)),
                              waits=[(s_v, v_gtb), (s_v, rel_pm2v[co]),
                                     (s_sc, rel_pm1sc if co == 0 else 0)],
                              inc=s_pe if s_ == 7 else None)
                        pe_mm2 = p.now(s_pe)
                        E("vector", lambda v, co=co, gidx=gidx, e=e, bst=bst:
                          v.tensor_scalar(stage[bst][:, co * 512:(co + 1) * 512],
                                          pm2[co][:],
                                          cw[:, gidx * 8 + e: gidx * 8 + e + 1], 0.0,
                                          ALU.mult, ALU.add),
                          waits=[(s_pe, pe_mm2), (s_v, cw_done),
                                 (s_st, rel_stage[bst])], inc=s_v, drain=True)
                        rel_pm2v[co] = p.now(s_v)
                    v_st = p.now(s_v)
                    if e == 0 and blk == 0:
                        E("gpsimd", lambda g, gidx=gidx, bst=bst: g.dma_start(
                            out=acc_scr.ap()[gidx * 128:(gidx + 1) * 128, :],
                            in_=stage[bst][:]),
                          waits=[(s_v, v_st)], inc=s_st, amt=16)
                    else:
                        E("gpsimd", lambda g, gidx=gidx, bst=bst: g.dma_start(
                            out=acc_scr.ap()[gidx * 128:(gidx + 1) * 128, :],
                            in_=stage[bst][:], accum_op=ALU.add),
                          waits=[(s_v, v_st), (s_st, prev_acc_st)], inc=s_st, amt=16)
                    rel_stage[bst] = p.now(s_st)
                rel_gtb = p.now(s_pe)
                rel_xsb[b2] = p.now(s_pe)
            rel_w_pe = p.now(s_pe)
            prev_acc_st = p.now(s_st)
    acc_done = p.now(s_st)
    if debug:
        E("gpsimd", lambda g: g.dma_start(out=dbg_acc.ap(), in_=acc_scr.ap()),
          waits=[(s_st, acc_done)], inc=s_st, amt=16)

    rel_at = 0
    rel_ob = 0
    for i in range(NT):
        b = i % 2
        E("sync", lambda y, i=i, b=b: y.dma_start(
            out=rbuf[b][:], in_=res1.ap()[i * 128:(i + 1) * 128, :]),
          waits=[(s_v, rel_rbuf[b])], inc=s_ld, amt=16)
        E("sync", lambda y, i=i: y.dma_start(
            out=atile[:], in_=acc_scr.ap()[i * 128:(i + 1) * 128, :]),
          waits=[(s_st, acc_done), (s_v, rel_at)], inc=s_ld, amt=16)
        ldc = p.now(s_ld)
        E("vector", lambda v, b=b: v.tensor_tensor(obuf[:], rbuf[b][:], atile[:],
                                                   ALU.add),
          waits=[(s_ld, ldc), (s_st, rel_ob)], inc=s_v, drain=True)
        rel_rbuf[b] = p.now(s_v)
        rel_at = p.now(s_v)
        vo = p.now(s_v)
        E("gpsimd", lambda g, i=i: g.dma_start(
            out=out.ap()[i * 128:(i + 1) * 128, :], in_=obuf[:]),
          waits=[(s_v, vo)], inc=s_st, amt=16)
        rel_ob = p.now(s_st)
    fin = p.now(s_st)
    E("gpsimd", lambda g: g.nop(), waits=[(s_st, fin)])
    return p.finalize() if fin_own else None


def run_launch_b(res1_full, inputs, debug=False, lg=None):
    if lg is None:
        lg = run_launch_b1(res1_full, inputs)
    lgflat = lg.reshape(8 * T, 8)
    nc = build_launch_b2(debug)
    pre = np.asarray(inputs["L0_ffn_ff_pre_act_weight"], np.float32)
    gat = np.asarray(inputs["L0_ffn_gate_weight"], np.float32)
    pos = np.asarray(inputs["L0_ffn_ff_post_act_weight"], np.float32)
    com = dict(
        preT=np.ascontiguousarray(pre.transpose(0, 2, 1)),
        gateT=np.ascontiguousarray(gat.transpose(0, 2, 1)),
        postT=np.ascontiguousarray(pos.transpose(0, 2, 1)),
        idx8=np.tile(np.arange(8, dtype=np.float32), (128, 1)),
        idx8m8=np.tile(np.arange(8, dtype=np.float32) - 8.0, (128, 1)),
        ident=np.eye(128, dtype=np.float32),
    )
    in_maps = []
    for b in range(8):
        m = dict(com)
        m["res1"] = np.ascontiguousarray(res1_full[b])
        m["lgall"] = np.ascontiguousarray(np.roll(lgflat, -b * T, axis=0))
        in_maps.append(m)
    res = run_bass_kernel_spmd(nc, in_maps, core_ids=list(range(8)))
    if debug:
        return res.results
    return np.stack([r["out"] for r in res.results])



ARENA_BYTES = 210944

def emit_gather_rotate(ctx, cc_in, cc_out, lgrot, rotpD):
    nc, p = ctx.nc, ctx.p
    ctx.begin_phase("rot")
    E = p.emit
    s_cc = p.sem("s_cc")
    s_ld = p.sem("s_ld")
    s_pe = p.sem("s_pe", owner="tensor")
    s_v = p.sem("s_v", owner="vector")
    s_st = p.sem("s_st")
    rsb = ctx.sb("rsb", [8, 8], F32)
    chunk = ctx.sb("rchunk", [8, 4096], F32)
    stg = [ctx.sb(f"rstage{i}", [8, 512], F32) for i in range(2)]
    ps = ctx.psum("prot", [8, 512], F32)

    def blockview(dram, col0, width):
        ap = dram.ap().copy()
        while len(ap.ap) > 1:
            ap.ap.pop()
        ap.ap[0] = [8 * T, 8]
        ap.ap.append([1, width])
        ap.offset = col0
        return ap

    E("gpsimd", lambda g: g.collective_compute(
        "AllGather", ALU.bypass, replica_groups=[list(range(8))],
        ins=[cc_in.ap().opt()], outs=[cc_out.ap().opt()]), inc=s_cc, amt=1)
    E("sync", lambda y: y.dma_start(out=rsb, in_=rotpD.ap()), inc=s_ld, amt=16)
    rel_chunk = 0
    rel_stage = [0, 0]
    rel_ps = 0
    for c in range(8):
        E("sync", lambda y, c=c: y.dma_start(out=chunk, in_=blockview(cc_out, c * 4096,
                                                                      4096)),
          waits=[(s_cc, 1), (s_pe, rel_chunk)], inc=s_ld, amt=16)
        ldc = p.now(s_ld)
        for k in range(8):
            bs = k % 2
            E("tensor", lambda t, k=k: t.matmul(ps, rsb, chunk[:, k * 512:(k + 1) * 512],
                                                start=True, stop=True),
              waits=[(s_ld, ldc), (s_v, rel_ps)], inc=s_pe)
            pe = p.now(s_pe)
            E("vector", lambda v, bs=bs: v.tensor_copy(stg[bs][:], ps),
              waits=[(s_pe, pe), (s_st, rel_stage[bs])], inc=s_v, drain=True)
            rel_ps = p.now(s_v)
            vv = p.now(s_v)
            E("gpsimd", lambda g, c=c, k=k, bs=bs: g.dma_start(
                out=blockview(lgrot, c * 4096 + k * 512, 512), in_=stg[bs][:]),
              waits=[(s_v, vv)], inc=s_st, amt=16)
            rel_stage[bs] = p.now(s_st)
        rel_chunk = p.now(s_pe)
    fin = p.now(s_st)
    E("gpsimd", lambda g: g.nop(), waits=[(s_st, fin)])


def build_fused(debug=False):
    nc = bass.Bass("TRN2", target_bir_lowering=False, debug=False,
                       disable_frame_to_traceback=True)
    p = Prog(nc)
    arena = Arena(nc, ARENA_BYTES)
    pool = PsumPool(nc)
    ctx = FusedCtx(nc, p, arena, pool)

    xin = nc.dram_tensor("xin", [T, D], F32, kind="ExternalInput")
    out = nc.dram_tensor("out", [T, D], F32, kind="ExternalOutput")
    rotpD = nc.dram_tensor("rotp", [8, 8], F32, kind="ExternalInput")
    res1_scr = nc.dram_tensor("res1_scr", [T, D], F32, kind="Internal")
    res2_scr = nc.dram_tensor("res2_scr", [T, D], F32, kind="Internal")
    res3_scr = nc.dram_tensor("res3_scr", [T, D], F32, kind="Internal")
    cc_in = nc.dram_tensor("cc_in", [T, 8], F32, kind="Internal")
    cc_out = nc.dram_tensor("cc_out", [8 * T, 8], F32, kind="Internal")
    lgrot_scr = nc.dram_tensor("lgrot_scr", [8 * T, 8], F32, kind="Internal")

    ctx.redirect = {
        ("a", "xin"): xin, ("a", "out"): res1_scr,
        ("b1", "res1"): res1_scr, ("b1", "lg"): cc_in,
        ("b2", "res1"): res1_scr, ("b2", "lgall"): lgrot_scr, ("b2", "out"): res2_scr,
        ("c", "res2"): res2_scr, ("c", "out"): res3_scr,
        ("d", "res3"): res3_scr, ("d", "out"): out,
    }
    if debug:
        dbg_r1 = nc.dram_tensor("dbg_r1", [T, D], F32, kind="ExternalOutput")
        dbg_r2 = nc.dram_tensor("dbg_r2", [T, D], F32, kind="ExternalOutput")
        dbg_r3 = nc.dram_tensor("dbg_r3", [T, D], F32, kind="ExternalOutput")
        dbg_lgr = nc.dram_tensor("dbg_lgr", [8 * T, 8], F32, kind="ExternalOutput")

    build_launch_a(ctx=ctx)
    build_launch_b1(ctx=ctx)
    emit_gather_rotate(ctx, cc_in, cc_out, lgrot_scr, rotpD)
    build_launch_b2(ctx=ctx)
    build_launch_c(ctx=ctx)
    build_launch_d(ctx=ctx)
    p.barrier()
    if debug:
        s_dbg = p.sem("s_dbg")
        for (dst, src) in ((dbg_r1, res1_scr), (dbg_r2, res2_scr), (dbg_r3, res3_scr),
                           (dbg_lgr, lgrot_scr)):
            p.emit("gpsimd", lambda g, dst=dst, src=src: g.dma_start(
                out=dst.ap(), in_=src.ap()), inc=s_dbg, amt=16)
        fb = p.now(s_dbg)
        p.emit("gpsimd", lambda g: g.nop(), waits=[(s_dbg, fb)])
    return p.finalize(), ctx.input_names


def kernel(**inputs):
    x = np.ascontiguousarray(np.asarray(inputs["L0_attn_x"], np.float32))
    nc, input_names = build_fused()
    preps = {
        "a": prep_launch_a(inputs),
        "b1": dict(wrt=np.ascontiguousarray(
            np.asarray(inputs["L0_ffn_router_weight"], np.float32).T),
            ident=np.eye(128, dtype=np.float32)),
        "b2": dict(
            preT=np.ascontiguousarray(np.asarray(
                inputs["L0_ffn_ff_pre_act_weight"], np.float32).transpose(0, 2, 1)),
            gateT=np.ascontiguousarray(np.asarray(
                inputs["L0_ffn_gate_weight"], np.float32).transpose(0, 2, 1)),
            postT=np.ascontiguousarray(np.asarray(
                inputs["L0_ffn_ff_post_act_weight"], np.float32).transpose(0, 2, 1)),
            idx8=np.tile(np.arange(8, dtype=np.float32), (128, 1)),
            idx8m8=np.tile(np.arange(8, dtype=np.float32) - 8.0, (128, 1)),
            ident=np.eye(128, dtype=np.float32),
        ),
        "c": prep_launch_c(inputs),
        "d": prep_launch_d(inputs),
    }
    common = {}
    for fname, phase, orig in input_names:
        common[fname] = np.ascontiguousarray(np.asarray(preps[phase][orig],
                                                        np.float32))
    in_maps = []
    for b in range(8):
        m = dict(common)
        m["xin"] = np.ascontiguousarray(x[b])
        rp = np.zeros((8, 8), np.float32)
        for i in range(8):
            rp[(i + b) % 8, i] = 1.0
        m["rotp"] = rp
        in_maps.append(m)
    res = run_bass_kernel_spmd(nc, in_maps, core_ids=list(range(8)))
    return np.stack([r["out"] for r in res.results]).astype(np.float32)
